# version 14
# baseline (speedup 1.0000x reference)
from contextlib import ExitStack

import numpy as np
import ml_dtypes
import concourse.bass as bass
import concourse.mybir as mybir
from concourse.bass_utils import run_bass_kernel_spmd

F32 = mybir.dt.float32
BF16 = mybir.dt.bfloat16
I8 = mybir.dt.int8
AF = mybir.ActivationFunctionType
ALU = mybir.AluOpType
AX = mybir.AxisListType.X

DEBUG_DUMPS = False
NCORES = 8
B, S, D = 4, 8192, 1024
T = 4096
NT = T // 128
NCH = T // 512
DC = D // 128
NH, DH = 16, 64
NP = NH // 2
I = 2816
IC = I // 128
HT = D // 128
EPS_ATTN = 1e-6
EPS_LN = 1e-5
QMAX = 127.0

ENGS = ("tensor", "vector", "scalar", "sync", "gpsimd")
COMPUTE_ENGS = ("tensor", "vector", "scalar")


class Plan:

    def __init__(self):
        self.steps = []
        self.count = {}
        self.eng_of = {}

    @staticmethod
    def _flatten(deps, out):
        for d in deps:
            if d is None or d == () or d == []:
                continue
            if isinstance(d, tuple) and len(d) == 2 and isinstance(d[0], str):
                out.append(d)
            else:
                Plan._flatten(d, out)

    def add(self, eng, fn, deps=(), inc=1, stream=None):
        stream = stream or eng
        self.eng_of.setdefault(stream, eng)
        flat = []
        self._flatten(deps, flat)
        self.count[stream] = self.count.get(stream, 0) + inc
        tok = (stream, self.count[stream])
        self.steps.append((eng, fn, tuple(flat)))
        return tok

    def barrier(self):
        return tuple((s, v) for s, v in self.count.items() if v > 0)


def build_program():
    nc = bass.Bass("TRN2", target_bir_lowering=False)
    plan = Plan()

    x_in = nc.dram_tensor("x", [T, D], F32, kind="ExternalInput").ap()
    wq_in = nc.dram_tensor("wq", [D, D], F32, kind="ExternalInput").ap()
    wk_in = nc.dram_tensor("wk", [D, D], F32, kind="ExternalInput").ap()
    wv_in = nc.dram_tensor("wv", [D, D], F32, kind="ExternalInput").ap()
    wo_in = nc.dram_tensor("wo", [D, D], F32, kind="ExternalInput").ap()
    bq_in = nc.dram_tensor("bq", [1, D], F32, kind="ExternalInput").ap()
    bk_in = nc.dram_tensor("bk", [1, D], F32, kind="ExternalInput").ap()
    bv_in = nc.dram_tensor("bv", [1, D], F32, kind="ExternalInput").ap()
    bo_in = nc.dram_tensor("bo", [1, D], F32, kind="ExternalInput").ap()
    clipx_in = nc.dram_tensor("clipx", [1, 1], F32, kind="ExternalInput").ap()
    clipd_in = nc.dram_tensor("clipd", [1, 1], F32, kind="ExternalInput").ap()
    wg_in = nc.dram_tensor("wg", [I, D], F32, kind="ExternalInput").ap()
    wu_in = nc.dram_tensor("wu", [I, D], F32, kind="ExternalInput").ap()
    wd_in = nc.dram_tensor("wd", [D, I], F32, kind="ExternalInput").ap()
    wgt_in = nc.dram_tensor("wgt", [D, I], F32, kind="ExternalInput").ap()
    wut_in = nc.dram_tensor("wut", [D, I], F32, kind="ExternalInput").ap()
    wdt_in = nc.dram_tensor("wdt", [I, D], F32, kind="ExternalInput").ap()
    oh_in = nc.dram_tensor("onehot", [NH, NP * 128], BF16, kind="ExternalInput").ap()
    out_d = nc.dram_tensor("out", [T, D], F32, kind="ExternalOutput").ap()
    if DEBUG_DUMPS:
        dbg_x1 = nc.dram_tensor("dbg_x1", [NT, 128, D], BF16, kind="ExternalOutput").ap()
        dbg_h = nc.dram_tensor("dbg_h", [IC, 128, T], BF16, kind="ExternalOutput").ap()
        dbg_q = nc.dram_tensor("dbg_q", [DC, 128, T], BF16, kind="ExternalOutput").ap()
        dbg_kv = nc.dram_tensor("dbg_kv", [128, NP * 65], F32, kind="ExternalOutput").ap()
        dbg_wg = nc.dram_tensor("dbg_wg", [DC, 128, I], BF16, kind="ExternalOutput").ap()
        dbg_xq = nc.dram_tensor("dbg_xq", [DC, 128, T], BF16, kind="ExternalOutput").ap()
        dbg_wfm = nc.dram_tensor("dbg_wfm", [128, 2 * D], BF16, kind="ExternalOutput").ap()

    q_d = nc.dram_tensor("q_d", [DC, 128, T], BF16).ap()
    x1_d = nc.dram_tensor("x1_d", [NT, 128, D], BF16).ap()
    h_d = nc.dram_tensor("h_d", [IC, 128, T], BF16).ap()
    wgq_d = nc.dram_tensor("wgq_d", [DC, 128, I], BF16).ap()
    wuq_d = nc.dram_tensor("wuq_d", [DC, 128, I], BF16).ap()
    wdq_d = nc.dram_tensor("wdq_d", [IC, 128, D], BF16).ap()
    rsc_d = nc.dram_tensor("rsc_d", [1, I], F32).ap()
    cc_kv_i = nc.dram_tensor("cc_kv_i", [128, NP * 65], F32).ap()
    cc_kv_o = nc.dram_tensor("cc_kv_o", [128, NP * 65], F32).ap()
    cc_m1_i = nc.dram_tensor("cc_m1_i", [1, 1], F32).ap()
    cc_m1_o = nc.dram_tensor("cc_m1_o", [1, 1], F32).ap()
    cc_m2_i = nc.dram_tensor("cc_m2_i", [1, 1], F32).ap()
    cc_m2_o = nc.dram_tensor("cc_m2_o", [1, 1], F32).ap()
    pmax_d = nc.dram_tensor("pmax_d", [1, 128], F32).ap()
    bc1_d = nc.dram_tensor("bc1_d", [1, 1], F32).ap()
    bc2_d = nc.dram_tensor("bc2_d", [1, 1], F32).ap()
    bc3_d = nc.dram_tensor("bc3_d", [1, 1], F32).ap()
    swd_dram = nc.dram_tensor("swd_dram", [1, D], F32).ap()

    sems = {}
    chain_cnt = {"vector": 0, "scalar": 0}

    def vchain(eng, ins):
        chain_cnt["vector"] += 1
        ins.then_inc(sems["vchain"], 1)
        eng.wait_ge(sems["vchain"], chain_cnt["vector"])

    def schain(eng, ins):
        chain_cnt["scalar"] += 1
        ins.then_inc(sems["schain"], 1)
        eng.wait_ge(sems["schain"], chain_cnt["scalar"])

    def dma(queue, stream, out, in_, deps=(), slow=False):
        def fn(eng):
            eng.dma_start(out=out, in_=in_,
                          allow_slow_non_contiguous=slow).then_inc(sems[stream], 16)
        return plan.add(queue, fn, deps, inc=16, stream=stream)

    def dmas(queue, stream, pairs, deps=()):
        def fn(eng):
            for o, i_ in pairs:
                eng.dma_start(out=o, in_=i_).then_inc(sems[stream], 16)
        return plan.add(queue, fn, deps, inc=16 * len(pairs), stream=stream)

    def step(eng, fn, deps=(), inc=1):
        return plan.add(eng, fn, deps, inc)

    psA = [nc.alloc_psum_tensor(f"psA{i}", [128, 512], F32) for i in range(2)]
    psB = [nc.alloc_psum_tensor(f"psB{i}", [128, 1024], F32) for i in range(2)]
    psC = [nc.alloc_psum_tensor(f"psC{i}", [128, 512], F32) for i in range(2)]

    ones_row = nc.alloc_sbuf_tensor("ones_row", [1, 128], BF16)
    ones512 = nc.alloc_sbuf_tensor("ones512", [1, 512], BF16)
    bq_bf = nc.alloc_sbuf_tensor("bq_bf", [1, D], BF16)
    bk_bf = nc.alloc_sbuf_tensor("bk_bf", [1, D], BF16)
    bv_bf = nc.alloc_sbuf_tensor("bv_bf", [1, D], BF16)
    bo_bf = nc.alloc_sbuf_tensor("bo_bf", [1, D], BF16)
    clipx_s = nc.alloc_sbuf_tensor("clipx_s", [1, 1], F32)
    clipd_s = nc.alloc_sbuf_tensor("clipd_s", [1, 1], F32)
    oh_sb = nc.alloc_sbuf_tensor("oh_sb", [NH, NP * 128], BF16)
    kv_sb = nc.alloc_sbuf_tensor("kv_sb", [128, NP * 65], F32)
    kvt_sb = nc.alloc_sbuf_tensor("kvt_sb", [128, NP * 65], F32)
    xmax_cols = nc.alloc_sbuf_tensor("xmax_cols", [128, NT], F32)
    hmax_cols = nc.alloc_sbuf_tensor("hmax_cols", [128, IC * NCH], F32)
    smax1 = nc.alloc_sbuf_tensor("smax1", [1, 128], F32)
    gmax1 = nc.alloc_sbuf_tensor("gmax1", [1, 1], F32)
    gmax2 = nc.alloc_sbuf_tensor("gmax2", [1, 1], F32)
    sc_small = nc.alloc_sbuf_tensor("sc_small", [1, 8], F32)
    rsx_bc = nc.alloc_sbuf_tensor("rsx_bc", [128, 1], F32)
    sx_bc = nc.alloc_sbuf_tensor("sx_bc", [128, 1], F32)
    rsh_bc = nc.alloc_sbuf_tensor("rsh_bc", [128, 1], F32)
    swg_cols = nc.alloc_sbuf_tensor("swg_cols", [128, IC], F32)
    swu_cols = nc.alloc_sbuf_tensor("swu_cols", [128, IC], F32)
    swd_cols = nc.alloc_sbuf_tensor("swd_cols", [128, HT], F32)
    svec_g = nc.alloc_sbuf_tensor("svec_g", [128, IC], F32)
    svec_u = nc.alloc_sbuf_tensor("svec_u", [128, IC], F32)
    swd_row = nc.alloc_sbuf_tensor("swd_row", [1, D], F32)
    swd_sc = nc.alloc_sbuf_tensor("swd_sc", [1, D], F32)
    ln_sc = nc.alloc_sbuf_tensor("ln_sc", [128, 16], F32)
    red_sc = nc.alloc_sbuf_tensor("red_sc", [128, 1], F32)
    ln2_sc = nc.alloc_sbuf_tensor("ln2_sc", [128, 16], F32)
    rsums = nc.alloc_sbuf_tensor("rsums", [128, 4], F32)
    ssqs = nc.alloc_sbuf_tensor("ssqs", [128, 4], F32)
    rsums2 = nc.alloc_sbuf_tensor("rsums2", [128, 4], F32)
    ssqs2 = nc.alloc_sbuf_tensor("ssqs2", [128, 4], F32)
    rcols = nc.alloc_sbuf_tensor("rcols", [128, IC], F32)

    t_bias = dmas("gpsimd", "bias", [(bq_bf[:, :], bq_in), (bk_bf[:, :], bk_in),
                                     (bv_bf[:, :], bv_in), (bo_bf[:, :], bo_in)])
    t_clip = dmas("sync", "clip", [(clipx_s[:, :], clipx_in), (clipd_s[:, :], clipd_in),
                                   (oh_sb[:, :], oh_in)])

    def f_consts(eng):
        eng.memset(ones_row[:, :], 1.0)
        eng.memset(ones512[:, :], 1.0).then_inc(sems["vector"], 1)
    t_ones = step("vector", f_consts)

    stk_a = ExitStack()

    def sb_a(name, shape, dtype):
        return stk_a.enter_context(nc.sbuf_tensor(name, shape, dtype))

    wq_bf = sb_a("wq_bf", [128, DC * D], BF16)
    wk_bf = sb_a("wk_bf", [128, DC * D], BF16)
    wv_bf = sb_a("wv_bf", [128, DC * D], BF16)
    t_wq = dmas("gpsimd", "wq", [(wq_bf[:, k * D:(k + 1) * D], wq_in[k * 128:(k + 1) * 128, :])
                                 for k in range(DC)])
    t_wk = dmas("gpsimd", "wk", [(wk_bf[:, k * D:(k + 1) * D], wk_in[k * 128:(k + 1) * 128, :])
                                 for k in range(DC)])
    t_wv = dmas("gpsimd", "wv", [(wv_bf[:, k * D:(k + 1) * D], wv_in[k * 128:(k + 1) * 128, :])
                                 for k in range(DC)])

    wrow_ = sb_a("wrow", [128, 2 * I], F32)
    wrow_sl = lambda s: wrow_[:, s * I:(s + 1) * I]
    wtc_ = sb_a("wtc", [128, 3 * I], F32)
    wtc = lambda s: wtc_[:, s * I:(s + 1) * I]
    wti8_ = sb_a("wti8", [128, 2 * I], I8)
    wti8 = lambda s: wti8_[:, s * I:(s + 1) * I]
    rs_bc = sb_a("rs_bc", [128, I], BF16)
    rs2_bc = sb_a("rs2_bc", [128, 2 * D], BF16)

    prep_q = []
    ps = {"rowfree0": (), "rowfree1": (), "tcfree0": (), "tcfree1": (),
          "tcfree2": (), "i8free0": (), "i8free1": (), "rcols": (),
          "rsd": (), "rsbc": (), "rsbc_readers": []}

    def mk_rows(w_in, cols, n_rows, width, toks):
        nb_max = 2 if width <= 1024 else 1
        units = []
        bidx = [0]

        def mk(rt0, nb):
            def u():
                sl = bidx[0] % 2
                bidx[0] += 1
                buf = wrow_sl(sl)
                src_ap = w_in[rt0 * 128:(rt0 + nb) * 128, :].rearrange(
                    "(a p) d -> p a d", p=128)
                t_l = dma("gpsimd", f"pwr{sl}",
                          buf[:, 0:nb * width].rearrange("p (a d) -> p a d", d=width),
                          src_ap, deps=(ps[f"rowfree{sl}"],))

                def f_red(eng, rt0=rt0, nb=nb, cols=cols, width=width, buf=buf):
                    for j in range(nb):
                        ins = eng.tensor_reduce(
                            out=cols[:, rt0 + j:rt0 + j + 1],
                            in_=buf[:, j * width:(j + 1) * width],
                            axis=AX, op=ALU.max, apply_absolute_value=True)
                        if j < nb - 1:
                            vchain(eng, ins)
                    ins.then_inc(sems["vector"], 1)
                t_r = step("vector", f_red, deps=(t_l,))
                ps[f"rowfree{sl}"] = (t_r,)
                toks.append(t_r)
            return u
        rt0 = 0
        while rt0 < n_rows:
            nb = min(nb_max, n_rows - rt0)
            units.append(mk(rt0, nb))
            rt0 += nb
        return units

    def mk_rsc(cols, n, rowtoks):
        def u():
            def f_rc(eng, cols=cols, n=n):
                eng.reciprocal(out=rcols[:, 0:n], in_=cols[:, 0:n]).then_inc(
                    sems["vector"], 1)
            t_rc = step("vector", f_rc, deps=(tuple(rowtoks), ps["rcols"]))
            ps["rcols"] = (t_rc,)
            dest = rsc_d.rearrange("a (i p) -> (a p) i", p=128)[:, 0:n]
            t_w = dma("gpsimd", "prsd", dest, rcols[:, 0:n],
                      deps=(t_rc, ps["rsd"]), slow=True)
            t_bc = dma("gpsimd", "prsb", rs_bc[:, 0:n * 128],
                       rsc_d[:, 0:n * 128].partition_broadcast(128),
                       deps=(t_w, tuple(ps["rsbc_readers"])))
            ps["rsd"] = (t_bc,)
            ps["rsbc"] = (t_bc,)
            ps["rsbc_readers"] = []
        return [u]

    def mk_fm(wt_in, width, n_chunks, dest_dram):
        units = []
        cnt = [0]
        nc2 = max(1, I // width // 2 + (0 if width == I else 1))
        nc2 = 1 if width == I else 2

        def flush(eng_unused=None):
            st = ps.pop("pend_st", None)
            if st:
                st()

        def mk(hc, nb):
            def u():
                flush()
                sl = cnt[0] % 3
                j = cnt[0] % 2
                cnt[0] += 1
                t_l = dma("gpsimd", f"ptc{sl}",
                          wtc(sl)[:, 0:nb * width].rearrange(
                              "p (a d) -> p a d", d=width),
                          wt_in[hc * 128:(hc + nb) * 128, :].rearrange(
                              "(a p) d -> p a d", p=128),
                          deps=(ps[f"tcfree{sl}"],))

                def f_q(eng, sl=sl, j=j, width=width, nb=nb):
                    eng.scalar_tensor_tensor(
                        out=wti8(j)[:, 0:nb * width], in0=wtc(sl)[:, 0:nb * width],
                        scalar=QMAX,
                        in1=rs_bc[:, 0:width] if nb == 1 else rs2_bc[:, 0:nb * width],
                        op0=ALU.mult, op1=ALU.mult).then_inc(sems["vector"], 1)
                t_q = step("vector", f_q,
                           deps=(t_l, ps["rsbc"], ps.get("rs2", ()),
                                 ps[f"i8free{j}"]))
                ps[f"tcfree{sl}"] = (t_q,)
                ps["rsbc_readers"].append(t_q)

                def emit_st(hc=hc, nb=nb, j=j, t_q=t_q, width=width):
                    t_s = dma("gpsimd", f"pst{j}",
                              dest_dram[hc:hc + nb, :, 0:width].transpose([1, 0, 2]),
                              wti8(j)[:, 0:nb * width].rearrange(
                                  "p (a d) -> p a d", d=width),
                              deps=(t_q,))
                    ps[f"i8free{j}"] = (t_s,)
                ps["pend_st"] = emit_st
            return u
        hc = 0
        while hc < n_chunks:
            nb = min(nc2, n_chunks - hc)
            units.append(mk(hc, nb))
            hc += nb
        units.append(lambda: flush())
        return units

    g_row_toks, u_row_toks, d_row_toks = [], [], []
    prep_q += mk_rows(wg_in, swg_cols, IC, D, g_row_toks)
    prep_q += mk_rows(wu_in, swu_cols, IC, D, u_row_toks)
    prep_q += mk_rows(wd_in, swd_cols, HT, I, d_row_toks)
    prep_q += mk_rsc(swg_cols, IC, g_row_toks)
    prep_q += mk_fm(wgt_in, I, DC, wgq_d)
    prep_q += mk_rsc(swu_cols, IC, u_row_toks)
    prep_q += mk_fm(wut_in, I, DC, wuq_d)
    prep_q += mk_rsc(swd_cols, HT, d_row_toks)

    def u_rs2():
        def f_cp(eng):
            vchain(eng, eng.tensor_copy(out=rs2_bc[:, 0:D], in_=rs_bc[:, 0:D]))
            eng.tensor_copy(out=rs2_bc[:, D:2 * D],
                            in_=rs_bc[:, 0:D]).then_inc(sems["vector"], 1)
        t_cp = step("vector", f_cp, deps=(ps["rsbc"],))
        ps["rsbc_readers"].append(t_cp)
        ps["rs2"] = (t_cp,)
    prep_q += [u_rs2]
    prep_q += mk_fm(wdt_in, D, IC, wdq_d)
    prep_idx = [0]

    def pump(n):
        while n > 0 and prep_idx[0] < len(prep_q):
            prep_q[prep_idx[0]]()
            prep_idx[0] += 1
            n -= 1

    x_tok_bf_ = sb_a("x_tok_bf", [128, 2 * D], BF16)
    x_tok_bf = lambda s: x_tok_bf_[:, s * D:(s + 1) * D]

    xfm_ = sb_a("xfm", [128, 2 * DC * 512], BF16)
    q_e = sb_a("q_e", [128, 512], BF16)
    q_m = sb_a("q_m", [128, 512], BF16)
    q_out_ = sb_a("q_out", [128, DC * 512], BF16)
    k_e = sb_a("k_e", [128, 1024], BF16)
    k_m = sb_a("k_m", [128, 1024], BF16)
    k_bf_ = sb_a("k_bf", [128, 4 * 1024], BF16)
    k_bf = lambda s: k_bf_[:, s * 1024:(s + 1) * 1024]
    v_aug_ = sb_a("v_aug", [128, 4 * NH * 65], BF16)
    v_aug = lambda s: v_aug_[:, s * NH * 65:(s + 1) * NH * 65]

    def xfm_ap(sl, c):
        return xfm_[:, (sl * DC + c) * 512:(sl * DC + c + 1) * 512]

    def f_vones(eng):
        for tt in range(4):
            va = v_aug(tt).rearrange("p (h c) -> p h c", c=65)
            ins = eng.memset(va[:, :, 64:65], 1.0)
        ins.then_inc(sems["vector"], 1)
    t_vones = step("vector", f_vones)

    xtok_free = [(), ()]
    xf_free = ()
    xfm_free = [(), ()]
    tr_tok = [[None] * 4, [None] * 4]
    ps_q_free = [(), ()]
    ps_k_free = ()
    ps_v_free = ()
    ps_kv_free = [(), ()]
    q_e_free = ()
    q_out_free = ()
    k_e_free = ()
    kslot_free = [()] * 4
    kv_sb_tok = ()
    last_kv_mm = ()

    for ch in range(NCH):
        sl = ch % 2
        for tt in range(4):
            t_idx = ch * 4 + tt
            tsl = t_idx % 2
            t_xld = dma("gpsimd", f"xt{tsl}", x_tok_bf(tsl),
                        x_in[t_idx * 128:(t_idx + 1) * 128, :],
                        deps=(xtok_free[tsl],))

            def f_tr(eng, sl=sl, tt=tt, tsl=tsl):
                for c in range(DC):
                    eng.dma_start(out=xfm_ap(sl, c)[:, tt * 128:(tt + 1) * 128],
                                  in_=x_tok_bf(tsl)[:, c * 128:(c + 1) * 128],
                                  transpose=True).then_inc(sems[f"tr{tsl}"], 16)
            t_tr = plan.add("sync", f_tr, (t_xld, xfm_free[sl]), inc=16 * DC,
                            stream=f"tr{tsl}")
            tr_tok[sl][tt] = t_tr
            xtok_free[tsl] = (t_tr,)
        ch_tr = tuple(tr_tok[sl])

        q_mm_toks = []
        qvec_chunk_toks = []
        for m in range(DC):
            bank = m % 2

            def f_qmm(eng, m=m, bank=bank, sl=sl):
                for k in range(DC):
                    eng.matmul(psA[bank][:, :],
                               wq_bf[:, k * D + m * 128: k * D + (m + 1) * 128],
                               xfm_ap(sl, k),
                               start=(k == 0), stop=False)
                eng.matmul(psA[bank][:, :],
                           bq_bf[:, m * 128:(m + 1) * 128],
                           ones512[:, :],
                           start=False, stop=True).then_inc(sems["tensor"], 1)
            t_qmm = step("tensor", f_qmm,
                         deps=(t_wq, t_bias, t_ones, ps_q_free[bank], ch_tr))
            q_mm_toks.append(t_qmm)

            def f_qexp(eng, bank=bank):
                eng.activation(out=q_e[:, :], in_=psA[bank][:, :],
                               func=AF.Exp).then_inc(sems["scalar"], 1)
            t_qexp = step("scalar", f_qexp, deps=(t_qmm, q_e_free))

            def f_qvec(eng, bank=bank, m=m):
                vchain(eng, eng.tensor_scalar(out=q_m[:, :], in0=q_e[:, :],
                                              scalar1=1.0, scalar2=None, op0=ALU.min))
                eng.scalar_tensor_tensor(
                    out=q_out_[:, m * 512:(m + 1) * 512], in0=psA[bank][:, :],
                    scalar=0.0, in1=q_m[:, :], op0=ALU.max,
                    op1=ALU.add).then_inc(sems["vector"], 1)
            t_qvec = step("vector", f_qvec, deps=(t_qexp, q_out_free))
            ps_q_free[bank] = (t_qvec,)
            q_e_free = (t_qvec,)
            qvec_chunk_toks.append(t_qvec)
            pump(1)

        t_qst = dma("scalar", "qst",
                    q_d[:, :, ch * 512:(ch + 1) * 512].transpose([1, 0, 2]),
                    q_out_[:, :].rearrange("p (a d) -> p a d", d=512),
                    deps=(tuple(qvec_chunk_toks),))
        q_out_free = (t_qst,)

        kvec_toks = []
        for tt in range(4):
            def f_kvmm(eng, sl=sl, tt=tt):
                for k in range(DC):
                    lhs = xfm_ap(sl, k)[:, tt * 128:(tt + 1) * 128]
                    for half in range(2):
                        eng.matmul(psB[0][:, half * 512:(half + 1) * 512], lhs,
                                   wk_bf[:, k * D + half * 512: k * D + (half + 1) * 512],
                                   start=(k == 0), stop=False)
                        eng.matmul(psB[1][:, half * 512:(half + 1) * 512], lhs,
                                   wv_bf[:, k * D + half * 512: k * D + (half + 1) * 512],
                                   start=(k == 0), stop=False)
                for half in range(2):
                    eng.matmul(psB[0][:, half * 512:(half + 1) * 512], ones_row[:, :],
                               bk_bf[:, half * 512:(half + 1) * 512],
                               start=False, stop=True)
                    ins = eng.matmul(psB[1][:, half * 512:(half + 1) * 512], ones_row[:, :],
                                     bv_bf[:, half * 512:(half + 1) * 512],
                                     start=False, stop=True)
                ins.then_inc(sems["tensor"], 1)
            t_kvmm = step("tensor", f_kvmm,
                          deps=(t_wk, t_wv, t_bias, t_ones, ps_k_free, ps_v_free,
                                kslot_free[tt], ch_tr))

            def f_kexp(eng):
                eng.activation(out=k_e[:, :], in_=psB[0][:, :],
                               func=AF.Exp).then_inc(sems["scalar"], 1)
            t_kexp = step("scalar", f_kexp, deps=(t_kvmm, k_e_free))

            def f_kvec(eng, tt=tt):
                vchain(eng, eng.tensor_scalar(out=k_m[:, :], in0=k_e[:, :],
                                              scalar1=1.0, scalar2=None, op0=ALU.min))
                eng.scalar_tensor_tensor(
                    out=k_bf(tt), in0=psB[0][:, :], scalar=0.0,
                    in1=k_m[:, :], op0=ALU.max, op1=ALU.add)
                va = v_aug(tt).rearrange("p (h c) -> p h c", c=65)
                pv = psB[1][:, :].rearrange("p (h c) -> p h c", c=64)
                eng.tensor_copy(out=va[:, :, 0:64], in_=pv[:, :, :]).then_inc(
                    sems["vector"], 1)
            t_kvec = step("vector", f_kvec, deps=(t_kexp, t_vones, kslot_free[tt]))
            ps_k_free = (t_kvec,)
            ps_v_free = (t_kvec,)
            k_e_free = (t_kvec,)
            kvec_toks.append(t_kvec)
            pump(1)

        for p in range(NP):
            bank = p % 2

            def f_kvps(eng, p=p, bank=bank):
                hA, hB = 2 * p, 2 * p + 1
                for tt in range(4):
                    eng.matmul(psC[bank][0:64, 0:65],
                               k_bf(tt)[:, hA * 64:(hA + 1) * 64],
                               v_aug(tt)[:, hA * 65:(hA + 1) * 65],
                               start=(tt == 0), stop=(tt == 3),
                               tile_position=(0, 0))
                for tt in range(4):
                    ins = eng.matmul(psC[bank][64:128, 0:65],
                                     k_bf(tt)[:, hB * 64:(hB + 1) * 64],
                                     v_aug(tt)[:, hB * 65:(hB + 1) * 65],
                                     start=(tt == 0), stop=(tt == 3),
                                     tile_position=(0, 64))
                ins.then_inc(sems["tensor"], 1)
            t_kvps = step("tensor", f_kvps, deps=(tuple(kvec_toks), ps_kv_free[bank]))
            last_kv_mm = (t_kvps,)

            def f_kvacc(eng, p=p, bank=bank, first=(ch == 0)):
                dst = kv_sb[:, p * 65:(p + 1) * 65]
                if first:
                    eng.tensor_copy(out=dst, in_=psC[bank][:, 0:65]).then_inc(
                        sems["vector"], 1)
                else:
                    eng.tensor_tensor(out=dst, in0=dst, in1=psC[bank][:, 0:65],
                                      op=ALU.add).then_inc(sems["vector"], 1)
            t_kvacc = step("vector", f_kvacc, deps=(t_kvps, kv_sb_tok))
            kv_sb_tok = (t_kvacc,)
            ps_kv_free[bank] = (t_kvacc,)

        for tt in range(4):
            kslot_free[tt] = last_kv_mm
        xfm_free[sl] = (q_mm_toks[-1], t_kvmm)

        pump(2)

    pump(len(prep_q))

    t_kv_out = dma("gpsimd", "cc", cc_kv_i, kv_sb[:, :], deps=(kv_sb_tok,))

    def f_cc_kv(eng):
        eng.collective_compute(
            "AllReduce", ALU.add,
            replica_groups=[[0, 1], [2, 3], [4, 5], [6, 7]],
            ins=[cc_kv_i.opt()], outs=[cc_kv_o.opt()]).then_inc(sems["gpsimd"], 1)
    t_cc_kv = step("gpsimd", f_cc_kv, deps=(t_kv_out,), inc=1)
    t_kv_in = dma("gpsimd", "cc", kvt_sb[:, :], cc_kv_o, deps=(t_cc_kv,))

    bar2 = plan.barrier()
    stk_a.close()

    stk_c = ExitStack()
    xq_fm = stk_c.enter_context(nc.sbuf_tensor("xq_fm", [128, DC * T], BF16))

    def xq_ap(c):
        return xq_fm[:, c * T:(c + 1) * T]

    stk_b = ExitStack()

    def sb_b(name, shape, dtype):
        return stk_b.enter_context(nc.sbuf_tensor(name, shape, dtype))

    wo_bf = sb_b("wo_bf", [128, DC * D], BF16)
    blockdiag = sb_b("blockdiag", [128, NP * 128], BF16)
    normlhs = sb_b("normlhs", [128, DC * NH], BF16)
    q_ch_ = sb_b("q_ch", [128, 2 * DC * 512], BF16)
    q_ch = lambda s, c: q_ch_[:, (s * DC + c) * 512:(s * DC + c + 1) * 512]
    mix_bf_ = sb_b("mix_bf", [128, 2 * DC * 512], BF16)
    mix_bf = lambda s: mix_bf_[:, s * DC * 512:(s + 1) * DC * 512]
    norm_sb = sb_b("norm_sb", [16, 512], F32)
    norm_sb2 = sb_b("norm_sb2", [16, 512], F32)
    rexp_sb_ = sb_b("rexp_sb", [128, 2 * 512], F32)
    rexp_sb = lambda s: rexp_sb_[:, s * 512:(s + 1) * 512]
    recip_bf = sb_b("recip_bf", [16, 512], BF16)
    x0_tile_ = sb_b("x0_tile", [128, 2 * D], F32)
    x0_tile = lambda s: x0_tile_[:, s * D:(s + 1) * D]
    r1_st_ = sb_b("r1_st", [128, 4 * D], F32)
    r1_st = lambda s: r1_st_[:, s * D:(s + 1) * D]
    x1_bf_ = sb_b("x1_bf", [128, 4 * D], BF16)
    x1_bf = lambda s: x1_bf_[:, s * D:(s + 1) * D]
    sq_junk = sb_b("sq_junk", [128, D], F32)

    t_wo = dmas("gpsimd", "wo", [(wo_bf[:, k * D:(k + 1) * D], wo_in[k * 128:(k + 1) * 128, :])
                                 for k in range(DC)], deps=(bar2,))

    def f_blockdiag(eng):
        vchain(eng, eng.memset(blockdiag[:, :], 0.0))
        vchain(eng, eng.memset(normlhs[:, :], 0.0))
        for p in range(NP):
            eng.tensor_copy(out=blockdiag[0:64, p * 128: p * 128 + 64],
                            in_=kvt_sb[0:64, p * 65: p * 65 + 64])
            eng.tensor_copy(out=blockdiag[64:128, p * 128 + 64: p * 128 + 128],
                            in_=kvt_sb[64:128, p * 65: p * 65 + 64])
            eng.tensor_copy(out=normlhs[0:64, p * NH + 2 * p: p * NH + 2 * p + 1],
                            in_=kvt_sb[0:64, p * 65 + 64: p * 65 + 65])
            ins = eng.tensor_copy(out=normlhs[64:128, p * NH + 2 * p + 1: p * NH + 2 * p + 2],
                                  in_=kvt_sb[64:128, p * 65 + 64: p * 65 + 65])
        ins.then_inc(sems["vector"], 1)
    t_bd = step("vector", f_blockdiag, deps=(t_kv_in, bar2))

    q_slot_free = [(), ()]
    mix_slot_free = [(), ()]
    ps_norm_free = ()
    ps_mix_free = [(), ()]
    ps_rexp_free = [(), ()]
    rexp_sb_free = [(), ()]
    ps_wo_free = ()
    recip_free = ()
    x0_free = [bar2, bar2]
    x1_free = [(), (), (), ()]
    stats_tok = ()
    r1_free = [()] * 4
    sqj_tok = ()
    x1_store_toks = []
    x1tr_toks = []
    xmax_toks = []

    for ch in range(NCH):
        sl = ch % 2
        t_qld = dmas("gpsimd", f"qch{sl}",
                     [(q_ch(sl, c), q_d[c, :, ch * 512:(ch + 1) * 512]) for c in range(DC)],
                     deps=(bar2, q_slot_free[sl]))

        def f_nmm(eng, sl=sl):
            for c in range(DC):
                ins = eng.matmul(psA[0][0:16, :],
                                 normlhs[:, c * NH:(c + 1) * NH],
                                 q_ch(sl, c),
                                 start=(c == 0), stop=(c == DC - 1))
            ins.then_inc(sems["tensor"], 1)
        t_nmm = step("tensor", f_nmm, deps=(t_qld, t_bd, ps_norm_free))

        def f_nrm(eng):
            vchain(eng, eng.tensor_scalar(out=norm_sb[:, :], in0=psA[0][0:16, :],
                                          scalar1=EPS_ATTN, scalar2=None, op0=ALU.max))
            vchain(eng, eng.reciprocal(out=norm_sb2[:, :], in_=norm_sb[:, :]))
            eng.tensor_copy(out=recip_bf[:, :], in_=norm_sb2[:, :]).then_inc(
                sems["vector"], 1)
        t_nrm = step("vector", f_nrm, deps=(t_nmm, recip_free))
        ps_norm_free = (t_nrm,)

        mix_toks = []
        last_mixmm = None
        for p in range(NP):
            bank = p % 2

            def f_mixmm(eng, p=p, bank=bank, sl=sl):
                eng.matmul(psC[bank][:, :],
                           blockdiag[:, p * 128:(p + 1) * 128],
                           q_ch(sl, p),
                           start=True, stop=True)
                eng.matmul(psB[0][:, bank * 512:(bank + 1) * 512],
                           oh_sb[:, p * 128:(p + 1) * 128],
                           recip_bf[:, :],
                           start=True, stop=True).then_inc(sems["tensor"], 1)
            t_mixmm = step("tensor", f_mixmm,
                           deps=(t_nrm, ps_mix_free[bank], ps_rexp_free[bank],
                                 mix_slot_free[sl], t_clip))
            last_mixmm = t_mixmm

            def f_rexpcp(eng, bank=bank):
                eng.copy(out=rexp_sb(bank),
                         in_=psB[0][:, bank * 512:(bank + 1) * 512]).then_inc(
                    sems["scalar"], 1)
            t_rexpcp = step("scalar", f_rexpcp,
                            deps=(t_mixmm, ps_rexp_free[bank], rexp_sb_free[bank]))
            ps_rexp_free[bank] = (t_rexpcp,)

            def f_div(eng, p=p, bank=bank, sl=sl):
                eng.tensor_tensor(out=mix_bf(sl)[:, p * 512:(p + 1) * 512],
                                  in0=psC[bank][:, :],
                                  in1=rexp_sb(bank),
                                  op=ALU.mult).then_inc(sems["vector"], 1)
            t_div = step("vector", f_div, deps=(t_mixmm, t_rexpcp))
            ps_mix_free[bank] = (t_div,)
            rexp_sb_free[bank] = (t_div,)
            mix_toks.append(t_div)
        recip_free = (last_mixmm,)
        q_slot_free[sl] = (last_mixmm, t_nmm)

        wo_toks = []
        sq_toks = []
        last_womm = None
        for tt in range(4):
            t_idx = ch * 4 + tt
            xsl = tt % 2
            t_x0 = dma("gpsimd", f"x0{xsl}", x0_tile(xsl),
                       x_in[t_idx * 128:(t_idx + 1) * 128, :], deps=(x0_free[xsl],))

            def f_womm(eng, tt=tt, sl=sl):
                for c in range(DC):
                    for half in range(2):
                        eng.matmul(psB[1][:, half * 512:(half + 1) * 512],
                                   mix_bf(sl)[:, c * 512 + tt * 128: c * 512 + (tt + 1) * 128],
                                   wo_bf[:, c * D + half * 512: c * D + (half + 1) * 512],
                                   start=(c == 0), stop=False)
                for half in range(2):
                    ins = eng.matmul(psB[1][:, half * 512:(half + 1) * 512],
                                     ones_row[:, :],
                                     bo_bf[:, half * 512:(half + 1) * 512],
                                     start=False, stop=True)
                ins.then_inc(sems["tensor"], 1)
            t_womm = step("tensor", f_womm, deps=(tuple(mix_toks), ps_wo_free, t_wo))
            last_womm = t_womm

            def f_r1(eng, tt=tt, xsl=xsl):
                eng.scalar_tensor_tensor(
                    out=r1_st(tt), in0=psB[1][:, :], scalar=1.0,
                    in1=x0_tile(xsl), op0=ALU.mult, op1=ALU.add,
                    accum_out=rsums[:, tt:tt + 1]).then_inc(sems["vector"], 1)
            t_r1 = step("vector", f_r1, deps=(t_womm, t_x0, stats_tok, r1_free[tt]))
            ps_wo_free = (t_r1,)
            x0_free[xsl] = (t_r1,)
            wo_toks.append(t_r1)

            def f_sq(eng, tt=tt):
                eng.scalar_tensor_tensor(
                    out=sq_junk[:, :], in0=r1_st(tt), scalar=1.0, in1=r1_st(tt),
                    op0=ALU.mult, op1=ALU.mult,
                    accum_out=ssqs[:, tt:tt + 1]).then_inc(sems["vector"], 1)
            t_sq = step("vector", f_sq, deps=(t_r1, sqj_tok))
            sqj_tok = (t_sq,)
            sq_toks.append(t_sq)
        mix_slot_free[sl] = (last_womm,)

        def f_stats(eng):
            negm = ln_sc[:, 12:16]
            v = ln_sc[:, 4:8]
            vchain(eng, eng.tensor_scalar(out=negm, in0=rsums[:, 0:4], scalar1=-1.0 / D,
                                          scalar2=None, op0=ALU.mult))
            vchain(eng, eng.tensor_tensor(out=v, in0=negm, in1=negm, op=ALU.mult))
            vchain(eng, eng.scalar_tensor_tensor(out=v, in0=ssqs[:, 0:4], scalar=1.0 / D,
                                                 in1=v, op0=ALU.mult, op1=ALU.subtract))
            vchain(eng, eng.tensor_scalar(out=v, in0=v, scalar1=EPS_LN, scalar2=None,
                                          op0=ALU.add))
            eng.reciprocal(out=ln_sc[:, 8:12], in_=v).then_inc(sems["vector"], 1)
        t_stats = step("vector", f_stats, deps=(tuple(wo_toks), tuple(sq_toks)))

        def f_rstd(eng):
            eng.activation(out=ln_sc[:, 8:12], in_=ln_sc[:, 8:12],
                           func=AF.Sqrt).then_inc(sems["scalar"], 1)
        t_rstd = step("scalar", f_rstd, deps=(t_stats,))
        stats_tok = (t_rstd,)

        for tt in range(4):
            t_idx = ch * 4 + tt
            xsl = tt

            def f_aff(eng, tt=tt, xsl=xsl):
                eng.tensor_scalar(out=x1_bf(xsl), in0=r1_st(tt),
                                  scalar1=ln_sc[:, 12 + tt:13 + tt],
                                  scalar2=ln_sc[:, 8 + tt:9 + tt],
                                  op0=ALU.add, op1=ALU.mult).then_inc(sems["vector"], 1)
            t_aff = step("vector", f_aff, deps=(stats_tok, x1_free[xsl]))
            r1_free[tt] = (t_aff,)

            def f_xmax(eng, t_idx=t_idx, xsl=xsl):
                eng.tensor_reduce(out=xmax_cols[:, t_idx:t_idx + 1],
                                  in_=x1_bf(xsl), axis=AX, op=ALU.max,
                                  apply_absolute_value=True).then_inc(sems["vector"], 1)
            t_xmax = step("vector", f_xmax, deps=(t_aff,))
            xmax_toks.append(t_xmax)
            t_x1st = dma("gpsimd", f"x1st{xsl}", x1_d[t_idx, :, :], x1_bf(xsl),
                         deps=(t_aff,))

            tq = "sync"

            def f_x1tr(eng, t_idx=t_idx, xsl=xsl):
                for c in range(DC):
                    eng.dma_start(out=xq_ap(c)[:, t_idx * 128:(t_idx + 1) * 128],
                                  in_=x1_bf(xsl)[:, c * 128:(c + 1) * 128],
                                  transpose=True).then_inc(sems[f"x1tr{xsl}"], 16)
            t_x1tr = plan.add(tq, f_x1tr, (t_aff, bar2), inc=16 * DC,
                              stream=f"x1tr{xsl}")
            x1_free[xsl] = (t_x1st, t_xmax, t_x1tr)
            x1_store_toks.append(t_x1st)
            x1tr_toks.append(t_x1tr)

    bar3 = plan.barrier()
    stk_b.close()

    def f_xred(eng):
        eng.tensor_reduce(out=red_sc[:, :], in_=xmax_cols[:, :], axis=AX,
                          op=ALU.max).then_inc(sems["vector"], 1)
    t_xred = step("vector", f_xred, deps=(tuple(xmax_toks),))
    t_pm1 = dma("gpsimd", "pm", pmax_d, red_sc[:, :], deps=(t_xred,))
    t_pm1b = dma("gpsimd", "pm", smax1[:, :], pmax_d, deps=(t_pm1,))

    def f_xred2(eng):
        eng.tensor_reduce(out=gmax1[:, :], in_=smax1[:, :], axis=AX,
                          op=ALU.max).then_inc(sems["vector"], 1)
    t_xred2 = step("vector", f_xred2, deps=(t_pm1b,))
    t_m1o = dma("gpsimd", "cc", cc_m1_i, gmax1[:, :], deps=(t_xred2,))

    def f_cc_m1(eng):
        eng.collective_compute(
            "AllReduce", ALU.max, replica_groups=[list(range(NCORES))],
            ins=[cc_m1_i.opt()], outs=[cc_m1_o.opt()]).then_inc(sems["gpsimd"], 1)
    t_cc_m1 = step("gpsimd", f_cc_m1, deps=(t_m1o,), inc=1)
    t_m1i = dma("gpsimd", "cc", gmax1[:, :], cc_m1_o, deps=(t_cc_m1,))

    def f_sigx(eng):
        eng.activation(out=sc_small[:, 0:1], in_=clipx_s[:, :],
                       func=AF.Sigmoid).then_inc(sems["scalar"], 1)
    t_sigx = step("scalar", f_sigx, deps=(t_clip, bar3))

    def f_sx(eng):
        vchain(eng, eng.tensor_tensor(out=sc_small[:, 1:2], in0=sc_small[:, 0:1],
                                      in1=gmax1[:, :], op=ALU.mult))
        vchain(eng, eng.tensor_scalar(out=sc_small[:, 1:2], in0=sc_small[:, 1:2],
                                      scalar1=1.0 / QMAX, scalar2=None, op0=ALU.mult))
        eng.reciprocal(out=sc_small[:, 2:3], in_=sc_small[:, 1:2]).then_inc(
            sems["vector"], 1)
    t_sx = step("vector", f_sx, deps=(t_sigx, t_m1i))
    t_b1 = dma("gpsimd", "bc1", bc1_d, sc_small[:, 1:2], deps=(t_sx,))
    t_b2 = dma("gpsimd", "bc2", bc2_d, sc_small[:, 2:3], deps=(t_sx,))
    t_sxbc = dma("gpsimd", "bc1", sx_bc[:, :], bc1_d.partition_broadcast(128),
                 deps=(t_b1,))
    t_rsxbc = dma("gpsimd", "bc2", rsx_bc[:, :], bc2_d.partition_broadcast(128),
                  deps=(t_b2,))

    def f_svec(eng):
        vchain(eng, eng.tensor_scalar(out=svec_g[:, :], in0=swg_cols[:, :],
                                      scalar1=sx_bc[:, :], scalar2=1.0 / QMAX,
                                      op0=ALU.mult, op1=ALU.mult))
        eng.tensor_scalar(out=svec_u[:, :], in0=swu_cols[:, :],
                          scalar1=sx_bc[:, :], scalar2=1.0 / QMAX,
                          op0=ALU.mult, op1=ALU.mult).then_inc(sems["vector"], 1)
    t_svec = step("vector", f_svec, deps=(t_sxbc, tuple(g_row_toks), tuple(u_row_toks)))

    stk_c1 = ExitStack()

    def sb_c1(name, shape, dtype):
        return stk_c1.enter_context(nc.sbuf_tensor(name, shape, dtype))

    xqi8 = sb_c1("xqi8", [128, 512], I8)
    wfm_ = sb_c1("wfm", [128, 2 * 2 * D], BF16)
    wfm = lambda s, gu: wfm_[:, (s * 2 + gu) * D:(s * 2 + gu + 1) * D]
    g_act = sb_c1("g_act", [128, 512], F32)
    h_bf_ = sb_c1("h_bf", [128, 2 * 512], BF16)
    h_bf = lambda s: h_bf_[:, s * 512:(s + 1) * 512]

    g_banks = [psA[0][:, :], psA[1][:, :], psB[0][:, 0:512], psB[0][:, 512:1024]]
    u_banks = [psC[0][:, :], psC[1][:, :], psB[1][:, 0:512], psB[1][:, 512:1024]]
    quant_toks = []
    for ch in range(NCH):
        ch_tr4 = tuple(x1tr_toks[ch * 4 + k] for k in range(4))

        def f_xq(eng, ch=ch):
            for hc in range(DC):
                strip = xq_ap(hc)[:, ch * 512:(ch + 1) * 512]
                vchain(eng, eng.tensor_scalar(out=xqi8[:, :], in0=strip,
                                              scalar1=rsx_bc[:, :], scalar2=None,
                                              op0=ALU.mult))
                ins = eng.tensor_copy(out=strip, in_=xqi8[:, :])
                if hc < DC - 1:
                    vchain(eng, ins)
            ins.then_inc(sems["vector"], 1)
        prev_q = quant_toks[-1] if quant_toks else ()
        t_xq = step("vector", f_xq, deps=(t_rsxbc, ch_tr4, bar3, prev_q))
        quant_toks.append(t_xq)

    def wload(i, wsl, deps):
        src_g = wgq_d[:, :, i * 128:(i + 1) * 128].transpose([1, 0, 2])
        src_u = wuq_d[:, :, i * 128:(i + 1) * 128].transpose([1, 0, 2])
        return dmas("sync", f"wl{wsl}",
                    [(wfm(wsl, 0).rearrange("p (c j) -> p c j", j=128), src_g),
                     (wfm(wsl, 1).rearrange("p (c j) -> p c j", j=128), src_u)],
                    deps=deps)

    wfm_free = [(), ()]
    ps_g_free = [(), (), (), ()]
    ps_u_free = [(), (), (), ()]
    g_act_free = ()
    h_slot_free = [(), ()]
    wl_toks = [None] * IC
    h_store_toks = []
    hmax_toks = []
    last_gumm = ()

    wl_toks[0] = wload(0, 0, deps=(bar3,))
    if IC > 1:
        wl_toks[1] = wload(1, 1, deps=(bar3,))

    for i in range(IC):
        wsl = i % 2
        t_wl = wl_toks[i]

        gumm_toks = []
        for ch in range(NCH):
            bank = ch % 4

            def f_gumm(eng, wsl=wsl, ch=ch, bank=bank):
                for hc in range(DC):
                    eng.matmul(g_banks[bank],
                               wfm(wsl, 0)[:, hc * 128:(hc + 1) * 128],
                               xq_ap(hc)[:, ch * 512:(ch + 1) * 512],
                               start=(hc == 0), stop=(hc == DC - 1))
                for hc in range(DC):
                    ins = eng.matmul(u_banks[bank],
                                     wfm(wsl, 1)[:, hc * 128:(hc + 1) * 128],
                                     xq_ap(hc)[:, ch * 512:(ch + 1) * 512],
                                     start=(hc == 0), stop=(hc == DC - 1))
                ins.then_inc(sems["tensor"], 1)
            t_gumm = step("tensor", f_gumm,
                          deps=(t_wl, quant_toks[ch], ps_g_free[bank], ps_u_free[bank]))
            gumm_toks.append(t_gumm)

            def f_gact(eng, i=i, bank=bank):
                eng.activation(out=g_act[:, :], in_=g_banks[bank], func=AF.Silu,
                               scale=svec_g[:, i:i + 1]).then_inc(sems["scalar"], 1)
            t_gact = step("scalar", f_gact, deps=(t_gumm, t_svec, g_act_free))

            hsl = ch % 2

            def f_h(eng, i=i, ch=ch, hsl=hsl, bank=bank):
                vchain(eng, eng.scalar_tensor_tensor(
                    out=h_bf(hsl), in0=u_banks[bank], scalar=svec_u[:, i:i + 1],
                    in1=g_act[:, :], op0=ALU.mult, op1=ALU.mult))
                eng.tensor_reduce(out=hmax_cols[:, i * NCH + ch:i * NCH + ch + 1],
                                  in_=h_bf(hsl), axis=AX, op=ALU.max,
                                  apply_absolute_value=True).then_inc(sems["vector"], 1)
            t_h = step("vector", f_h, deps=(t_gact, h_slot_free[hsl], t_svec))
            hmax_toks.append(t_h)
            ps_g_free[bank] = (t_h,)
            ps_u_free[bank] = (t_h,)
            g_act_free = (t_h,)
            t_hst = dma("gpsimd", f"hst{hsl}", h_d[i, :, ch * 512:(ch + 1) * 512],
                        h_bf(hsl), deps=(t_h,))
            h_slot_free[hsl] = (t_hst,)
            h_store_toks.append(t_hst)

        last_gumm = (gumm_toks[-1],)
        if DEBUG_DUMPS and i == 20:
            dmas("gpsimd", "dbgw", [(dbg_wfm[:, 0:D], wfm(wsl, 0)),
                                    (dbg_wfm[:, D:2 * D], wfm(wsl, 1))],
                 deps=(gumm_toks[-1],))
        if i + 2 < IC:
            wfm_free[wsl] = (gumm_toks[-1],)
            wl_toks[i + 2] = wload(i + 2, wsl, deps=(gumm_toks[-1],))

    if DEBUG_DUMPS:
        dmas("gpsimd", "dbgd",
             [(dbg_xq[c, :, :], xq_ap(c)) for c in range(DC)],
             deps=(last_gumm, tuple(quant_toks)))

    t_swdr = dmas("gpsimd", "swd", [(swd_dram[:, ht * 128:(ht + 1) * 128],
                                     swd_cols[:, ht:ht + 1]) for ht in range(HT)],
                  deps=(tuple(d_row_toks),))
    t_swdl = dma("gpsimd", "swd", swd_row[:, :], swd_dram, deps=(t_swdr,))

    d_free = (last_gumm, g_act_free, h_slot_free[0], h_slot_free[1],
              tuple(h_store_toks))
    bar4 = plan.barrier()
    stk_c1.close()
    stk_c.close()

    stk_d = ExitStack()

    def sb_d(name, shape, dtype):
        return stk_d.enter_context(nc.sbuf_tensor(name, shape, dtype))

    h_ld_ = sb_d("h_ld", [128, 2 * IC * 512], BF16)
    h_ld = lambda s, ic: h_ld_[:, (s * IC + ic) * 512:(s * IC + ic + 1) * 512]
    hq_i8_ = sb_d("hq_i8", [128, 2 * 1024], I8)
    hq_i8 = lambda s: hq_i8_[:, s * 1024:(s + 1) * 1024]
    x1f_ = sb_d("x1f", [128, 2 * D], BF16)
    x1f = lambda s: x1f_[:, s * D:(s + 1) * D]
    r2_st_ = sb_d("r2_st", [128, 4 * D], F32)
    r2_st = lambda s: r2_st_[:, s * D:(s + 1) * D]
    t2_st = sb_d("t2_st", [128, D], F32)
    out_t_ = sb_d("out_t", [128, 2 * D], F32)
    out_t = lambda s: out_t_[:, s * D:(s + 1) * D]
    sq_junk2 = sb_d("sq_junk2", [128, D], F32)
    swd_bc = sb_d("swd_bc", [128, D], F32)
    wd_fm = sb_d("wd_fm", [128, IC * D], BF16)

    def wdfm_ap(ic):
        return wd_fm[:, ic * D:(ic + 1) * D]

    t_wdfm = dmas("sync", "wdfm", [(wdfm_ap(ic), wdq_d[ic, :, :]) for ic in range(IC)],
                  deps=(d_free,))

    def f_hred(eng):
        eng.tensor_reduce(out=red_sc[:, :], in_=hmax_cols[:, :], axis=AX,
                          op=ALU.max).then_inc(sems["vector"], 1)
    t_hred = step("vector", f_hred, deps=(tuple(hmax_toks),))
    t_pm2 = dma("gpsimd", "pm", pmax_d, red_sc[:, :], deps=(t_hred, t_pm1b))
    t_pm2b = dma("gpsimd", "pm", smax1[:, :], pmax_d, deps=(t_pm2, t_xred2))

    def f_hred2(eng):
        eng.tensor_reduce(out=gmax2[:, :], in_=smax1[:, :], axis=AX,
                          op=ALU.max).then_inc(sems["vector"], 1)
    t_hred2 = step("vector", f_hred2, deps=(t_pm2b,))
    t_m2o = dma("gpsimd", "cc", cc_m2_i, gmax2[:, :], deps=(t_hred2,))

    def f_cc_m2(eng):
        eng.collective_compute(
            "AllReduce", ALU.max, replica_groups=[list(range(NCORES))],
            ins=[cc_m2_i.opt()], outs=[cc_m2_o.opt()]).then_inc(sems["gpsimd"], 1)
    t_cc_m2 = step("gpsimd", f_cc_m2, deps=(t_m2o,), inc=1)
    t_m2i = dma("gpsimd", "cc", gmax2[:, :], cc_m2_o, deps=(t_cc_m2,))

    def f_sigd(eng):
        eng.activation(out=sc_small[:, 3:4], in_=clipd_s[:, :],
                       func=AF.Sigmoid).then_inc(sems["scalar"], 1)
    t_sigd = step("scalar", f_sigd, deps=(t_clip, bar4))

    def f_sh(eng):
        vchain(eng, eng.tensor_tensor(out=sc_small[:, 4:5], in0=sc_small[:, 3:4],
                                      in1=gmax2[:, :], op=ALU.mult))
        vchain(eng, eng.tensor_scalar(out=sc_small[:, 4:5], in0=sc_small[:, 4:5],
                                      scalar1=1.0 / QMAX, scalar2=None, op0=ALU.mult))
        vchain(eng, eng.reciprocal(out=sc_small[:, 5:6], in_=sc_small[:, 4:5]))
        eng.tensor_scalar(out=sc_small[:, 7:8], in0=sc_small[:, 4:5],
                          scalar1=1.0 / QMAX, scalar2=None,
                          op0=ALU.mult).then_inc(sems["vector"], 1)
    t_sh = step("vector", f_sh, deps=(t_sigd, t_m2i))
    t_b3 = dma("gpsimd", "bc3", bc3_d, sc_small[:, 5:6], deps=(t_sh,))
    t_rshbc = dma("gpsimd", "bc3", rsh_bc[:, :], bc3_d.partition_broadcast(128),
                  deps=(t_b3,))

    def f_swdsc(eng):
        eng.activation(out=swd_sc[:, :], in_=swd_row[:, :], func=AF.Copy,
                       scale=sc_small[:, 7:8]).then_inc(sems["scalar"], 1)
    t_swdsc = step("scalar", f_swdsc, deps=(t_sh, t_swdl))
    t_swdd = dma("gpsimd", "swd", swd_dram, swd_sc[:, :], deps=(t_swdsc, t_swdl))
    t_swdbc = dma("gpsimd", "swd", swd_bc[:, :], swd_dram.partition_broadcast(128),
                  deps=(t_swdd,))

    h_ld_free = [(), ()]
    ps_dn_free = [(), ()]
    x1f_free = [(), ()]
    out_free = [(), ()]
    stats2_tok = ()
    r2_free = [()] * 4
    i8_free = [(), ()]
    out_toks = []

    for ch in range(NCH):
        sl = ch % 2
        t_hld = dmas("sync", f"hld{sl}",
                     [(h_ld(sl, ic), h_d[ic, :, ch * 512:(ch + 1) * 512])
                      for ic in range(IC)],
                     deps=(d_free, h_ld_free[sl]))

        hq_toks = []
        for j, ic in enumerate(range(0, IC, 2)):
            n2 = min(2, IC - ic)
            isl = j % 2

            def f_hqs(eng, sl=sl, ic=ic, isl=isl, n2=n2):
                eng.activation(out=hq_i8(isl)[:, 0:n2 * 512],
                               in_=h_ld_[:, (sl * IC + ic) * 512:
                                         (sl * IC + ic + n2) * 512],
                               func=AF.Copy,
                               scale=rsh_bc[:, :]).then_inc(sems["scalar"], 1)
            t_hqs = step("scalar", f_hqs, deps=(t_hld, t_rshbc, i8_free[isl]))

            def f_hqv(eng, sl=sl, ic=ic, isl=isl, n2=n2):
                eng.tensor_copy(out=h_ld_[:, (sl * IC + ic) * 512:
                                          (sl * IC + ic + n2) * 512],
                                in_=hq_i8(isl)[:, 0:n2 * 512]).then_inc(
                    sems["vector"], 1)
            t_hqv = step("vector", f_hqv, deps=(t_hqs,))
            i8_free[isl] = (t_hqv,)
            hq_toks.append(t_hqv)
        all_hq = tuple(hq_toks)

        dn_toks = []
        sq2_toks = []
        for tt in range(4):
            t_idx = ch * 4 + tt
            xsl = tt % 2
            dbank = tt % 2
            t_x1f = dma("gpsimd", f"x1f{xsl}", x1f(xsl), x1_d[t_idx, :, :],
                        deps=(d_free, bar3, x1f_free[xsl]))

            def f_dnmm(eng, sl=sl, tt=tt, dbank=dbank):
                for ic in range(IC):
                    for half in range(2):
                        ins = eng.matmul(
                            psB[dbank][:, half * 512:(half + 1) * 512],
                            h_ld(sl, ic)[:, tt * 128:(tt + 1) * 128],
                            wdfm_ap(ic)[:, half * 512:(half + 1) * 512],
                            start=(ic == 0), stop=(ic == IC - 1))
                ins.then_inc(sems["tensor"], 1)
            t_dnmm = step("tensor", f_dnmm,
                          deps=(all_hq, ps_dn_free[dbank], t_wdfm, bar4))

            def f_r2(eng, tt=tt, xsl=xsl, dbank=dbank):
                vchain(eng, eng.tensor_tensor(out=t2_st[:, :], in0=psB[dbank][:, :],
                                              in1=swd_bc[:, :], op=ALU.mult))
                eng.scalar_tensor_tensor(
                    out=r2_st(tt), in0=t2_st[:, :], scalar=1.0,
                    in1=x1f(xsl), op0=ALU.mult, op1=ALU.add,
                    accum_out=rsums2[:, tt:tt + 1]).then_inc(sems["vector"], 1)
            prev_r2 = dn_toks[-1] if dn_toks else ()
            t_r2 = step("vector", f_r2,
                        deps=(t_dnmm, t_x1f, t_swdbc, stats2_tok, r2_free[tt], prev_r2))
            ps_dn_free[dbank] = (t_r2,)
            x1f_free[xsl] = (t_r2,)
            dn_toks.append(t_r2)

            def f_sq2(eng, tt=tt):
                eng.scalar_tensor_tensor(
                    out=sq_junk2[:, :], in0=r2_st(tt), scalar=1.0, in1=r2_st(tt),
                    op0=ALU.mult, op1=ALU.mult,
                    accum_out=ssqs2[:, tt:tt + 1]).then_inc(sems["vector"], 1)
            t_sq2 = step("vector", f_sq2, deps=(t_r2, sqj_tok))
            sqj_tok = (t_sq2,)
            sq2_toks.append(t_sq2)

        h_ld_free[sl] = (dn_toks[-1],)

        def f_stats2(eng):
            negm = ln2_sc[:, 12:16]
            v = ln2_sc[:, 4:8]
            vchain(eng, eng.tensor_scalar(out=negm, in0=rsums2[:, 0:4],
                                          scalar1=-1.0 / D, scalar2=None, op0=ALU.mult))
            vchain(eng, eng.tensor_tensor(out=v, in0=negm, in1=negm, op=ALU.mult))
            vchain(eng, eng.scalar_tensor_tensor(out=v, in0=ssqs2[:, 0:4],
                                                 scalar=1.0 / D, in1=v,
                                                 op0=ALU.mult, op1=ALU.subtract))
            vchain(eng, eng.tensor_scalar(out=v, in0=v, scalar1=EPS_LN, scalar2=None,
                                          op0=ALU.add))
            eng.reciprocal(out=ln2_sc[:, 8:12], in_=v).then_inc(sems["vector"], 1)
        t_stats2 = step("vector", f_stats2, deps=(tuple(dn_toks), tuple(sq2_toks)))

        def f_rstd2(eng):
            eng.activation(out=ln2_sc[:, 8:12], in_=ln2_sc[:, 8:12],
                           func=AF.Sqrt).then_inc(sems["scalar"], 1)
        t_rstd2 = step("scalar", f_rstd2, deps=(t_stats2,))
        stats2_tok = (t_rstd2,)

        for tt in range(4):
            t_idx = ch * 4 + tt
            osl = tt % 2

            def f_aff2(eng, tt=tt, osl=osl):
                eng.tensor_scalar(out=out_t(osl), in0=r2_st(tt),
                                  scalar1=ln2_sc[:, 12 + tt:13 + tt],
                                  scalar2=ln2_sc[:, 8 + tt:9 + tt],
                                  op0=ALU.add, op1=ALU.mult).then_inc(sems["vector"], 1)
            t_aff2 = step("vector", f_aff2, deps=(stats2_tok, out_free[osl]))
            r2_free[tt] = (t_aff2,)
            t_ost = dma("gpsimd", f"ost{osl}", out_d[t_idx * 128:(t_idx + 1) * 128, :],
                        out_t(osl), deps=(t_aff2,))
            out_free[osl] = (t_ost,)
            out_toks.append(t_ost)

    stk_d.close()

    if DEBUG_DUMPS:
        t_dbg = dmas("gpsimd", "dbgd",
                     [(dbg_x1, x1_d), (dbg_h, h_d), (dbg_q, q_d), (dbg_wg, wgq_d)],
                     deps=(tuple(out_toks),))
        t_dbg2 = dma("gpsimd", "dbgd", dbg_kv, cc_kv_o, deps=(t_dbg,))
        out_toks.append(t_dbg2)

    def f_final(eng):
        eng.memset(sc_small[:, 6:7], 0.0).then_inc(sems["vector"], 1)
    step("vector", f_final, deps=(tuple(out_toks),))

    for s in plan.count:
        sems[s] = nc.alloc_semaphore(name=f"sem_{s}")
    sems["vchain"] = nc.alloc_semaphore(name="sem_vchain")
    sems["schain"] = nc.alloc_semaphore(name="sem_schain")

    def emit(eng_name, eng):
        watermark = {s: 0 for s in plan.count}
        for (e, fn, deps) in plan.steps:
            if e != eng_name:
                continue
            for (de, dv) in deps:
                if watermark[de] >= dv:
                    continue
                eng.wait_ge(sems[de], dv)
                watermark[de] = dv
            fn(eng)

    with nc.Block() as block:
        @block.tensor
        def _(tensor):
            emit("tensor", tensor)

        @block.vector
        def _(vector):
            emit("vector", vector)

        @block.scalar
        def _(scalar):
            emit("scalar", scalar)

        @block.sync
        def _(sync):
            emit("sync", sync)

        @block.gpsimd
        def _(gpsimd):
            emit("gpsimd", gpsimd)

    return nc


_NC_CACHE = []


def kernel(L0_attn_inputs_q, L0_attn_Wq, L0_attn_bq, L0_attn_Wk, L0_attn_bk,
           L0_attn_Wv, L0_attn_bv, L0_attn_Wo, L0_attn_bo,
           L0_ffn_clip_x, L0_ffn_clip_down, L0_ffn_w_gate, L0_ffn_w_up,
           L0_ffn_w_down):
    x = np.ascontiguousarray(np.asarray(L0_attn_inputs_q, dtype=np.float32))
    onehot = np.zeros((NH, NP * 128), dtype=ml_dtypes.bfloat16)
    for p in range(NP):
        onehot[2 * p, p * 128:p * 128 + 64] = 1.0
        onehot[2 * p + 1, p * 128 + 64:p * 128 + 128] = 1.0

    wg = np.ascontiguousarray(L0_ffn_w_gate, dtype=np.float32)
    wu = np.ascontiguousarray(L0_ffn_w_up, dtype=np.float32)
    wd = np.ascontiguousarray(L0_ffn_w_down, dtype=np.float32)
    common = {
        "wq": np.ascontiguousarray(L0_attn_Wq, dtype=np.float32),
        "wk": np.ascontiguousarray(L0_attn_Wk, dtype=np.float32),
        "wv": np.ascontiguousarray(L0_attn_Wv, dtype=np.float32),
        "wo": np.ascontiguousarray(L0_attn_Wo, dtype=np.float32),
        "bq": np.ascontiguousarray(L0_attn_bq, dtype=np.float32).reshape(1, D),
        "bk": np.ascontiguousarray(L0_attn_bk, dtype=np.float32).reshape(1, D),
        "bv": np.ascontiguousarray(L0_attn_bv, dtype=np.float32).reshape(1, D),
        "bo": np.ascontiguousarray(L0_attn_bo, dtype=np.float32).reshape(1, D),
        "clipx": np.ascontiguousarray(L0_ffn_clip_x, dtype=np.float32).reshape(1, 1),
        "clipd": np.ascontiguousarray(L0_ffn_clip_down, dtype=np.float32).reshape(1, 1),
        "wg": wg, "wu": wu, "wd": wd,
        "wgt": np.ascontiguousarray(wg.T),
        "wut": np.ascontiguousarray(wu.T),
        "wdt": np.ascontiguousarray(wd.T),
        "onehot": onehot,
    }
    in_maps = []
    for c in range(NCORES):
        b, half = c // 2, c % 2
        xc = np.ascontiguousarray(x[b, half * T:(half + 1) * T, :])
        in_maps.append({"x": xc, **common})

    if not _NC_CACHE:
        _NC_CACHE.append(build_program())
    nc = _NC_CACHE[0]
    res = run_bass_kernel_spmd(nc, in_maps, core_ids=list(range(NCORES)))
    out = np.empty((B, S, D), dtype=np.float32)
    for c in range(NCORES):
        b, half = c // 2, c % 2
        out[b, half * T:(half + 1) * T, :] = res.results[c]["out"]
    return out


# version 15
# speedup vs baseline: 1.0632x; 1.0632x over previous
from contextlib import ExitStack

import numpy as np
import ml_dtypes
import concourse.bass as bass
import concourse.mybir as mybir
from concourse.bass_utils import run_bass_kernel_spmd

F32 = mybir.dt.float32
BF16 = mybir.dt.bfloat16
I8 = mybir.dt.int8
AF = mybir.ActivationFunctionType
ALU = mybir.AluOpType
AX = mybir.AxisListType.X

DEBUG_DUMPS = False
NCORES = 8
B, S, D = 4, 8192, 1024
T = 4096
NT = T // 128
NCH = T // 512
DC = D // 128
NH, DH = 16, 64
NP = NH // 2
I = 2816
IC = I // 128
HT = D // 128
EPS_ATTN = 1e-6
EPS_LN = 1e-5
QMAX = 127.0

ENGS = ("tensor", "vector", "scalar", "sync", "gpsimd")
COMPUTE_ENGS = ("tensor", "vector", "scalar")


class Plan:

    def __init__(self):
        self.steps = []
        self.count = {}
        self.eng_of = {}

    @staticmethod
    def _flatten(deps, out):
        for d in deps:
            if d is None or d == () or d == []:
                continue
            if isinstance(d, tuple) and len(d) == 2 and isinstance(d[0], str):
                out.append(d)
            else:
                Plan._flatten(d, out)

    def add(self, eng, fn, deps=(), inc=1, stream=None):
        stream = stream or eng
        self.eng_of.setdefault(stream, eng)
        flat = []
        self._flatten(deps, flat)
        self.count[stream] = self.count.get(stream, 0) + inc
        tok = (stream, self.count[stream])
        self.steps.append((eng, fn, tuple(flat)))
        return tok

    def barrier(self):
        return tuple((s, v) for s, v in self.count.items() if v > 0)


def build_program():
    nc = bass.Bass("TRN2", target_bir_lowering=False)
    plan = Plan()

    x_in = nc.dram_tensor("x", [T, D], F32, kind="ExternalInput").ap()
    wq_in = nc.dram_tensor("wq", [D, D], F32, kind="ExternalInput").ap()
    wk_in = nc.dram_tensor("wk", [D, D], F32, kind="ExternalInput").ap()
    wv_in = nc.dram_tensor("wv", [D, D], F32, kind="ExternalInput").ap()
    wo_in = nc.dram_tensor("wo", [D, D], F32, kind="ExternalInput").ap()
    bq_in = nc.dram_tensor("bq", [1, D], F32, kind="ExternalInput").ap()
    bk_in = nc.dram_tensor("bk", [1, D], F32, kind="ExternalInput").ap()
    bv_in = nc.dram_tensor("bv", [1, D], F32, kind="ExternalInput").ap()
    bo_in = nc.dram_tensor("bo", [1, D], F32, kind="ExternalInput").ap()
    clipx_in = nc.dram_tensor("clipx", [1, 1], F32, kind="ExternalInput").ap()
    clipd_in = nc.dram_tensor("clipd", [1, 1], F32, kind="ExternalInput").ap()
    wg_in = nc.dram_tensor("wg", [I, D], F32, kind="ExternalInput").ap()
    wu_in = nc.dram_tensor("wu", [I, D], F32, kind="ExternalInput").ap()
    wd_in = nc.dram_tensor("wd", [D, I], F32, kind="ExternalInput").ap()
    wgt_in = nc.dram_tensor("wgt", [D, I], F32, kind="ExternalInput").ap()
    wut_in = nc.dram_tensor("wut", [D, I], F32, kind="ExternalInput").ap()
    wdt_in = nc.dram_tensor("wdt", [I, D], F32, kind="ExternalInput").ap()
    oh_in = nc.dram_tensor("onehot", [NH, NP * 128], BF16, kind="ExternalInput").ap()
    out_d = nc.dram_tensor("out", [T, D], F32, kind="ExternalOutput").ap()
    if DEBUG_DUMPS:
        dbg_x1 = nc.dram_tensor("dbg_x1", [NT, 128, D], BF16, kind="ExternalOutput").ap()
        dbg_h = nc.dram_tensor("dbg_h", [IC, 128, T], BF16, kind="ExternalOutput").ap()
        dbg_q = nc.dram_tensor("dbg_q", [DC, 128, T], BF16, kind="ExternalOutput").ap()
        dbg_kv = nc.dram_tensor("dbg_kv", [128, NP * 65], F32, kind="ExternalOutput").ap()
        dbg_wg = nc.dram_tensor("dbg_wg", [DC, 128, I], BF16, kind="ExternalOutput").ap()
        dbg_xq = nc.dram_tensor("dbg_xq", [DC, 128, T], BF16, kind="ExternalOutput").ap()
        dbg_wfm = nc.dram_tensor("dbg_wfm", [128, 2 * D], BF16, kind="ExternalOutput").ap()

    q_d = nc.dram_tensor("q_d", [DC, 128, T], BF16).ap()
    x1_d = nc.dram_tensor("x1_d", [NT, 128, D], BF16).ap()
    h_d = nc.dram_tensor("h_d", [IC, 128, T], BF16).ap()
    wgq_d = nc.dram_tensor("wgq_d", [DC, 128, I], BF16).ap()
    wuq_d = nc.dram_tensor("wuq_d", [DC, 128, I], BF16).ap()
    wdq_d = nc.dram_tensor("wdq_d", [IC, 128, D], BF16).ap()
    rsc_d = nc.dram_tensor("rsc_d", [1, I], F32).ap()
    cc_kv_i = nc.dram_tensor("cc_kv_i", [128, NP * 65], F32).ap()
    cc_kv_o = nc.dram_tensor("cc_kv_o", [128, NP * 65], F32).ap()
    cc_m1_i = nc.dram_tensor("cc_m1_i", [1, 1], F32).ap()
    cc_m1_o = nc.dram_tensor("cc_m1_o", [1, 1], F32).ap()
    cc_m2_i = nc.dram_tensor("cc_m2_i", [1, 1], F32).ap()
    cc_m2_o = nc.dram_tensor("cc_m2_o", [1, 1], F32).ap()
    pmax_d = nc.dram_tensor("pmax_d", [1, 128], F32).ap()
    bc1_d = nc.dram_tensor("bc1_d", [1, 1], F32).ap()
    bc2_d = nc.dram_tensor("bc2_d", [1, 1], F32).ap()
    bc3_d = nc.dram_tensor("bc3_d", [1, 1], F32).ap()
    swd_dram = nc.dram_tensor("swd_dram", [1, D], F32).ap()

    sems = {}
    chain_cnt = {"vector": 0, "scalar": 0}

    def vchain(eng, ins):
        chain_cnt["vector"] += 1
        ins.then_inc(sems["vchain"], 1)
        eng.wait_ge(sems["vchain"], chain_cnt["vector"])

    def schain(eng, ins):
        chain_cnt["scalar"] += 1
        ins.then_inc(sems["schain"], 1)
        eng.wait_ge(sems["schain"], chain_cnt["scalar"])

    def dma(queue, stream, out, in_, deps=(), slow=False):
        def fn(eng):
            eng.dma_start(out=out, in_=in_,
                          allow_slow_non_contiguous=slow).then_inc(sems[stream], 16)
        return plan.add(queue, fn, deps, inc=16, stream=stream)

    def dmas(queue, stream, pairs, deps=()):
        def fn(eng):
            for o, i_ in pairs:
                eng.dma_start(out=o, in_=i_).then_inc(sems[stream], 16)
        return plan.add(queue, fn, deps, inc=16 * len(pairs), stream=stream)

    def step(eng, fn, deps=(), inc=1):
        return plan.add(eng, fn, deps, inc)

    psA = [nc.alloc_psum_tensor(f"psA{i}", [128, 512], F32) for i in range(2)]
    psB = [nc.alloc_psum_tensor(f"psB{i}", [128, 1024], F32) for i in range(2)]
    psC = [nc.alloc_psum_tensor(f"psC{i}", [128, 512], F32) for i in range(2)]

    ones_row = nc.alloc_sbuf_tensor("ones_row", [1, 128], BF16)
    ones512 = nc.alloc_sbuf_tensor("ones512", [1, 512], BF16)
    bq_bf = nc.alloc_sbuf_tensor("bq_bf", [1, D], BF16)
    bk_bf = nc.alloc_sbuf_tensor("bk_bf", [1, D], BF16)
    bv_bf = nc.alloc_sbuf_tensor("bv_bf", [1, D], BF16)
    bo_bf = nc.alloc_sbuf_tensor("bo_bf", [1, D], BF16)
    clipx_s = nc.alloc_sbuf_tensor("clipx_s", [1, 1], F32)
    clipd_s = nc.alloc_sbuf_tensor("clipd_s", [1, 1], F32)
    oh_sb = nc.alloc_sbuf_tensor("oh_sb", [NH, NP * 128], BF16)
    kv_sb = nc.alloc_sbuf_tensor("kv_sb", [128, NP * 65], F32)
    kvt_sb = nc.alloc_sbuf_tensor("kvt_sb", [128, NP * 65], F32)
    xmax_cols = nc.alloc_sbuf_tensor("xmax_cols", [128, NT], F32)
    hmax_cols = nc.alloc_sbuf_tensor("hmax_cols", [128, IC * NCH], F32)
    smax1 = nc.alloc_sbuf_tensor("smax1", [1, 128], F32)
    gmax1 = nc.alloc_sbuf_tensor("gmax1", [1, 1], F32)
    gmax2 = nc.alloc_sbuf_tensor("gmax2", [1, 1], F32)
    sc_small = nc.alloc_sbuf_tensor("sc_small", [1, 8], F32)
    rsx_bc = nc.alloc_sbuf_tensor("rsx_bc", [128, 1], F32)
    sx_bc = nc.alloc_sbuf_tensor("sx_bc", [128, 1], F32)
    rsh_bc = nc.alloc_sbuf_tensor("rsh_bc", [128, 1], F32)
    swg_cols = nc.alloc_sbuf_tensor("swg_cols", [128, IC], F32)
    swu_cols = nc.alloc_sbuf_tensor("swu_cols", [128, IC], F32)
    swd_cols = nc.alloc_sbuf_tensor("swd_cols", [128, HT], F32)
    svec_g = nc.alloc_sbuf_tensor("svec_g", [128, IC], F32)
    svec_u = nc.alloc_sbuf_tensor("svec_u", [128, IC], F32)
    swd_row = nc.alloc_sbuf_tensor("swd_row", [1, D], F32)
    swd_sc = nc.alloc_sbuf_tensor("swd_sc", [1, D], F32)
    ln_sc = nc.alloc_sbuf_tensor("ln_sc", [128, 16], F32)
    red_sc = nc.alloc_sbuf_tensor("red_sc", [128, 1], F32)
    ln2_sc = nc.alloc_sbuf_tensor("ln2_sc", [128, 16], F32)
    rsums = nc.alloc_sbuf_tensor("rsums", [128, 4], F32)
    ssqs = nc.alloc_sbuf_tensor("ssqs", [128, 4], F32)
    rsums2 = nc.alloc_sbuf_tensor("rsums2", [128, 4], F32)
    ssqs2 = nc.alloc_sbuf_tensor("ssqs2", [128, 4], F32)
    rcols = nc.alloc_sbuf_tensor("rcols", [128, IC], F32)

    t_bias = dmas("gpsimd", "bias", [(bq_bf[:, :], bq_in), (bk_bf[:, :], bk_in),
                                     (bv_bf[:, :], bv_in), (bo_bf[:, :], bo_in)])
    t_clip = dmas("sync", "clip", [(clipx_s[:, :], clipx_in), (clipd_s[:, :], clipd_in),
                                   (oh_sb[:, :], oh_in)])

    def f_consts(eng):
        eng.memset(ones_row[:, :], 1.0)
        eng.memset(ones512[:, :], 1.0).then_inc(sems["vector"], 1)
    t_ones = step("vector", f_consts)

    stk_a = ExitStack()

    def sb_a(name, shape, dtype):
        return stk_a.enter_context(nc.sbuf_tensor(name, shape, dtype))

    wq_bf = sb_a("wq_bf", [128, DC * D], BF16)
    wk_bf = sb_a("wk_bf", [128, DC * D], BF16)
    wv_bf = sb_a("wv_bf", [128, DC * D], BF16)
    t_wq = dmas("gpsimd", "wq", [(wq_bf[:, k * D:(k + 1) * D], wq_in[k * 128:(k + 1) * 128, :])
                                 for k in range(DC)])
    t_wk = dmas("gpsimd", "wk", [(wk_bf[:, k * D:(k + 1) * D], wk_in[k * 128:(k + 1) * 128, :])
                                 for k in range(DC)])
    t_wv = dmas("gpsimd", "wv", [(wv_bf[:, k * D:(k + 1) * D], wv_in[k * 128:(k + 1) * 128, :])
                                 for k in range(DC)])

    wrow_ = sb_a("wrow", [128, 2 * I], F32)
    wrow_sl = lambda s: wrow_[:, s * I:(s + 1) * I]
    wtc_ = sb_a("wtc", [128, 3 * I], F32)
    wtc = lambda s: wtc_[:, s * I:(s + 1) * I]
    wti8_ = sb_a("wti8", [128, 2 * I], I8)
    wti8 = lambda s: wti8_[:, s * I:(s + 1) * I]
    rs_bc = sb_a("rs_bc", [128, I], BF16)
    rs2_bc = sb_a("rs2_bc", [128, 2 * D], BF16)

    prep_q = []
    ps = {"rowfree0": (), "rowfree1": (), "tcfree0": (), "tcfree1": (),
          "tcfree2": (), "i8free0": (), "i8free1": (), "rcols": (),
          "rsd": (), "rsbc": (), "rsbc_readers": []}

    def mk_rows(w_in, cols, n_rows, width, toks):
        nb_max = 2 if width <= 1024 else 1
        units = []
        bidx = [0]

        def mk(rt0, nb):
            def u():
                sl = bidx[0] % 2
                bidx[0] += 1
                buf = wrow_sl(sl)
                src_ap = w_in[rt0 * 128:(rt0 + nb) * 128, :].rearrange(
                    "(a p) d -> p a d", p=128)
                t_l = dma("gpsimd", f"pwr{sl}",
                          buf[:, 0:nb * width].rearrange("p (a d) -> p a d", d=width),
                          src_ap, deps=(ps[f"rowfree{sl}"],))

                def f_red(eng, rt0=rt0, nb=nb, cols=cols, width=width, buf=buf):
                    for j in range(nb):
                        ins = eng.tensor_reduce(
                            out=cols[:, rt0 + j:rt0 + j + 1],
                            in_=buf[:, j * width:(j + 1) * width],
                            axis=AX, op=ALU.max, apply_absolute_value=True)
                        if j < nb - 1:
                            vchain(eng, ins)
                    ins.then_inc(sems["vector"], 1)
                t_r = step("vector", f_red, deps=(t_l,))
                ps[f"rowfree{sl}"] = (t_r,)
                toks.append(t_r)
            return u
        rt0 = 0
        while rt0 < n_rows:
            nb = min(nb_max, n_rows - rt0)
            units.append(mk(rt0, nb))
            rt0 += nb
        return units

    def mk_rsc(cols, n, rowtoks):
        def u():
            def f_rc(eng, cols=cols, n=n):
                eng.reciprocal(out=rcols[:, 0:n], in_=cols[:, 0:n]).then_inc(
                    sems["vector"], 1)
            t_rc = step("vector", f_rc, deps=(tuple(rowtoks), ps["rcols"]))
            ps["rcols"] = (t_rc,)
            dest = rsc_d.rearrange("a (i p) -> (a p) i", p=128)[:, 0:n]
            t_w = dma("gpsimd", "prsd", dest, rcols[:, 0:n],
                      deps=(t_rc, ps["rsd"]), slow=True)
            t_bc = dma("gpsimd", "prsb", rs_bc[:, 0:n * 128],
                       rsc_d[:, 0:n * 128].partition_broadcast(128),
                       deps=(t_w, tuple(ps["rsbc_readers"])))
            ps["rsd"] = (t_bc,)
            ps["rsbc"] = (t_bc,)
            ps["rsbc_readers"] = []
        return [u]

    def mk_fm(wt_in, width, n_chunks, dest_dram):
        units = []
        cnt = [0]
        nc2 = max(1, I // width // 2 + (0 if width == I else 1))
        nc2 = 1 if width == I else 2

        def flush(eng_unused=None):
            st = ps.pop("pend_st", None)
            if st:
                st()

        def mk(hc, nb):
            def u():
                flush()
                sl = cnt[0] % 3
                j = cnt[0] % 2
                cnt[0] += 1
                t_l = dma("gpsimd", f"ptc{sl}",
                          wtc(sl)[:, 0:nb * width].rearrange(
                              "p (a d) -> p a d", d=width),
                          wt_in[hc * 128:(hc + nb) * 128, :].rearrange(
                              "(a p) d -> p a d", p=128),
                          deps=(ps[f"tcfree{sl}"],))

                def f_q(eng, sl=sl, j=j, width=width, nb=nb):
                    eng.scalar_tensor_tensor(
                        out=wti8(j)[:, 0:nb * width], in0=wtc(sl)[:, 0:nb * width],
                        scalar=QMAX,
                        in1=rs_bc[:, 0:width] if nb == 1 else rs2_bc[:, 0:nb * width],
                        op0=ALU.mult, op1=ALU.mult).then_inc(sems["vector"], 1)
                t_q = step("vector", f_q,
                           deps=(t_l, ps["rsbc"], ps.get("rs2", ()),
                                 ps[f"i8free{j}"]))
                ps[f"tcfree{sl}"] = (t_q,)
                ps["rsbc_readers"].append(t_q)

                def emit_st(hc=hc, nb=nb, j=j, t_q=t_q, width=width):
                    t_s = dma("gpsimd", f"pst{j}",
                              dest_dram[hc:hc + nb, :, 0:width].transpose([1, 0, 2]),
                              wti8(j)[:, 0:nb * width].rearrange(
                                  "p (a d) -> p a d", d=width),
                              deps=(t_q,))
                    ps[f"i8free{j}"] = (t_s,)
                ps["pend_st"] = emit_st
            return u
        hc = 0
        while hc < n_chunks:
            nb = min(nc2, n_chunks - hc)
            units.append(mk(hc, nb))
            hc += nb
        units.append(lambda: flush())
        return units

    g_row_toks, u_row_toks, d_row_toks = [], [], []
    prep_q += mk_rows(wg_in, swg_cols, IC, D, g_row_toks)
    prep_q += mk_rows(wu_in, swu_cols, IC, D, u_row_toks)
    prep_q += mk_rows(wd_in, swd_cols, HT, I, d_row_toks)
    prep_q += mk_rsc(swg_cols, IC, g_row_toks)
    prep_q += mk_fm(wgt_in, I, DC, wgq_d)
    prep_q += mk_rsc(swu_cols, IC, u_row_toks)
    prep_q += mk_fm(wut_in, I, DC, wuq_d)
    prep_q += mk_rsc(swd_cols, HT, d_row_toks)

    def u_rs2():
        def f_cp(eng):
            vchain(eng, eng.tensor_copy(out=rs2_bc[:, 0:D], in_=rs_bc[:, 0:D]))
            eng.tensor_copy(out=rs2_bc[:, D:2 * D],
                            in_=rs_bc[:, 0:D]).then_inc(sems["vector"], 1)
        t_cp = step("vector", f_cp, deps=(ps["rsbc"],))
        ps["rsbc_readers"].append(t_cp)
        ps["rs2"] = (t_cp,)
    prep_q += [u_rs2]
    prep_q += mk_fm(wdt_in, D, IC, wdq_d)
    prep_idx = [0]

    def pump(n):
        while n > 0 and prep_idx[0] < len(prep_q):
            prep_q[prep_idx[0]]()
            prep_idx[0] += 1
            n -= 1

    x_tok_bf_ = sb_a("x_tok_bf", [128, 2 * D], BF16)
    x_tok_bf = lambda s: x_tok_bf_[:, s * D:(s + 1) * D]

    xfm_ = sb_a("xfm", [128, 2 * DC * 512], BF16)
    q_e = sb_a("q_e", [128, 512], BF16)
    q_m = sb_a("q_m", [128, 512], BF16)
    q_out_ = sb_a("q_out", [128, DC * 512], BF16)
    k_e = sb_a("k_e", [128, 1024], BF16)
    k_m = sb_a("k_m", [128, 1024], BF16)
    k_bf_ = sb_a("k_bf", [128, 4 * 1024], BF16)
    k_bf = lambda s: k_bf_[:, s * 1024:(s + 1) * 1024]
    v_aug_ = sb_a("v_aug", [128, 4 * NH * 65], BF16)
    v_aug = lambda s: v_aug_[:, s * NH * 65:(s + 1) * NH * 65]

    def xfm_ap(sl, c):
        return xfm_[:, (sl * DC + c) * 512:(sl * DC + c + 1) * 512]

    def f_vones(eng):
        for tt in range(4):
            va = v_aug(tt).rearrange("p (h c) -> p h c", c=65)
            ins = eng.memset(va[:, :, 64:65], 1.0)
        ins.then_inc(sems["vector"], 1)
    t_vones = step("vector", f_vones)

    xtok_free = [(), ()]
    xf_free = ()
    xfm_free = [(), ()]
    tr_tok = [[None] * 4, [None] * 4]
    ps_q_free = [(), ()]
    ps_k_free = ()
    ps_v_free = ()
    ps_kv_free = [(), ()]
    q_e_free = ()
    q_out_free = ()
    k_e_free = ()
    kslot_free = [()] * 4
    kv_sb_tok = ()
    last_kv_mm = ()

    for ch in range(NCH):
        sl = ch % 2
        for tt in range(4):
            t_idx = ch * 4 + tt
            tsl = t_idx % 2
            t_xld = dma("gpsimd", f"xt{tsl}", x_tok_bf(tsl),
                        x_in[t_idx * 128:(t_idx + 1) * 128, :],
                        deps=(xtok_free[tsl],))

            def f_tr(eng, sl=sl, tt=tt, tsl=tsl):
                for c in range(DC):
                    eng.dma_start(out=xfm_ap(sl, c)[:, tt * 128:(tt + 1) * 128],
                                  in_=x_tok_bf(tsl)[:, c * 128:(c + 1) * 128],
                                  transpose=True).then_inc(sems[f"tr{tsl}"], 16)
            t_tr = plan.add("sync", f_tr, (t_xld, xfm_free[sl]), inc=16 * DC,
                            stream=f"tr{tsl}")
            tr_tok[sl][tt] = t_tr
            xtok_free[tsl] = (t_tr,)
        ch_tr = tuple(tr_tok[sl])

        q_mm_toks = []
        qvec_chunk_toks = []
        for m in range(DC):
            bank = m % 2

            def f_qmm(eng, m=m, bank=bank, sl=sl):
                for k in range(DC):
                    eng.matmul(psA[bank][:, :],
                               wq_bf[:, k * D + m * 128: k * D + (m + 1) * 128],
                               xfm_ap(sl, k),
                               start=(k == 0), stop=False)
                eng.matmul(psA[bank][:, :],
                           bq_bf[:, m * 128:(m + 1) * 128],
                           ones512[:, :],
                           start=False, stop=True).then_inc(sems["tensor"], 1)
            t_qmm = step("tensor", f_qmm,
                         deps=(t_wq, t_bias, t_ones, ps_q_free[bank], ch_tr))
            q_mm_toks.append(t_qmm)

            def f_qexp(eng, bank=bank):
                eng.activation(out=q_e[:, :], in_=psA[bank][:, :],
                               func=AF.Exp).then_inc(sems["scalar"], 1)
            t_qexp = step("scalar", f_qexp, deps=(t_qmm, q_e_free))

            def f_qvec(eng, bank=bank, m=m):
                vchain(eng, eng.tensor_scalar(out=q_m[:, :], in0=q_e[:, :],
                                              scalar1=1.0, scalar2=None, op0=ALU.min))
                eng.scalar_tensor_tensor(
                    out=q_out_[:, m * 512:(m + 1) * 512], in0=psA[bank][:, :],
                    scalar=0.0, in1=q_m[:, :], op0=ALU.max,
                    op1=ALU.add).then_inc(sems["vector"], 1)
            t_qvec = step("vector", f_qvec, deps=(t_qexp, q_out_free))
            ps_q_free[bank] = (t_qvec,)
            q_e_free = (t_qvec,)
            qvec_chunk_toks.append(t_qvec)
            pump(1)

        t_qst = dma("scalar", "qst",
                    q_d[:, :, ch * 512:(ch + 1) * 512].transpose([1, 0, 2]),
                    q_out_[:, :].rearrange("p (a d) -> p a d", d=512),
                    deps=(tuple(qvec_chunk_toks),))
        q_out_free = (t_qst,)

        kvec_toks = []
        for tt in range(4):
            def f_kvmm(eng, sl=sl, tt=tt):
                for k in range(DC):
                    lhs = xfm_ap(sl, k)[:, tt * 128:(tt + 1) * 128]
                    for half in range(2):
                        eng.matmul(psB[0][:, half * 512:(half + 1) * 512], lhs,
                                   wk_bf[:, k * D + half * 512: k * D + (half + 1) * 512],
                                   start=(k == 0), stop=False)
                        eng.matmul(psB[1][:, half * 512:(half + 1) * 512], lhs,
                                   wv_bf[:, k * D + half * 512: k * D + (half + 1) * 512],
                                   start=(k == 0), stop=False)
                for half in range(2):
                    eng.matmul(psB[0][:, half * 512:(half + 1) * 512], ones_row[:, :],
                               bk_bf[:, half * 512:(half + 1) * 512],
                               start=False, stop=True)
                    ins = eng.matmul(psB[1][:, half * 512:(half + 1) * 512], ones_row[:, :],
                                     bv_bf[:, half * 512:(half + 1) * 512],
                                     start=False, stop=True)
                ins.then_inc(sems["tensor"], 1)
            t_kvmm = step("tensor", f_kvmm,
                          deps=(t_wk, t_wv, t_bias, t_ones, ps_k_free, ps_v_free,
                                kslot_free[tt], ch_tr))

            def f_kexp(eng):
                eng.activation(out=k_e[:, :], in_=psB[0][:, :],
                               func=AF.Exp).then_inc(sems["scalar"], 1)
            t_kexp = step("scalar", f_kexp, deps=(t_kvmm, k_e_free))

            def f_kvec(eng, tt=tt):
                vchain(eng, eng.tensor_scalar(out=k_m[:, :], in0=k_e[:, :],
                                              scalar1=1.0, scalar2=None, op0=ALU.min))
                eng.scalar_tensor_tensor(
                    out=k_bf(tt), in0=psB[0][:, :], scalar=0.0,
                    in1=k_m[:, :], op0=ALU.max, op1=ALU.add)
                va = v_aug(tt).rearrange("p (h c) -> p h c", c=65)
                pv = psB[1][:, :].rearrange("p (h c) -> p h c", c=64)
                eng.tensor_copy(out=va[:, :, 0:64], in_=pv[:, :, :]).then_inc(
                    sems["vector"], 1)
            t_kvec = step("vector", f_kvec, deps=(t_kexp, t_vones, kslot_free[tt]))
            ps_k_free = (t_kvec,)
            ps_v_free = (t_kvec,)
            k_e_free = (t_kvec,)
            kvec_toks.append(t_kvec)
            pump(1)

        for p in range(NP):
            bank = p % 2

            def f_kvps(eng, p=p, bank=bank):
                hA, hB = 2 * p, 2 * p + 1
                for tt in range(4):
                    eng.matmul(psC[bank][0:64, 0:65],
                               k_bf(tt)[:, hA * 64:(hA + 1) * 64],
                               v_aug(tt)[:, hA * 65:(hA + 1) * 65],
                               start=(tt == 0), stop=(tt == 3),
                               tile_position=(0, 0))
                for tt in range(4):
                    ins = eng.matmul(psC[bank][64:128, 0:65],
                                     k_bf(tt)[:, hB * 64:(hB + 1) * 64],
                                     v_aug(tt)[:, hB * 65:(hB + 1) * 65],
                                     start=(tt == 0), stop=(tt == 3),
                                     tile_position=(0, 64))
                ins.then_inc(sems["tensor"], 1)
            t_kvps = step("tensor", f_kvps, deps=(tuple(kvec_toks), ps_kv_free[bank]))
            last_kv_mm = (t_kvps,)

            def f_kvacc(eng, p=p, bank=bank, first=(ch == 0)):
                dst = kv_sb[:, p * 65:(p + 1) * 65]
                if first:
                    eng.tensor_copy(out=dst, in_=psC[bank][:, 0:65]).then_inc(
                        sems["vector"], 1)
                else:
                    eng.tensor_tensor(out=dst, in0=dst, in1=psC[bank][:, 0:65],
                                      op=ALU.add).then_inc(sems["vector"], 1)
            t_kvacc = step("vector", f_kvacc, deps=(t_kvps, kv_sb_tok))
            kv_sb_tok = (t_kvacc,)
            ps_kv_free[bank] = (t_kvacc,)

        for tt in range(4):
            kslot_free[tt] = last_kv_mm
        xfm_free[sl] = (q_mm_toks[-1], t_kvmm)

        pump(2)

    pump(len(prep_q))

    t_kv_out = dma("gpsimd", "cc", cc_kv_i, kv_sb[:, :], deps=(kv_sb_tok,))

    def f_cc_kv(eng):
        eng.collective_compute(
            "AllReduce", ALU.add,
            replica_groups=[[0, 1], [2, 3], [4, 5], [6, 7]],
            ins=[cc_kv_i.opt()], outs=[cc_kv_o.opt()]).then_inc(sems["gpsimd"], 1)
    t_cc_kv = step("gpsimd", f_cc_kv, deps=(t_kv_out,), inc=1)
    t_kv_in = dma("gpsimd", "cc", kvt_sb[:, :], cc_kv_o, deps=(t_cc_kv,))

    bar2 = plan.barrier()
    stk_a.close()

    stk_c = ExitStack()
    xq_fm = stk_c.enter_context(nc.sbuf_tensor("xq_fm", [128, DC * T], BF16))

    def xq_ap(c):
        return xq_fm[:, c * T:(c + 1) * T]

    stk_b = ExitStack()

    def sb_b(name, shape, dtype):
        return stk_b.enter_context(nc.sbuf_tensor(name, shape, dtype))

    wo_bf = sb_b("wo_bf", [128, DC * D], BF16)
    blockdiag = sb_b("blockdiag", [128, NP * 128], BF16)
    normlhs = sb_b("normlhs", [128, DC * NH], BF16)
    q_ch_ = sb_b("q_ch", [128, 2 * DC * 512], BF16)
    q_ch = lambda s, c: q_ch_[:, (s * DC + c) * 512:(s * DC + c + 1) * 512]
    mix_bf_ = sb_b("mix_bf", [128, 2 * DC * 512], BF16)
    mix_bf = lambda s: mix_bf_[:, s * DC * 512:(s + 1) * DC * 512]
    norm_sb = sb_b("norm_sb", [16, 512], F32)
    norm_sb2 = sb_b("norm_sb2", [16, 512], F32)
    rexp_sb_ = sb_b("rexp_sb", [128, 2 * 512], F32)
    rexp_sb = lambda s: rexp_sb_[:, s * 512:(s + 1) * 512]
    recip_bf = sb_b("recip_bf", [16, 512], BF16)
    x0_tile_ = sb_b("x0_tile", [128, 2 * D], F32)
    x0_tile = lambda s: x0_tile_[:, s * D:(s + 1) * D]
    r1_st_ = sb_b("r1_st", [128, 4 * D], F32)
    r1_st = lambda s: r1_st_[:, s * D:(s + 1) * D]
    x1_bf_ = sb_b("x1_bf", [128, 4 * D], BF16)
    x1_bf = lambda s: x1_bf_[:, s * D:(s + 1) * D]
    sq_junk = sb_b("sq_junk", [128, D], F32)

    t_wo = dmas("gpsimd", "wo", [(wo_bf[:, k * D:(k + 1) * D], wo_in[k * 128:(k + 1) * 128, :])
                                 for k in range(DC)], deps=(bar2,))

    def f_blockdiag(eng):
        vchain(eng, eng.memset(blockdiag[:, :], 0.0))
        vchain(eng, eng.memset(normlhs[:, :], 0.0))
        for p in range(NP):
            eng.tensor_copy(out=blockdiag[0:64, p * 128: p * 128 + 64],
                            in_=kvt_sb[0:64, p * 65: p * 65 + 64])
            eng.tensor_copy(out=blockdiag[64:128, p * 128 + 64: p * 128 + 128],
                            in_=kvt_sb[64:128, p * 65: p * 65 + 64])
            eng.tensor_copy(out=normlhs[0:64, p * NH + 2 * p: p * NH + 2 * p + 1],
                            in_=kvt_sb[0:64, p * 65 + 64: p * 65 + 65])
            ins = eng.tensor_copy(out=normlhs[64:128, p * NH + 2 * p + 1: p * NH + 2 * p + 2],
                                  in_=kvt_sb[64:128, p * 65 + 64: p * 65 + 65])
        ins.then_inc(sems["vector"], 1)
    t_bd = step("vector", f_blockdiag, deps=(t_kv_in, bar2))

    q_slot_free = [(), ()]
    mix_slot_free = [(), ()]
    ps_norm_free = ()
    ps_mix_free = [(), ()]
    ps_rexp_free = [(), ()]
    rexp_sb_free = [(), ()]
    ps_wo_free = ()
    recip_free = ()
    x0_free = [bar2, bar2]
    x1_free = [(), (), (), ()]
    stats_tok = ()
    r1_free = [()] * 4
    sqj_tok = ()
    x1_store_toks = []
    x1tr_toks = []
    xmax_toks = []
    pend_tail_c = None

    def emit_tail_c(wo_toks, sq_toks, ch):
        nonlocal stats_tok

        def f_stats(eng):
            negm = ln_sc[:, 12:16]
            v = ln_sc[:, 4:8]
            vchain(eng, eng.tensor_scalar(out=negm, in0=rsums[:, 0:4], scalar1=-1.0 / D,
                                          scalar2=None, op0=ALU.mult))
            vchain(eng, eng.tensor_tensor(out=v, in0=negm, in1=negm, op=ALU.mult))
            vchain(eng, eng.scalar_tensor_tensor(out=v, in0=ssqs[:, 0:4], scalar=1.0 / D,
                                                 in1=v, op0=ALU.mult, op1=ALU.subtract))
            vchain(eng, eng.tensor_scalar(out=v, in0=v, scalar1=EPS_LN, scalar2=None,
                                          op0=ALU.add))
            eng.reciprocal(out=ln_sc[:, 8:12], in_=v).then_inc(sems["vector"], 1)
        t_stats = step("vector", f_stats, deps=(tuple(wo_toks), tuple(sq_toks)))

        def f_rstd(eng):
            eng.activation(out=ln_sc[:, 8:12], in_=ln_sc[:, 8:12],
                           func=AF.Sqrt).then_inc(sems["scalar"], 1)
        t_rstd = step("scalar", f_rstd, deps=(t_stats,))
        stats_tok = (t_rstd,)

        for tt in range(4):
            t_idx = ch * 4 + tt
            xsl = tt

            def f_aff(eng, tt=tt, xsl=xsl):
                eng.tensor_scalar(out=x1_bf(xsl), in0=r1_st(tt),
                                  scalar1=ln_sc[:, 12 + tt:13 + tt],
                                  scalar2=ln_sc[:, 8 + tt:9 + tt],
                                  op0=ALU.add, op1=ALU.mult).then_inc(sems["vector"], 1)
            t_aff = step("vector", f_aff, deps=((t_rstd,), x1_free[xsl]))
            r1_free[tt] = (t_aff,)

            def f_xmax(eng, t_idx=t_idx, xsl=xsl):
                eng.tensor_reduce(out=xmax_cols[:, t_idx:t_idx + 1],
                                  in_=x1_bf(xsl), axis=AX, op=ALU.max,
                                  apply_absolute_value=True).then_inc(sems["vector"], 1)
            t_xmax = step("vector", f_xmax, deps=(t_aff,))
            xmax_toks.append(t_xmax)
            t_x1st = dma("gpsimd", f"x1st{xsl}", x1_d[t_idx, :, :], x1_bf(xsl),
                         deps=(t_aff,))

            def f_x1tr(eng, t_idx=t_idx, xsl=xsl):
                for c in range(DC):
                    eng.dma_start(out=xq_ap(c)[:, t_idx * 128:(t_idx + 1) * 128],
                                  in_=x1_bf(xsl)[:, c * 128:(c + 1) * 128],
                                  transpose=True).then_inc(sems[f"x1tr{xsl}"], 16)
            t_x1tr = plan.add("sync", f_x1tr, (t_aff, bar2), inc=16 * DC,
                              stream=f"x1tr{xsl}")
            x1_free[xsl] = (t_x1st, t_xmax, t_x1tr)
            x1_store_toks.append(t_x1st)
            x1tr_toks.append(t_x1tr)

    for ch in range(NCH):
        sl = ch % 2
        t_qld = dmas("gpsimd", f"qch{sl}",
                     [(q_ch(sl, c), q_d[c, :, ch * 512:(ch + 1) * 512]) for c in range(DC)],
                     deps=(bar2, q_slot_free[sl]))

        def f_nmm(eng, sl=sl):
            for c in range(DC):
                ins = eng.matmul(psA[0][0:16, :],
                                 normlhs[:, c * NH:(c + 1) * NH],
                                 q_ch(sl, c),
                                 start=(c == 0), stop=(c == DC - 1))
            ins.then_inc(sems["tensor"], 1)
        t_nmm = step("tensor", f_nmm, deps=(t_qld, t_bd, ps_norm_free))

        def f_nrm(eng):
            vchain(eng, eng.tensor_scalar(out=norm_sb[:, :], in0=psA[0][0:16, :],
                                          scalar1=EPS_ATTN, scalar2=None, op0=ALU.max))
            vchain(eng, eng.reciprocal(out=norm_sb2[:, :], in_=norm_sb[:, :]))
            eng.tensor_copy(out=recip_bf[:, :], in_=norm_sb2[:, :]).then_inc(
                sems["vector"], 1)
        t_nrm = step("vector", f_nrm, deps=(t_nmm, recip_free))
        ps_norm_free = (t_nrm,)

        mix_toks = []
        last_mixmm = None
        for p in range(NP):
            bank = p % 2

            def f_mixmm(eng, p=p, bank=bank, sl=sl):
                eng.matmul(psC[bank][:, :],
                           blockdiag[:, p * 128:(p + 1) * 128],
                           q_ch(sl, p),
                           start=True, stop=True)
                eng.matmul(psB[0][:, bank * 512:(bank + 1) * 512],
                           oh_sb[:, p * 128:(p + 1) * 128],
                           recip_bf[:, :],
                           start=True, stop=True).then_inc(sems["tensor"], 1)
            t_mixmm = step("tensor", f_mixmm,
                           deps=(t_nrm, ps_mix_free[bank], ps_rexp_free[bank],
                                 mix_slot_free[sl], t_clip))
            last_mixmm = t_mixmm

            def f_rexpcp(eng, bank=bank):
                eng.copy(out=rexp_sb(bank),
                         in_=psB[0][:, bank * 512:(bank + 1) * 512]).then_inc(
                    sems["scalar"], 1)
            t_rexpcp = step("scalar", f_rexpcp,
                            deps=(t_mixmm, ps_rexp_free[bank], rexp_sb_free[bank]))
            ps_rexp_free[bank] = (t_rexpcp,)

            def f_div(eng, p=p, bank=bank, sl=sl):
                eng.tensor_tensor(out=mix_bf(sl)[:, p * 512:(p + 1) * 512],
                                  in0=psC[bank][:, :],
                                  in1=rexp_sb(bank),
                                  op=ALU.mult).then_inc(sems["vector"], 1)
            t_div = step("vector", f_div, deps=(t_mixmm, t_rexpcp))
            ps_mix_free[bank] = (t_div,)
            rexp_sb_free[bank] = (t_div,)
            mix_toks.append(t_div)
        recip_free = (last_mixmm,)
        q_slot_free[sl] = (last_mixmm, t_nmm)

        if pend_tail_c is not None:
            emit_tail_c(*pend_tail_c)
            pend_tail_c = None

        wo_toks = []
        sq_toks = []
        last_womm = None
        for tt in range(4):
            t_idx = ch * 4 + tt
            xsl = tt % 2
            t_x0 = dma("gpsimd", f"x0{xsl}", x0_tile(xsl),
                       x_in[t_idx * 128:(t_idx + 1) * 128, :], deps=(x0_free[xsl],))

            def f_womm(eng, tt=tt, sl=sl):
                for c in range(DC):
                    for half in range(2):
                        eng.matmul(psB[1][:, half * 512:(half + 1) * 512],
                                   mix_bf(sl)[:, c * 512 + tt * 128: c * 512 + (tt + 1) * 128],
                                   wo_bf[:, c * D + half * 512: c * D + (half + 1) * 512],
                                   start=(c == 0), stop=False)
                for half in range(2):
                    ins = eng.matmul(psB[1][:, half * 512:(half + 1) * 512],
                                     ones_row[:, :],
                                     bo_bf[:, half * 512:(half + 1) * 512],
                                     start=False, stop=True)
                ins.then_inc(sems["tensor"], 1)
            t_womm = step("tensor", f_womm, deps=(tuple(mix_toks), ps_wo_free, t_wo))
            last_womm = t_womm

            def f_r1(eng, tt=tt, xsl=xsl):
                eng.scalar_tensor_tensor(
                    out=r1_st(tt), in0=psB[1][:, :], scalar=1.0,
                    in1=x0_tile(xsl), op0=ALU.mult, op1=ALU.add,
                    accum_out=rsums[:, tt:tt + 1]).then_inc(sems["vector"], 1)
            t_r1 = step("vector", f_r1, deps=(t_womm, t_x0, stats_tok, r1_free[tt]))
            ps_wo_free = (t_r1,)
            x0_free[xsl] = (t_r1,)
            wo_toks.append(t_r1)

            def f_sq(eng, tt=tt):
                eng.scalar_tensor_tensor(
                    out=sq_junk[:, :], in0=r1_st(tt), scalar=1.0, in1=r1_st(tt),
                    op0=ALU.mult, op1=ALU.mult,
                    accum_out=ssqs[:, tt:tt + 1]).then_inc(sems["vector"], 1)
            t_sq = step("vector", f_sq, deps=(t_r1, sqj_tok))
            sqj_tok = (t_sq,)
            sq_toks.append(t_sq)
        mix_slot_free[sl] = (last_womm,)
        pend_tail_c = (wo_toks, sq_toks, ch)

    emit_tail_c(*pend_tail_c)
    pend_tail_c = None

    bar3 = plan.barrier()
    stk_b.close()

    def f_xred(eng):
        eng.tensor_reduce(out=red_sc[:, :], in_=xmax_cols[:, :], axis=AX,
                          op=ALU.max).then_inc(sems["vector"], 1)
    t_xred = step("vector", f_xred, deps=(tuple(xmax_toks),))
    t_pm1 = dma("gpsimd", "pm", pmax_d, red_sc[:, :], deps=(t_xred,))
    t_pm1b = dma("gpsimd", "pm", smax1[:, :], pmax_d, deps=(t_pm1,))

    def f_xred2(eng):
        eng.tensor_reduce(out=gmax1[:, :], in_=smax1[:, :], axis=AX,
                          op=ALU.max).then_inc(sems["vector"], 1)
    t_xred2 = step("vector", f_xred2, deps=(t_pm1b,))
    t_m1o = dma("gpsimd", "cc", cc_m1_i, gmax1[:, :], deps=(t_xred2,))

    def f_cc_m1(eng):
        eng.collective_compute(
            "AllReduce", ALU.max, replica_groups=[list(range(NCORES))],
            ins=[cc_m1_i.opt()], outs=[cc_m1_o.opt()]).then_inc(sems["gpsimd"], 1)
    t_cc_m1 = step("gpsimd", f_cc_m1, deps=(t_m1o,), inc=1)
    t_m1i = dma("gpsimd", "cc", gmax1[:, :], cc_m1_o, deps=(t_cc_m1,))

    def f_sigx(eng):
        eng.activation(out=sc_small[:, 0:1], in_=clipx_s[:, :],
                       func=AF.Sigmoid).then_inc(sems["scalar"], 1)
    t_sigx = step("scalar", f_sigx, deps=(t_clip, bar3))

    def f_sx(eng):
        vchain(eng, eng.tensor_tensor(out=sc_small[:, 1:2], in0=sc_small[:, 0:1],
                                      in1=gmax1[:, :], op=ALU.mult))
        vchain(eng, eng.tensor_scalar(out=sc_small[:, 1:2], in0=sc_small[:, 1:2],
                                      scalar1=1.0 / QMAX, scalar2=None, op0=ALU.mult))
        eng.reciprocal(out=sc_small[:, 2:3], in_=sc_small[:, 1:2]).then_inc(
            sems["vector"], 1)
    t_sx = step("vector", f_sx, deps=(t_sigx, t_m1i))
    t_b1 = dma("gpsimd", "bc1", bc1_d, sc_small[:, 1:2], deps=(t_sx,))
    t_b2 = dma("gpsimd", "bc2", bc2_d, sc_small[:, 2:3], deps=(t_sx,))
    t_sxbc = dma("gpsimd", "bc1", sx_bc[:, :], bc1_d.partition_broadcast(128),
                 deps=(t_b1,))
    t_rsxbc = dma("gpsimd", "bc2", rsx_bc[:, :], bc2_d.partition_broadcast(128),
                  deps=(t_b2,))

    def f_svec(eng):
        vchain(eng, eng.tensor_scalar(out=svec_g[:, :], in0=swg_cols[:, :],
                                      scalar1=sx_bc[:, :], scalar2=1.0 / QMAX,
                                      op0=ALU.mult, op1=ALU.mult))
        eng.tensor_scalar(out=svec_u[:, :], in0=swu_cols[:, :],
                          scalar1=sx_bc[:, :], scalar2=1.0 / QMAX,
                          op0=ALU.mult, op1=ALU.mult).then_inc(sems["vector"], 1)
    t_svec = step("vector", f_svec, deps=(t_sxbc, tuple(g_row_toks), tuple(u_row_toks)))

    stk_c1 = ExitStack()

    def sb_c1(name, shape, dtype):
        return stk_c1.enter_context(nc.sbuf_tensor(name, shape, dtype))

    xqi8 = sb_c1("xqi8", [128, 512], I8)
    wfm_ = sb_c1("wfm", [128, 2 * 2 * D], BF16)
    wfm = lambda s, gu: wfm_[:, (s * 2 + gu) * D:(s * 2 + gu + 1) * D]
    g_act = sb_c1("g_act", [128, 512], F32)
    h_bf_ = sb_c1("h_bf", [128, 2 * 512], BF16)
    h_bf = lambda s: h_bf_[:, s * 512:(s + 1) * 512]

    g_banks = [psA[0][:, :], psA[1][:, :], psB[0][:, 0:512], psB[0][:, 512:1024]]
    u_banks = [psC[0][:, :], psC[1][:, :], psB[1][:, 0:512], psB[1][:, 512:1024]]
    quant_toks = []
    for ch in range(NCH):
        ch_tr4 = tuple(x1tr_toks[ch * 4 + k] for k in range(4))

        def f_xq(eng, ch=ch):
            for hc in range(DC):
                strip = xq_ap(hc)[:, ch * 512:(ch + 1) * 512]
                vchain(eng, eng.tensor_scalar(out=xqi8[:, :], in0=strip,
                                              scalar1=rsx_bc[:, :], scalar2=None,
                                              op0=ALU.mult))
                ins = eng.tensor_copy(out=strip, in_=xqi8[:, :])
                if hc < DC - 1:
                    vchain(eng, ins)
            ins.then_inc(sems["vector"], 1)
        prev_q = quant_toks[-1] if quant_toks else ()
        t_xq = step("vector", f_xq, deps=(t_rsxbc, ch_tr4, bar3, prev_q))
        quant_toks.append(t_xq)

    def wload(i, wsl, deps):
        src_g = wgq_d[:, :, i * 128:(i + 1) * 128].transpose([1, 0, 2])
        src_u = wuq_d[:, :, i * 128:(i + 1) * 128].transpose([1, 0, 2])
        return dmas("sync", f"wl{wsl}",
                    [(wfm(wsl, 0).rearrange("p (c j) -> p c j", j=128), src_g),
                     (wfm(wsl, 1).rearrange("p (c j) -> p c j", j=128), src_u)],
                    deps=deps)

    wfm_free = [(), ()]
    ps_g_free = [(), (), (), ()]
    ps_u_free = [(), (), (), ()]
    g_act_free = ()
    h_slot_free = [(), ()]
    wl_toks = [None] * IC
    h_store_toks = []
    hmax_toks = []
    last_gumm = ()

    wl_toks[0] = wload(0, 0, deps=(bar3,))
    if IC > 1:
        wl_toks[1] = wload(1, 1, deps=(bar3,))

    for i in range(IC):
        wsl = i % 2
        t_wl = wl_toks[i]

        gumm_toks = []
        for ch in range(NCH):
            bank = ch % 4

            def f_gumm(eng, wsl=wsl, ch=ch, bank=bank):
                for hc in range(DC):
                    eng.matmul(g_banks[bank],
                               wfm(wsl, 0)[:, hc * 128:(hc + 1) * 128],
                               xq_ap(hc)[:, ch * 512:(ch + 1) * 512],
                               start=(hc == 0), stop=(hc == DC - 1))
                for hc in range(DC):
                    ins = eng.matmul(u_banks[bank],
                                     wfm(wsl, 1)[:, hc * 128:(hc + 1) * 128],
                                     xq_ap(hc)[:, ch * 512:(ch + 1) * 512],
                                     start=(hc == 0), stop=(hc == DC - 1))
                ins.then_inc(sems["tensor"], 1)
            t_gumm = step("tensor", f_gumm,
                          deps=(t_wl, quant_toks[ch], ps_g_free[bank], ps_u_free[bank]))
            gumm_toks.append(t_gumm)

            def f_gact(eng, i=i, bank=bank):
                eng.activation(out=g_act[:, :], in_=g_banks[bank], func=AF.Silu,
                               scale=svec_g[:, i:i + 1]).then_inc(sems["scalar"], 1)
            t_gact = step("scalar", f_gact, deps=(t_gumm, t_svec, g_act_free))

            hsl = ch % 2

            def f_h(eng, i=i, ch=ch, hsl=hsl, bank=bank):
                vchain(eng, eng.scalar_tensor_tensor(
                    out=h_bf(hsl), in0=u_banks[bank], scalar=svec_u[:, i:i + 1],
                    in1=g_act[:, :], op0=ALU.mult, op1=ALU.mult))
                eng.tensor_reduce(out=hmax_cols[:, i * NCH + ch:i * NCH + ch + 1],
                                  in_=h_bf(hsl), axis=AX, op=ALU.max,
                                  apply_absolute_value=True).then_inc(sems["vector"], 1)
            t_h = step("vector", f_h, deps=(t_gact, h_slot_free[hsl], t_svec))
            hmax_toks.append(t_h)
            ps_g_free[bank] = (t_h,)
            ps_u_free[bank] = (t_h,)
            g_act_free = (t_h,)
            t_hst = dma("gpsimd", f"hst{hsl}", h_d[i, :, ch * 512:(ch + 1) * 512],
                        h_bf(hsl), deps=(t_h,))
            h_slot_free[hsl] = (t_hst,)
            h_store_toks.append(t_hst)

        last_gumm = (gumm_toks[-1],)
        if DEBUG_DUMPS and i == 20:
            dmas("gpsimd", "dbgw", [(dbg_wfm[:, 0:D], wfm(wsl, 0)),
                                    (dbg_wfm[:, D:2 * D], wfm(wsl, 1))],
                 deps=(gumm_toks[-1],))
        if i + 2 < IC:
            wfm_free[wsl] = (gumm_toks[-1],)
            wl_toks[i + 2] = wload(i + 2, wsl, deps=(gumm_toks[-1],))

    if DEBUG_DUMPS:
        dmas("gpsimd", "dbgd",
             [(dbg_xq[c, :, :], xq_ap(c)) for c in range(DC)],
             deps=(last_gumm, tuple(quant_toks)))

    t_swdr = dmas("gpsimd", "swd", [(swd_dram[:, ht * 128:(ht + 1) * 128],
                                     swd_cols[:, ht:ht + 1]) for ht in range(HT)],
                  deps=(tuple(d_row_toks),))
    t_swdl = dma("gpsimd", "swd", swd_row[:, :], swd_dram, deps=(t_swdr,))

    d_free = (last_gumm, g_act_free, h_slot_free[0], h_slot_free[1],
              tuple(h_store_toks))
    bar4 = plan.barrier()
    stk_c1.close()
    stk_c.close()

    stk_d = ExitStack()

    def sb_d(name, shape, dtype):
        return stk_d.enter_context(nc.sbuf_tensor(name, shape, dtype))

    h_ld_ = sb_d("h_ld", [128, 2 * IC * 512], BF16)
    h_ld = lambda s, ic: h_ld_[:, (s * IC + ic) * 512:(s * IC + ic + 1) * 512]
    hq_i8_ = sb_d("hq_i8", [128, 2 * 1024], I8)
    hq_i8 = lambda s: hq_i8_[:, s * 1024:(s + 1) * 1024]
    x1f_ = sb_d("x1f", [128, 2 * D], BF16)
    x1f = lambda s: x1f_[:, s * D:(s + 1) * D]
    r2_st_ = sb_d("r2_st", [128, 4 * D], F32)
    r2_st = lambda s: r2_st_[:, s * D:(s + 1) * D]
    t2_st = sb_d("t2_st", [128, D], F32)
    out_t_ = sb_d("out_t", [128, 4 * D], F32)
    out_t = lambda s: out_t_[:, s * D:(s + 1) * D]
    sq_junk2 = sb_d("sq_junk2", [128, D], F32)
    swd_bc = sb_d("swd_bc", [128, D], F32)
    wd_fm = sb_d("wd_fm", [128, IC * D], BF16)

    def wdfm_ap(ic):
        return wd_fm[:, ic * D:(ic + 1) * D]

    t_wdfm = dmas("sync", "wdfm", [(wdfm_ap(ic), wdq_d[ic, :, :]) for ic in range(IC)],
                  deps=(d_free,))

    def f_hred(eng):
        eng.tensor_reduce(out=red_sc[:, :], in_=hmax_cols[:, :], axis=AX,
                          op=ALU.max).then_inc(sems["vector"], 1)
    t_hred = step("vector", f_hred, deps=(tuple(hmax_toks),))
    t_pm2 = dma("gpsimd", "pm", pmax_d, red_sc[:, :], deps=(t_hred, t_pm1b))
    t_pm2b = dma("gpsimd", "pm", smax1[:, :], pmax_d, deps=(t_pm2, t_xred2))

    def f_hred2(eng):
        eng.tensor_reduce(out=gmax2[:, :], in_=smax1[:, :], axis=AX,
                          op=ALU.max).then_inc(sems["vector"], 1)
    t_hred2 = step("vector", f_hred2, deps=(t_pm2b,))
    t_m2o = dma("gpsimd", "cc", cc_m2_i, gmax2[:, :], deps=(t_hred2,))

    def f_cc_m2(eng):
        eng.collective_compute(
            "AllReduce", ALU.max, replica_groups=[list(range(NCORES))],
            ins=[cc_m2_i.opt()], outs=[cc_m2_o.opt()]).then_inc(sems["gpsimd"], 1)
    t_cc_m2 = step("gpsimd", f_cc_m2, deps=(t_m2o,), inc=1)
    t_m2i = dma("gpsimd", "cc", gmax2[:, :], cc_m2_o, deps=(t_cc_m2,))

    def f_sigd(eng):
        eng.activation(out=sc_small[:, 3:4], in_=clipd_s[:, :],
                       func=AF.Sigmoid).then_inc(sems["scalar"], 1)
    t_sigd = step("scalar", f_sigd, deps=(t_clip, bar4))

    def f_sh(eng):
        vchain(eng, eng.tensor_tensor(out=sc_small[:, 4:5], in0=sc_small[:, 3:4],
                                      in1=gmax2[:, :], op=ALU.mult))
        vchain(eng, eng.tensor_scalar(out=sc_small[:, 4:5], in0=sc_small[:, 4:5],
                                      scalar1=1.0 / QMAX, scalar2=None, op0=ALU.mult))
        vchain(eng, eng.reciprocal(out=sc_small[:, 5:6], in_=sc_small[:, 4:5]))
        eng.tensor_scalar(out=sc_small[:, 7:8], in0=sc_small[:, 4:5],
                          scalar1=1.0 / QMAX, scalar2=None,
                          op0=ALU.mult).then_inc(sems["vector"], 1)
    t_sh = step("vector", f_sh, deps=(t_sigd, t_m2i))
    t_b3 = dma("gpsimd", "bc3", bc3_d, sc_small[:, 5:6], deps=(t_sh,))
    t_rshbc = dma("gpsimd", "bc3", rsh_bc[:, :], bc3_d.partition_broadcast(128),
                  deps=(t_b3,))

    def f_swdsc(eng):
        eng.activation(out=swd_sc[:, :], in_=swd_row[:, :], func=AF.Copy,
                       scale=sc_small[:, 7:8]).then_inc(sems["scalar"], 1)
    t_swdsc = step("scalar", f_swdsc, deps=(t_sh, t_swdl))
    t_swdd = dma("gpsimd", "swd", swd_dram, swd_sc[:, :], deps=(t_swdsc, t_swdl))
    t_swdbc = dma("gpsimd", "swd", swd_bc[:, :], swd_dram.partition_broadcast(128),
                  deps=(t_swdd,))

    h_ld_free = [(), ()]
    ps_dn_free = [(), ()]
    x1f_free = [(), ()]
    out_free = [(), (), (), ()]
    stats2_tok = ()
    r2_free = [()] * 4
    i8_free = [(), ()]
    out_toks = []
    pend_tail_f = None

    def emit_tail_f(dn_toks, sq2_toks, ch):
        nonlocal stats2_tok

        def f_stats2(eng):
            negm = ln2_sc[:, 12:16]
            v = ln2_sc[:, 4:8]
            vchain(eng, eng.tensor_scalar(out=negm, in0=rsums2[:, 0:4],
                                          scalar1=-1.0 / D, scalar2=None, op0=ALU.mult))
            vchain(eng, eng.tensor_tensor(out=v, in0=negm, in1=negm, op=ALU.mult))
            vchain(eng, eng.scalar_tensor_tensor(out=v, in0=ssqs2[:, 0:4],
                                                 scalar=1.0 / D, in1=v,
                                                 op0=ALU.mult, op1=ALU.subtract))
            vchain(eng, eng.tensor_scalar(out=v, in0=v, scalar1=EPS_LN, scalar2=None,
                                          op0=ALU.add))
            eng.reciprocal(out=ln2_sc[:, 8:12], in_=v).then_inc(sems["vector"], 1)
        t_stats2 = step("vector", f_stats2, deps=(tuple(dn_toks), tuple(sq2_toks)))

        def f_rstd2(eng):
            eng.activation(out=ln2_sc[:, 8:12], in_=ln2_sc[:, 8:12],
                           func=AF.Sqrt).then_inc(sems["scalar"], 1)
        t_rstd2 = step("scalar", f_rstd2, deps=(t_stats2,))
        stats2_tok = (t_rstd2,)

        for tt in range(4):
            t_idx = ch * 4 + tt
            osl = tt

            def f_aff2(eng, tt=tt, osl=osl):
                eng.tensor_scalar(out=out_t(osl), in0=r2_st(tt),
                                  scalar1=ln2_sc[:, 12 + tt:13 + tt],
                                  scalar2=ln2_sc[:, 8 + tt:9 + tt],
                                  op0=ALU.add, op1=ALU.mult).then_inc(sems["vector"], 1)
            t_aff2 = step("vector", f_aff2, deps=((t_rstd2,), out_free[osl]))
            r2_free[tt] = (t_aff2,)
            t_ost = dma("gpsimd", f"ost{osl}", out_d[t_idx * 128:(t_idx + 1) * 128, :],
                        out_t(osl), deps=(t_aff2,))
            out_free[osl] = (t_ost,)
            out_toks.append(t_ost)

    for ch in range(NCH):
        sl = ch % 2
        t_hld = dmas("sync", f"hld{sl}",
                     [(h_ld(sl, ic), h_d[ic, :, ch * 512:(ch + 1) * 512])
                      for ic in range(IC)],
                     deps=(d_free, h_ld_free[sl]))

        hq_toks = []
        for j, ic in enumerate(range(0, IC, 2)):
            n2 = min(2, IC - ic)
            isl = j % 2

            def f_hqs(eng, sl=sl, ic=ic, isl=isl, n2=n2):
                eng.activation(out=hq_i8(isl)[:, 0:n2 * 512],
                               in_=h_ld_[:, (sl * IC + ic) * 512:
                                         (sl * IC + ic + n2) * 512],
                               func=AF.Copy,
                               scale=rsh_bc[:, :]).then_inc(sems["scalar"], 1)
            t_hqs = step("scalar", f_hqs, deps=(t_hld, t_rshbc, i8_free[isl]))

            def f_hqv(eng, sl=sl, ic=ic, isl=isl, n2=n2):
                eng.tensor_copy(out=h_ld_[:, (sl * IC + ic) * 512:
                                          (sl * IC + ic + n2) * 512],
                                in_=hq_i8(isl)[:, 0:n2 * 512]).then_inc(
                    sems["vector"], 1)
            t_hqv = step("vector", f_hqv, deps=(t_hqs,))
            i8_free[isl] = (t_hqv,)
            hq_toks.append(t_hqv)
        all_hq = tuple(hq_toks)

        if pend_tail_f is not None:
            emit_tail_f(*pend_tail_f)
            pend_tail_f = None

        dn_toks = []
        sq2_toks = []
        for tt in range(4):
            t_idx = ch * 4 + tt
            xsl = tt % 2
            dbank = tt % 2
            t_x1f = dma("gpsimd", f"x1f{xsl}", x1f(xsl), x1_d[t_idx, :, :],
                        deps=(d_free, bar3, x1f_free[xsl]))

            def f_dnmm(eng, sl=sl, tt=tt, dbank=dbank):
                for ic in range(IC):
                    for half in range(2):
                        ins = eng.matmul(
                            psB[dbank][:, half * 512:(half + 1) * 512],
                            h_ld(sl, ic)[:, tt * 128:(tt + 1) * 128],
                            wdfm_ap(ic)[:, half * 512:(half + 1) * 512],
                            start=(ic == 0), stop=(ic == IC - 1))
                ins.then_inc(sems["tensor"], 1)
            t_dnmm = step("tensor", f_dnmm,
                          deps=(all_hq, ps_dn_free[dbank], t_wdfm, bar4))

            def f_r2(eng, tt=tt, xsl=xsl, dbank=dbank):
                vchain(eng, eng.tensor_tensor(out=t2_st[:, :], in0=psB[dbank][:, :],
                                              in1=swd_bc[:, :], op=ALU.mult))
                eng.scalar_tensor_tensor(
                    out=r2_st(tt), in0=t2_st[:, :], scalar=1.0,
                    in1=x1f(xsl), op0=ALU.mult, op1=ALU.add,
                    accum_out=rsums2[:, tt:tt + 1]).then_inc(sems["vector"], 1)
            prev_r2 = dn_toks[-1] if dn_toks else ()
            t_r2 = step("vector", f_r2,
                        deps=(t_dnmm, t_x1f, t_swdbc, stats2_tok, r2_free[tt], prev_r2))
            ps_dn_free[dbank] = (t_r2,)
            x1f_free[xsl] = (t_r2,)
            dn_toks.append(t_r2)

            def f_sq2(eng, tt=tt):
                eng.scalar_tensor_tensor(
                    out=sq_junk2[:, :], in0=r2_st(tt), scalar=1.0, in1=r2_st(tt),
                    op0=ALU.mult, op1=ALU.mult,
                    accum_out=ssqs2[:, tt:tt + 1]).then_inc(sems["vector"], 1)
            t_sq2 = step("vector", f_sq2, deps=(t_r2, sqj_tok))
            sqj_tok = (t_sq2,)
            sq2_toks.append(t_sq2)

        h_ld_free[sl] = (dn_toks[-1],)
        pend_tail_f = (dn_toks, sq2_toks, ch)

    emit_tail_f(*pend_tail_f)
    pend_tail_f = None

    stk_d.close()

    if DEBUG_DUMPS:
        t_dbg = dmas("gpsimd", "dbgd",
                     [(dbg_x1, x1_d), (dbg_h, h_d), (dbg_q, q_d), (dbg_wg, wgq_d)],
                     deps=(tuple(out_toks),))
        t_dbg2 = dma("gpsimd", "dbgd", dbg_kv, cc_kv_o, deps=(t_dbg,))
        out_toks.append(t_dbg2)

    def f_final(eng):
        eng.memset(sc_small[:, 6:7], 0.0).then_inc(sems["vector"], 1)
    step("vector", f_final, deps=(tuple(out_toks),))

    for s in plan.count:
        sems[s] = nc.alloc_semaphore(name=f"sem_{s}")
    sems["vchain"] = nc.alloc_semaphore(name="sem_vchain")
    sems["schain"] = nc.alloc_semaphore(name="sem_schain")

    def emit(eng_name, eng):
        watermark = {s: 0 for s in plan.count}
        for (e, fn, deps) in plan.steps:
            if e != eng_name:
                continue
            for (de, dv) in deps:
                if watermark[de] >= dv:
                    continue
                eng.wait_ge(sems[de], dv)
                watermark[de] = dv
            fn(eng)

    with nc.Block() as block:
        @block.tensor
        def _(tensor):
            emit("tensor", tensor)

        @block.vector
        def _(vector):
            emit("vector", vector)

        @block.scalar
        def _(scalar):
            emit("scalar", scalar)

        @block.sync
        def _(sync):
            emit("sync", sync)

        @block.gpsimd
        def _(gpsimd):
            emit("gpsimd", gpsimd)

    return nc


_NC_CACHE = []


def kernel(L0_attn_inputs_q, L0_attn_Wq, L0_attn_bq, L0_attn_Wk, L0_attn_bk,
           L0_attn_Wv, L0_attn_bv, L0_attn_Wo, L0_attn_bo,
           L0_ffn_clip_x, L0_ffn_clip_down, L0_ffn_w_gate, L0_ffn_w_up,
           L0_ffn_w_down):
    x = np.ascontiguousarray(np.asarray(L0_attn_inputs_q, dtype=np.float32))
    onehot = np.zeros((NH, NP * 128), dtype=ml_dtypes.bfloat16)
    for p in range(NP):
        onehot[2 * p, p * 128:p * 128 + 64] = 1.0
        onehot[2 * p + 1, p * 128 + 64:p * 128 + 128] = 1.0

    wg = np.ascontiguousarray(L0_ffn_w_gate, dtype=np.float32)
    wu = np.ascontiguousarray(L0_ffn_w_up, dtype=np.float32)
    wd = np.ascontiguousarray(L0_ffn_w_down, dtype=np.float32)
    common = {
        "wq": np.ascontiguousarray(L0_attn_Wq, dtype=np.float32),
        "wk": np.ascontiguousarray(L0_attn_Wk, dtype=np.float32),
        "wv": np.ascontiguousarray(L0_attn_Wv, dtype=np.float32),
        "wo": np.ascontiguousarray(L0_attn_Wo, dtype=np.float32),
        "bq": np.ascontiguousarray(L0_attn_bq, dtype=np.float32).reshape(1, D),
        "bk": np.ascontiguousarray(L0_attn_bk, dtype=np.float32).reshape(1, D),
        "bv": np.ascontiguousarray(L0_attn_bv, dtype=np.float32).reshape(1, D),
        "bo": np.ascontiguousarray(L0_attn_bo, dtype=np.float32).reshape(1, D),
        "clipx": np.ascontiguousarray(L0_ffn_clip_x, dtype=np.float32).reshape(1, 1),
        "clipd": np.ascontiguousarray(L0_ffn_clip_down, dtype=np.float32).reshape(1, 1),
        "wg": wg, "wu": wu, "wd": wd,
        "wgt": np.ascontiguousarray(wg.T),
        "wut": np.ascontiguousarray(wu.T),
        "wdt": np.ascontiguousarray(wd.T),
        "onehot": onehot,
    }
    in_maps = []
    for c in range(NCORES):
        b, half = c // 2, c % 2
        xc = np.ascontiguousarray(x[b, half * T:(half + 1) * T, :])
        in_maps.append({"x": xc, **common})

    if not _NC_CACHE:
        _NC_CACHE.append(build_program())
    nc = _NC_CACHE[0]
    res = run_bass_kernel_spmd(nc, in_maps, core_ids=list(range(NCORES)))
    out = np.empty((B, S, D), dtype=np.float32)
    for c in range(NCORES):
        b, half = c // 2, c % 2
        out[b, half * T:(half + 1) * T, :] = res.results[c]["out"]
    return out


# version 16
# speedup vs baseline: 1.1008x; 1.0353x over previous
from contextlib import ExitStack

import numpy as np
import ml_dtypes
import concourse.bass as bass
import concourse.mybir as mybir
from concourse.bass_utils import run_bass_kernel_spmd

F32 = mybir.dt.float32
BF16 = mybir.dt.bfloat16
I8 = mybir.dt.int8
AF = mybir.ActivationFunctionType
ALU = mybir.AluOpType
AX = mybir.AxisListType.X

DEBUG_DUMPS = False
NCORES = 8
B, S, D = 4, 8192, 1024
T = 4096
NT = T // 128
NCH = T // 512
DC = D // 128
NH, DH = 16, 64
NP = NH // 2
I = 2816
IC = I // 128
HT = D // 128
EPS_ATTN = 1e-6
EPS_LN = 1e-5
QMAX = 127.0

ENGS = ("tensor", "vector", "scalar", "sync", "gpsimd")
COMPUTE_ENGS = ("tensor", "vector", "scalar")


class Plan:

    def __init__(self):
        self.steps = []
        self.count = {}
        self.eng_of = {}

    @staticmethod
    def _flatten(deps, out):
        for d in deps:
            if d is None or d == () or d == []:
                continue
            if isinstance(d, tuple) and len(d) == 2 and isinstance(d[0], str):
                out.append(d)
            else:
                Plan._flatten(d, out)

    def add(self, eng, fn, deps=(), inc=1, stream=None):
        stream = stream or eng
        self.eng_of.setdefault(stream, eng)
        flat = []
        self._flatten(deps, flat)
        self.count[stream] = self.count.get(stream, 0) + inc
        tok = (stream, self.count[stream])
        self.steps.append((eng, fn, tuple(flat)))
        return tok

    def barrier(self):
        return tuple((s, v) for s, v in self.count.items() if v > 0)


def build_program():
    nc = bass.Bass("TRN2", target_bir_lowering=False)
    plan = Plan()

    x_in = nc.dram_tensor("x", [T, D], F32, kind="ExternalInput").ap()
    wq_in = nc.dram_tensor("wq", [D, D], F32, kind="ExternalInput").ap()
    wk_in = nc.dram_tensor("wk", [D, D], F32, kind="ExternalInput").ap()
    wv_in = nc.dram_tensor("wv", [D, D], F32, kind="ExternalInput").ap()
    wo_in = nc.dram_tensor("wo", [D, D], F32, kind="ExternalInput").ap()
    bq_in = nc.dram_tensor("bq", [1, D], F32, kind="ExternalInput").ap()
    bk_in = nc.dram_tensor("bk", [1, D], F32, kind="ExternalInput").ap()
    bv_in = nc.dram_tensor("bv", [1, D], F32, kind="ExternalInput").ap()
    bo_in = nc.dram_tensor("bo", [1, D], F32, kind="ExternalInput").ap()
    clipx_in = nc.dram_tensor("clipx", [1, 1], F32, kind="ExternalInput").ap()
    clipd_in = nc.dram_tensor("clipd", [1, 1], F32, kind="ExternalInput").ap()
    wg_in = nc.dram_tensor("wg", [I, D], F32, kind="ExternalInput").ap()
    wu_in = nc.dram_tensor("wu", [I, D], F32, kind="ExternalInput").ap()
    wd_in = nc.dram_tensor("wd", [D, I], F32, kind="ExternalInput").ap()
    wgt_in = nc.dram_tensor("wgt", [D, I], F32, kind="ExternalInput").ap()
    wut_in = nc.dram_tensor("wut", [D, I], F32, kind="ExternalInput").ap()
    wdt_in = nc.dram_tensor("wdt", [I, D], F32, kind="ExternalInput").ap()
    oh_in = nc.dram_tensor("onehot", [NH, NP * 128], BF16, kind="ExternalInput").ap()
    out_d = nc.dram_tensor("out", [T, D], F32, kind="ExternalOutput").ap()
    if DEBUG_DUMPS:
        dbg_x1 = nc.dram_tensor("dbg_x1", [NT, 128, D], BF16, kind="ExternalOutput").ap()
        dbg_h = nc.dram_tensor("dbg_h", [IC, 128, T], BF16, kind="ExternalOutput").ap()
        dbg_q = nc.dram_tensor("dbg_q", [DC, 128, T], BF16, kind="ExternalOutput").ap()
        dbg_kv = nc.dram_tensor("dbg_kv", [128, NP * 65], F32, kind="ExternalOutput").ap()
        dbg_wg = nc.dram_tensor("dbg_wg", [DC, 128, I], BF16, kind="ExternalOutput").ap()
        dbg_xq = nc.dram_tensor("dbg_xq", [DC, 128, T], BF16, kind="ExternalOutput").ap()
        dbg_wfm = nc.dram_tensor("dbg_wfm", [128, 2 * D], BF16, kind="ExternalOutput").ap()

    q_d = nc.dram_tensor("q_d", [DC, 128, T], BF16).ap()
    x1_d = nc.dram_tensor("x1_d", [NT, 128, D], BF16).ap()
    h_d = nc.dram_tensor("h_d", [IC, 128, T], BF16).ap()
    wgq_d = nc.dram_tensor("wgq_d", [DC, 128, I], BF16).ap()
    wuq_d = nc.dram_tensor("wuq_d", [DC, 128, I], BF16).ap()
    wdq_d = nc.dram_tensor("wdq_d", [IC, 128, D], BF16).ap()
    rsc_d = nc.dram_tensor("rsc_d", [1, I], F32).ap()
    cc_kv_i = nc.dram_tensor("cc_kv_i", [128, NP * 65], F32).ap()
    cc_kv_o = nc.dram_tensor("cc_kv_o", [128, NP * 65], F32).ap()
    cc_m1_i = nc.dram_tensor("cc_m1_i", [1, 1], F32).ap()
    cc_m1_o = nc.dram_tensor("cc_m1_o", [1, 1], F32).ap()
    cc_m2_i = nc.dram_tensor("cc_m2_i", [1, 1], F32).ap()
    cc_m2_o = nc.dram_tensor("cc_m2_o", [1, 1], F32).ap()
    pmax_d = nc.dram_tensor("pmax_d", [1, 128], F32).ap()
    bc1_d = nc.dram_tensor("bc1_d", [1, 1], F32).ap()
    bc2_d = nc.dram_tensor("bc2_d", [1, 1], F32).ap()
    bc3_d = nc.dram_tensor("bc3_d", [1, 1], F32).ap()
    swd_dram = nc.dram_tensor("swd_dram", [1, D], F32).ap()

    sems = {}
    chain_cnt = {"vector": 0, "scalar": 0}

    def vchain(eng, ins):
        chain_cnt["vector"] += 1
        ins.then_inc(sems["vchain"], 1)
        eng.wait_ge(sems["vchain"], chain_cnt["vector"])

    def schain(eng, ins):
        chain_cnt["scalar"] += 1
        ins.then_inc(sems["schain"], 1)
        eng.wait_ge(sems["schain"], chain_cnt["scalar"])

    def dma(queue, stream, out, in_, deps=(), slow=False):
        def fn(eng):
            eng.dma_start(out=out, in_=in_,
                          allow_slow_non_contiguous=slow).then_inc(sems[stream], 16)
        return plan.add(queue, fn, deps, inc=16, stream=stream)

    def dmas(queue, stream, pairs, deps=()):
        def fn(eng):
            for o, i_ in pairs:
                eng.dma_start(out=o, in_=i_).then_inc(sems[stream], 16)
        return plan.add(queue, fn, deps, inc=16 * len(pairs), stream=stream)

    def step(eng, fn, deps=(), inc=1):
        return plan.add(eng, fn, deps, inc)

    psA = [nc.alloc_psum_tensor(f"psA{i}", [128, 512], F32) for i in range(2)]
    psB = [nc.alloc_psum_tensor(f"psB{i}", [128, 1024], F32) for i in range(2)]
    psC = [nc.alloc_psum_tensor(f"psC{i}", [128, 512], F32) for i in range(2)]

    ones_row = nc.alloc_sbuf_tensor("ones_row", [1, 128], BF16)
    ones512 = nc.alloc_sbuf_tensor("ones512", [1, 512], BF16)
    bq_bf = nc.alloc_sbuf_tensor("bq_bf", [1, D], BF16)
    bk_bf = nc.alloc_sbuf_tensor("bk_bf", [1, D], BF16)
    bv_bf = nc.alloc_sbuf_tensor("bv_bf", [1, D], BF16)
    bo_bf = nc.alloc_sbuf_tensor("bo_bf", [1, D], BF16)
    clipx_s = nc.alloc_sbuf_tensor("clipx_s", [1, 1], F32)
    clipd_s = nc.alloc_sbuf_tensor("clipd_s", [1, 1], F32)
    oh_sb = nc.alloc_sbuf_tensor("oh_sb", [NH, NP * 128], BF16)
    kv_sb = nc.alloc_sbuf_tensor("kv_sb", [128, NP * 65], F32)
    kvt_sb = nc.alloc_sbuf_tensor("kvt_sb", [128, NP * 65], F32)
    xmax_cols = nc.alloc_sbuf_tensor("xmax_cols", [128, NT], F32)
    hmax_cols = nc.alloc_sbuf_tensor("hmax_cols", [128, IC * NCH], F32)
    smax1 = nc.alloc_sbuf_tensor("smax1", [1, 128], F32)
    gmax1 = nc.alloc_sbuf_tensor("gmax1", [1, 1], F32)
    gmax2 = nc.alloc_sbuf_tensor("gmax2", [1, 1], F32)
    sc_small = nc.alloc_sbuf_tensor("sc_small", [1, 8], F32)
    rsx_bc = nc.alloc_sbuf_tensor("rsx_bc", [128, 1], F32)
    sx_bc = nc.alloc_sbuf_tensor("sx_bc", [128, 1], F32)
    rsh_bc = nc.alloc_sbuf_tensor("rsh_bc", [128, 1], F32)
    swg_cols = nc.alloc_sbuf_tensor("swg_cols", [128, IC], F32)
    swu_cols = nc.alloc_sbuf_tensor("swu_cols", [128, IC], F32)
    swd_cols = nc.alloc_sbuf_tensor("swd_cols", [128, HT], F32)
    svec_g = nc.alloc_sbuf_tensor("svec_g", [128, IC], F32)
    svec_u = nc.alloc_sbuf_tensor("svec_u", [128, IC], F32)
    swd_row = nc.alloc_sbuf_tensor("swd_row", [1, D], F32)
    swd_sc = nc.alloc_sbuf_tensor("swd_sc", [1, D], F32)
    ln_sc = nc.alloc_sbuf_tensor("ln_sc", [128, 16], F32)
    red_sc = nc.alloc_sbuf_tensor("red_sc", [128, 1], F32)
    ln2_sc = nc.alloc_sbuf_tensor("ln2_sc", [128, 16], F32)
    rsums = nc.alloc_sbuf_tensor("rsums", [128, 4], F32)
    ssqs = nc.alloc_sbuf_tensor("ssqs", [128, 4], F32)
    rsums2 = nc.alloc_sbuf_tensor("rsums2", [128, 4], F32)
    ssqs2 = nc.alloc_sbuf_tensor("ssqs2", [128, 4], F32)
    rcols = nc.alloc_sbuf_tensor("rcols", [128, IC], F32)

    t_bias = dmas("gpsimd", "bias", [(bq_bf[:, :], bq_in), (bk_bf[:, :], bk_in),
                                     (bv_bf[:, :], bv_in), (bo_bf[:, :], bo_in)])
    t_clip = dmas("sync", "clip", [(clipx_s[:, :], clipx_in), (clipd_s[:, :], clipd_in),
                                   (oh_sb[:, :], oh_in)])

    def f_consts(eng):
        eng.memset(ones_row[:, :], 1.0)
        eng.memset(ones512[:, :], 1.0).then_inc(sems["vector"], 1)
    t_ones = step("vector", f_consts)

    stk_a = ExitStack()

    def sb_a(name, shape, dtype):
        return stk_a.enter_context(nc.sbuf_tensor(name, shape, dtype))

    wq_bf = sb_a("wq_bf", [128, DC * D], BF16)
    wk_bf = sb_a("wk_bf", [128, DC * D], BF16)
    wv_bf = sb_a("wv_bf", [128, DC * D], BF16)
    t_wq = dmas("gpsimd", "wq", [(wq_bf[:, k * D:(k + 1) * D], wq_in[k * 128:(k + 1) * 128, :])
                                 for k in range(DC)])
    t_wk = dmas("gpsimd", "wk", [(wk_bf[:, k * D:(k + 1) * D], wk_in[k * 128:(k + 1) * 128, :])
                                 for k in range(DC)])
    t_wv = dmas("gpsimd", "wv", [(wv_bf[:, k * D:(k + 1) * D], wv_in[k * 128:(k + 1) * 128, :])
                                 for k in range(DC)])

    wrow_ = sb_a("wrow", [128, 2 * I], F32)
    wrow_sl = lambda s: wrow_[:, s * I:(s + 1) * I]
    wtc_ = sb_a("wtc", [128, 3 * I], F32)
    wtc = lambda s: wtc_[:, s * I:(s + 1) * I]
    wti8_ = sb_a("wti8", [128, 2 * I], I8)
    wti8 = lambda s: wti8_[:, s * I:(s + 1) * I]
    rs_bc = sb_a("rs_bc", [128, I], F32)
    rs2_bc = sb_a("rs2_bc", [128, 2 * D], F32)

    prep_q = []
    ps = {"rowfree0": (), "rowfree1": (), "tcfree0": (), "tcfree1": (),
          "tcfree2": (), "i8free0": (), "i8free1": (), "rcols": (),
          "rsd": (), "rsbc": (), "rsbc_readers": []}

    def mk_rows(w_in, cols, n_rows, width, toks):
        nb_max = 2 if width <= 1024 else 1
        units = []
        bidx = [0]

        def mk(rt0, nb):
            def u():
                sl = bidx[0] % 2
                bidx[0] += 1
                buf = wrow_sl(sl)
                src_ap = w_in[rt0 * 128:(rt0 + nb) * 128, :].rearrange(
                    "(a p) d -> p a d", p=128)
                t_l = dma("gpsimd", f"pwr{sl}",
                          buf[:, 0:nb * width].rearrange("p (a d) -> p a d", d=width),
                          src_ap, deps=(ps[f"rowfree{sl}"],))

                def f_red(eng, rt0=rt0, nb=nb, cols=cols, width=width, buf=buf):
                    for j in range(nb):
                        ins = eng.tensor_reduce(
                            out=cols[:, rt0 + j:rt0 + j + 1],
                            in_=buf[:, j * width:(j + 1) * width],
                            axis=AX, op=ALU.max, apply_absolute_value=True)
                        if j < nb - 1:
                            vchain(eng, ins)
                    ins.then_inc(sems["vector"], 1)
                t_r = step("vector", f_red, deps=(t_l,))
                ps[f"rowfree{sl}"] = (t_r,)
                toks.append(t_r)
            return u
        rt0 = 0
        while rt0 < n_rows:
            nb = min(nb_max, n_rows - rt0)
            units.append(mk(rt0, nb))
            rt0 += nb
        return units

    def mk_rsc(cols, n, rowtoks):
        def u():
            def f_rc(eng, cols=cols, n=n):
                eng.reciprocal(out=rcols[:, 0:n], in_=cols[:, 0:n]).then_inc(
                    sems["vector"], 1)
            t_rc = step("vector", f_rc, deps=(tuple(rowtoks), ps["rcols"]))
            ps["rcols"] = (t_rc,)
            dest = rsc_d.rearrange("a (i p) -> (a p) i", p=128)[:, 0:n]
            t_w = dma("scalar", "prsd", dest, rcols[:, 0:n],
                      deps=(t_rc, ps["rsd"]), slow=True)
            t_bc = dma("scalar", "prsb", rs_bc[:, 0:n * 128],
                       rsc_d[:, 0:n * 128].partition_broadcast(128),
                       deps=(t_w, tuple(ps["rsbc_readers"])))
            ps["rsd"] = (t_bc,)
            ps["rsbc"] = (t_bc,)
            ps["rsbc_readers"] = []
        return [u]

    def mk_fm(wt_in, width, n_chunks, dest_dram):
        units = []
        cnt = [0]
        nc2 = max(1, I // width // 2 + (0 if width == I else 1))
        nc2 = 1 if width == I else 2

        def flush(eng_unused=None):
            st = ps.pop("pend_st", None)
            if st:
                st()

        def mk(hc, nb):
            def u():
                flush()
                sl = cnt[0] % 3
                j = cnt[0] % 2
                cnt[0] += 1
                t_l = dma("gpsimd", f"ptc{sl}",
                          wtc(sl)[:, 0:nb * width].rearrange(
                              "p (a d) -> p a d", d=width),
                          wt_in[hc * 128:(hc + nb) * 128, :].rearrange(
                              "(a p) d -> p a d", p=128),
                          deps=(ps[f"tcfree{sl}"],))

                def f_q(eng, sl=sl, j=j, width=width, nb=nb):
                    eng.scalar_tensor_tensor(
                        out=wti8(j)[:, 0:nb * width], in0=wtc(sl)[:, 0:nb * width],
                        scalar=QMAX,
                        in1=rs_bc[:, 0:width] if nb == 1 else rs2_bc[:, 0:nb * width],
                        op0=ALU.mult, op1=ALU.mult).then_inc(sems["vector"], 1)
                t_q = step("vector", f_q,
                           deps=(t_l, ps["rsbc"], ps.get("rs2", ()),
                                 ps[f"i8free{j}"]))
                ps[f"tcfree{sl}"] = (t_q,)
                ps["rsbc_readers"].append(t_q)

                def emit_st(hc=hc, nb=nb, j=j, t_q=t_q, width=width):
                    t_s = dma("gpsimd", f"pst{j}",
                              dest_dram[hc:hc + nb, :, 0:width].transpose([1, 0, 2]),
                              wti8(j)[:, 0:nb * width].rearrange(
                                  "p (a d) -> p a d", d=width),
                              deps=(t_q,))
                    ps[f"i8free{j}"] = (t_s,)
                ps["pend_st"] = emit_st
            return u
        hc = 0
        while hc < n_chunks:
            nb = min(nc2, n_chunks - hc)
            units.append(mk(hc, nb))
            hc += nb
        units.append(lambda: flush())
        return units

    g_row_toks, u_row_toks, d_row_toks = [], [], []
    prep_q += mk_rows(wg_in, swg_cols, IC, D, g_row_toks)
    prep_q += mk_rows(wu_in, swu_cols, IC, D, u_row_toks)
    prep_q += mk_rows(wd_in, swd_cols, HT, I, d_row_toks)
    prep_q += mk_rsc(swg_cols, IC, g_row_toks)
    prep_q += mk_fm(wgt_in, I, DC, wgq_d)
    prep_q += mk_rsc(swu_cols, IC, u_row_toks)
    prep_q += mk_fm(wut_in, I, DC, wuq_d)
    prep_q += mk_rsc(swd_cols, HT, d_row_toks)

    def u_rs2():
        def f_cp(eng):
            vchain(eng, eng.tensor_copy(out=rs2_bc[:, 0:D], in_=rs_bc[:, 0:D]))
            eng.tensor_copy(out=rs2_bc[:, D:2 * D],
                            in_=rs_bc[:, 0:D]).then_inc(sems["vector"], 1)
        t_cp = step("vector", f_cp, deps=(ps["rsbc"],))
        ps["rsbc_readers"].append(t_cp)
        ps["rs2"] = (t_cp,)
    prep_q += [u_rs2]
    prep_q += mk_fm(wdt_in, D, IC, wdq_d)
    prep_idx = [0]

    def pump(n):
        while n > 0 and prep_idx[0] < len(prep_q):
            prep_q[prep_idx[0]]()
            prep_idx[0] += 1
            n -= 1

    x_tok_bf_ = sb_a("x_tok_bf", [128, 2 * D], BF16)
    x_tok_bf = lambda s: x_tok_bf_[:, s * D:(s + 1) * D]

    xfm_ = sb_a("xfm", [128, 2 * DC * 512], BF16)
    q_e = sb_a("q_e", [128, 512], BF16)
    q_m = sb_a("q_m", [128, 512], BF16)
    q_out_ = sb_a("q_out", [128, DC * 512], BF16)
    k_e = sb_a("k_e", [128, 1024], BF16)
    k_m = sb_a("k_m", [128, 1024], BF16)
    k_bf_ = sb_a("k_bf", [128, 4 * 1024], BF16)
    k_bf = lambda s: k_bf_[:, s * 1024:(s + 1) * 1024]
    v_aug_ = sb_a("v_aug", [128, 4 * NH * 65], BF16)
    v_aug = lambda s: v_aug_[:, s * NH * 65:(s + 1) * NH * 65]

    def xfm_ap(sl, c):
        return xfm_[:, (sl * DC + c) * 512:(sl * DC + c + 1) * 512]

    def f_vones(eng):
        for tt in range(4):
            va = v_aug(tt).rearrange("p (h c) -> p h c", c=65)
            ins = eng.memset(va[:, :, 64:65], 1.0)
        ins.then_inc(sems["vector"], 1)
    t_vones = step("vector", f_vones)

    xtok_free = [(), ()]
    xf_free = ()
    xfm_free = [(), ()]
    tr_tok = [[None] * 4, [None] * 4]
    ps_q_free = [(), ()]
    ps_k_free = ()
    ps_v_free = ()
    ps_kv_free = [(), ()]
    q_e_free = ()
    q_out_free = ()
    k_e_free = ()
    kslot_free = [()] * 4
    kv_sb_tok = ()
    last_kv_mm = ()

    for ch in range(NCH):
        sl = ch % 2
        for tt in range(4):
            t_idx = ch * 4 + tt
            tsl = t_idx % 2
            t_xld = dma("gpsimd", f"xt{tsl}", x_tok_bf(tsl),
                        x_in[t_idx * 128:(t_idx + 1) * 128, :],
                        deps=(xtok_free[tsl],))

            def f_tr(eng, sl=sl, tt=tt, tsl=tsl):
                for c in range(DC):
                    eng.dma_start(out=xfm_ap(sl, c)[:, tt * 128:(tt + 1) * 128],
                                  in_=x_tok_bf(tsl)[:, c * 128:(c + 1) * 128],
                                  transpose=True).then_inc(sems[f"tr{tsl}"], 16)
            t_tr = plan.add("sync", f_tr, (t_xld, xfm_free[sl]), inc=16 * DC,
                            stream=f"tr{tsl}")
            tr_tok[sl][tt] = t_tr
            xtok_free[tsl] = (t_tr,)
        ch_tr = tuple(tr_tok[sl])

        q_mm_toks = []
        qvec_chunk_toks = []
        for m in range(DC):
            bank = m % 2

            def f_qmm(eng, m=m, bank=bank, sl=sl):
                for k in range(DC):
                    eng.matmul(psA[bank][:, :],
                               wq_bf[:, k * D + m * 128: k * D + (m + 1) * 128],
                               xfm_ap(sl, k),
                               start=(k == 0), stop=False)
                eng.matmul(psA[bank][:, :],
                           bq_bf[:, m * 128:(m + 1) * 128],
                           ones512[:, :],
                           start=False, stop=True).then_inc(sems["tensor"], 1)
            t_qmm = step("tensor", f_qmm,
                         deps=(t_wq, t_bias, t_ones, ps_q_free[bank], ch_tr))
            q_mm_toks.append(t_qmm)

            def f_qexp(eng, bank=bank):
                eng.activation(out=q_e[:, :], in_=psA[bank][:, :],
                               func=AF.Exp).then_inc(sems["scalar"], 1)
            t_qexp = step("scalar", f_qexp, deps=(t_qmm, q_e_free))

            def f_qvec(eng, bank=bank, m=m):
                vchain(eng, eng.tensor_scalar(out=q_m[:, :], in0=q_e[:, :],
                                              scalar1=1.0, scalar2=None, op0=ALU.min))
                eng.scalar_tensor_tensor(
                    out=q_out_[:, m * 512:(m + 1) * 512], in0=psA[bank][:, :],
                    scalar=0.0, in1=q_m[:, :], op0=ALU.max,
                    op1=ALU.add).then_inc(sems["vector"], 1)
            t_qvec = step("vector", f_qvec, deps=(t_qexp, q_out_free))
            ps_q_free[bank] = (t_qvec,)
            q_e_free = (t_qvec,)
            qvec_chunk_toks.append(t_qvec)
            pump(1)

        t_qst = dma("scalar", "qst",
                    q_d[:, :, ch * 512:(ch + 1) * 512].transpose([1, 0, 2]),
                    q_out_[:, :].rearrange("p (a d) -> p a d", d=512),
                    deps=(tuple(qvec_chunk_toks),))
        q_out_free = (t_qst,)

        kvec_toks = []
        for tt in range(4):
            def f_kvmm(eng, sl=sl, tt=tt):
                for k in range(DC):
                    lhs = xfm_ap(sl, k)[:, tt * 128:(tt + 1) * 128]
                    for half in range(2):
                        eng.matmul(psB[0][:, half * 512:(half + 1) * 512], lhs,
                                   wk_bf[:, k * D + half * 512: k * D + (half + 1) * 512],
                                   start=(k == 0), stop=False)
                        eng.matmul(psB[1][:, half * 512:(half + 1) * 512], lhs,
                                   wv_bf[:, k * D + half * 512: k * D + (half + 1) * 512],
                                   start=(k == 0), stop=False)
                for half in range(2):
                    eng.matmul(psB[0][:, half * 512:(half + 1) * 512], ones_row[:, :],
                               bk_bf[:, half * 512:(half + 1) * 512],
                               start=False, stop=True)
                    ins = eng.matmul(psB[1][:, half * 512:(half + 1) * 512], ones_row[:, :],
                                     bv_bf[:, half * 512:(half + 1) * 512],
                                     start=False, stop=True)
                ins.then_inc(sems["tensor"], 1)
            t_kvmm = step("tensor", f_kvmm,
                          deps=(t_wk, t_wv, t_bias, t_ones, ps_k_free, ps_v_free,
                                kslot_free[tt], ch_tr))

            def f_kexp(eng):
                eng.activation(out=k_e[:, :], in_=psB[0][:, :],
                               func=AF.Exp).then_inc(sems["scalar"], 1)
            t_kexp = step("scalar", f_kexp, deps=(t_kvmm, k_e_free))

            def f_kvec(eng, tt=tt):
                vchain(eng, eng.tensor_scalar(out=k_m[:, :], in0=k_e[:, :],
                                              scalar1=1.0, scalar2=None, op0=ALU.min))
                eng.scalar_tensor_tensor(
                    out=k_bf(tt), in0=psB[0][:, :], scalar=0.0,
                    in1=k_m[:, :], op0=ALU.max, op1=ALU.add)
                va = v_aug(tt).rearrange("p (h c) -> p h c", c=65)
                pv = psB[1][:, :].rearrange("p (h c) -> p h c", c=64)
                eng.tensor_copy(out=va[:, :, 0:64], in_=pv[:, :, :]).then_inc(
                    sems["vector"], 1)
            t_kvec = step("vector", f_kvec, deps=(t_kexp, t_vones, kslot_free[tt]))
            ps_k_free = (t_kvec,)
            ps_v_free = (t_kvec,)
            k_e_free = (t_kvec,)
            kvec_toks.append(t_kvec)
            pump(1)

        for p in range(NP):
            bank = p % 2

            def f_kvps(eng, p=p, bank=bank):
                hA, hB = 2 * p, 2 * p + 1
                for tt in range(4):
                    eng.matmul(psC[bank][0:64, 0:65],
                               k_bf(tt)[:, hA * 64:(hA + 1) * 64],
                               v_aug(tt)[:, hA * 65:(hA + 1) * 65],
                               start=(tt == 0), stop=(tt == 3),
                               tile_position=(0, 0))
                for tt in range(4):
                    ins = eng.matmul(psC[bank][64:128, 0:65],
                                     k_bf(tt)[:, hB * 64:(hB + 1) * 64],
                                     v_aug(tt)[:, hB * 65:(hB + 1) * 65],
                                     start=(tt == 0), stop=(tt == 3),
                                     tile_position=(0, 64))
                ins.then_inc(sems["tensor"], 1)
            t_kvps = step("tensor", f_kvps, deps=(tuple(kvec_toks), ps_kv_free[bank]))
            last_kv_mm = (t_kvps,)

            def f_kvacc(eng, p=p, bank=bank, first=(ch == 0)):
                dst = kv_sb[:, p * 65:(p + 1) * 65]
                if first:
                    eng.tensor_copy(out=dst, in_=psC[bank][:, 0:65]).then_inc(
                        sems["vector"], 1)
                else:
                    eng.tensor_tensor(out=dst, in0=dst, in1=psC[bank][:, 0:65],
                                      op=ALU.add).then_inc(sems["vector"], 1)
            t_kvacc = step("vector", f_kvacc, deps=(t_kvps, kv_sb_tok))
            kv_sb_tok = (t_kvacc,)
            ps_kv_free[bank] = (t_kvacc,)

        for tt in range(4):
            kslot_free[tt] = last_kv_mm
        xfm_free[sl] = (q_mm_toks[-1], t_kvmm)

        pump(2)

    pump(len(prep_q))

    t_kv_out = dma("gpsimd", "cc", cc_kv_i, kv_sb[:, :], deps=(kv_sb_tok,))

    def f_cc_kv(eng):
        eng.collective_compute(
            "AllReduce", ALU.add,
            replica_groups=[[0, 1], [2, 3], [4, 5], [6, 7]],
            ins=[cc_kv_i.opt()], outs=[cc_kv_o.opt()]).then_inc(sems["gpsimd"], 1)
    t_cc_kv = step("gpsimd", f_cc_kv, deps=(t_kv_out,), inc=1)
    t_kv_in = dma("gpsimd", "cc", kvt_sb[:, :], cc_kv_o, deps=(t_cc_kv,))

    bar2 = plan.barrier()
    stk_a.close()

    stk_c = ExitStack()
    xq_fm = stk_c.enter_context(nc.sbuf_tensor("xq_fm", [128, DC * T], BF16))

    def xq_ap(c):
        return xq_fm[:, c * T:(c + 1) * T]

    stk_b = ExitStack()

    def sb_b(name, shape, dtype):
        return stk_b.enter_context(nc.sbuf_tensor(name, shape, dtype))

    wo_bf = sb_b("wo_bf", [128, DC * D], BF16)
    blockdiag = sb_b("blockdiag", [128, NP * 128], BF16)
    normlhs = sb_b("normlhs", [128, DC * NH], BF16)
    q_ch_ = sb_b("q_ch", [128, 2 * DC * 512], BF16)
    q_ch = lambda s, c: q_ch_[:, (s * DC + c) * 512:(s * DC + c + 1) * 512]
    mix_bf_ = sb_b("mix_bf", [128, 2 * DC * 512], BF16)
    mix_bf = lambda s: mix_bf_[:, s * DC * 512:(s + 1) * DC * 512]
    norm_sb = sb_b("norm_sb", [16, 512], F32)
    norm_sb2 = sb_b("norm_sb2", [16, 512], F32)
    rexp_sb_ = sb_b("rexp_sb", [128, 2 * 512], F32)
    rexp_sb = lambda s: rexp_sb_[:, s * 512:(s + 1) * 512]
    recip_bf = sb_b("recip_bf", [16, 512], BF16)
    x0_tile_ = sb_b("x0_tile", [128, 2 * D], F32)
    x0_tile = lambda s: x0_tile_[:, s * D:(s + 1) * D]
    r1_st_ = sb_b("r1_st", [128, 4 * D], F32)
    r1_st = lambda s: r1_st_[:, s * D:(s + 1) * D]
    x1_bf_ = sb_b("x1_bf", [128, 4 * D], BF16)
    x1_bf = lambda s: x1_bf_[:, s * D:(s + 1) * D]
    sq_junk = sb_b("sq_junk", [128, D], F32)

    t_wo = dmas("gpsimd", "wo", [(wo_bf[:, k * D:(k + 1) * D], wo_in[k * 128:(k + 1) * 128, :])
                                 for k in range(DC)], deps=(bar2,))

    def f_blockdiag(eng):
        vchain(eng, eng.memset(blockdiag[:, :], 0.0))
        vchain(eng, eng.memset(normlhs[:, :], 0.0))
        for p in range(NP):
            eng.tensor_copy(out=blockdiag[0:64, p * 128: p * 128 + 64],
                            in_=kvt_sb[0:64, p * 65: p * 65 + 64])
            eng.tensor_copy(out=blockdiag[64:128, p * 128 + 64: p * 128 + 128],
                            in_=kvt_sb[64:128, p * 65: p * 65 + 64])
            eng.tensor_copy(out=normlhs[0:64, p * NH + 2 * p: p * NH + 2 * p + 1],
                            in_=kvt_sb[0:64, p * 65 + 64: p * 65 + 65])
            ins = eng.tensor_copy(out=normlhs[64:128, p * NH + 2 * p + 1: p * NH + 2 * p + 2],
                                  in_=kvt_sb[64:128, p * 65 + 64: p * 65 + 65])
        ins.then_inc(sems["vector"], 1)
    t_bd = step("vector", f_blockdiag, deps=(t_kv_in, bar2))

    q_slot_free = [(), ()]
    mix_slot_free = [(), ()]
    ps_norm_free = ()
    ps_mix_free = [(), ()]
    ps_rexp_free = [(), ()]
    rexp_sb_free = [(), ()]
    ps_wo_free = ()
    recip_free = ()
    x0_free = [bar2, bar2]
    x1_free = [(), (), (), ()]
    stats_tok = ()
    r1_free = [()] * 4
    sqj_tok = ()
    x1_store_toks = []
    x1tr_toks = []
    xmax_toks = []
    pend_tail_c = None

    def emit_tail_c(wo_toks, sq_toks, ch):
        nonlocal stats_tok

        def f_stats(eng):
            negm = ln_sc[:, 12:16]
            v = ln_sc[:, 4:8]
            vchain(eng, eng.tensor_scalar(out=negm, in0=rsums[:, 0:4], scalar1=-1.0 / D,
                                          scalar2=None, op0=ALU.mult))
            vchain(eng, eng.tensor_tensor(out=v, in0=negm, in1=negm, op=ALU.mult))
            vchain(eng, eng.scalar_tensor_tensor(out=v, in0=ssqs[:, 0:4], scalar=1.0 / D,
                                                 in1=v, op0=ALU.mult, op1=ALU.subtract))
            vchain(eng, eng.tensor_scalar(out=v, in0=v, scalar1=EPS_LN, scalar2=None,
                                          op0=ALU.add))
            eng.reciprocal(out=ln_sc[:, 8:12], in_=v).then_inc(sems["vector"], 1)
        t_stats = step("vector", f_stats, deps=(tuple(wo_toks), tuple(sq_toks)))

        def f_rstd(eng):
            eng.activation(out=ln_sc[:, 8:12], in_=ln_sc[:, 8:12],
                           func=AF.Sqrt).then_inc(sems["scalar"], 1)
        t_rstd = step("scalar", f_rstd, deps=(t_stats,))
        stats_tok = (t_rstd,)

        for tt in range(4):
            t_idx = ch * 4 + tt
            xsl = tt

            def f_aff(eng, tt=tt, xsl=xsl):
                eng.tensor_scalar(out=x1_bf(xsl), in0=r1_st(tt),
                                  scalar1=ln_sc[:, 12 + tt:13 + tt],
                                  scalar2=ln_sc[:, 8 + tt:9 + tt],
                                  op0=ALU.add, op1=ALU.mult).then_inc(sems["vector"], 1)
            t_aff = step("vector", f_aff, deps=((t_rstd,), x1_free[xsl]))
            r1_free[tt] = (t_aff,)

            def f_xmax(eng, t_idx=t_idx, xsl=xsl):
                eng.tensor_reduce(out=xmax_cols[:, t_idx:t_idx + 1],
                                  in_=x1_bf(xsl), axis=AX, op=ALU.max,
                                  apply_absolute_value=True).then_inc(sems["vector"], 1)
            t_xmax = step("vector", f_xmax, deps=(t_aff,))
            xmax_toks.append(t_xmax)
            t_x1st = dma("gpsimd", f"x1st{xsl}", x1_d[t_idx, :, :], x1_bf(xsl),
                         deps=(t_aff,))

            def f_x1tr(eng, t_idx=t_idx, xsl=xsl):
                for c in range(DC):
                    eng.dma_start(out=xq_ap(c)[:, t_idx * 128:(t_idx + 1) * 128],
                                  in_=x1_bf(xsl)[:, c * 128:(c + 1) * 128],
                                  transpose=True).then_inc(sems[f"x1tr{xsl}"], 16)
            t_x1tr = plan.add("sync", f_x1tr, (t_aff, bar2), inc=16 * DC,
                              stream=f"x1tr{xsl}")
            x1_free[xsl] = (t_x1st, t_xmax, t_x1tr)
            x1_store_toks.append(t_x1st)
            x1tr_toks.append(t_x1tr)

    for ch in range(NCH):
        sl = ch % 2
        t_qld = dmas("gpsimd", f"qch{sl}",
                     [(q_ch(sl, c), q_d[c, :, ch * 512:(ch + 1) * 512]) for c in range(DC)],
                     deps=(bar2, q_slot_free[sl]))

        def f_nmm(eng, sl=sl):
            for c in range(DC):
                ins = eng.matmul(psA[0][0:16, :],
                                 normlhs[:, c * NH:(c + 1) * NH],
                                 q_ch(sl, c),
                                 start=(c == 0), stop=(c == DC - 1))
            ins.then_inc(sems["tensor"], 1)
        t_nmm = step("tensor", f_nmm, deps=(t_qld, t_bd, ps_norm_free))

        def f_nrm(eng):
            vchain(eng, eng.tensor_scalar(out=norm_sb[:, :], in0=psA[0][0:16, :],
                                          scalar1=EPS_ATTN, scalar2=None, op0=ALU.max))
            vchain(eng, eng.reciprocal(out=norm_sb2[:, :], in_=norm_sb[:, :]))
            eng.tensor_copy(out=recip_bf[:, :], in_=norm_sb2[:, :]).then_inc(
                sems["vector"], 1)
        t_nrm = step("vector", f_nrm, deps=(t_nmm, recip_free))
        ps_norm_free = (t_nrm,)

        mix_toks = []
        last_mixmm = None
        for p in range(NP):
            bank = p % 2

            def f_mixmm(eng, p=p, bank=bank, sl=sl):
                eng.matmul(psC[bank][:, :],
                           blockdiag[:, p * 128:(p + 1) * 128],
                           q_ch(sl, p),
                           start=True, stop=True)
                eng.matmul(psB[0][:, bank * 512:(bank + 1) * 512],
                           oh_sb[:, p * 128:(p + 1) * 128],
                           recip_bf[:, :],
                           start=True, stop=True).then_inc(sems["tensor"], 1)
            t_mixmm = step("tensor", f_mixmm,
                           deps=(t_nrm, ps_mix_free[bank], ps_rexp_free[bank],
                                 mix_slot_free[sl], t_clip))
            last_mixmm = t_mixmm

            def f_rexpcp(eng, bank=bank):
                eng.copy(out=rexp_sb(bank),
                         in_=psB[0][:, bank * 512:(bank + 1) * 512]).then_inc(
                    sems["scalar"], 1)
            t_rexpcp = step("scalar", f_rexpcp,
                            deps=(t_mixmm, ps_rexp_free[bank], rexp_sb_free[bank]))
            ps_rexp_free[bank] = (t_rexpcp,)

            def f_div(eng, p=p, bank=bank, sl=sl):
                eng.tensor_tensor(out=mix_bf(sl)[:, p * 512:(p + 1) * 512],
                                  in0=psC[bank][:, :],
                                  in1=rexp_sb(bank),
                                  op=ALU.mult).then_inc(sems["vector"], 1)
            t_div = step("vector", f_div, deps=(t_mixmm, t_rexpcp))
            ps_mix_free[bank] = (t_div,)
            rexp_sb_free[bank] = (t_div,)
            mix_toks.append(t_div)
        recip_free = (last_mixmm,)
        q_slot_free[sl] = (last_mixmm, t_nmm)

        if pend_tail_c is not None:
            emit_tail_c(*pend_tail_c)
            pend_tail_c = None

        wo_toks = []
        sq_toks = []
        last_womm = None
        for tt in range(4):
            t_idx = ch * 4 + tt
            xsl = tt % 2
            t_x0 = dma("gpsimd", f"x0{xsl}", x0_tile(xsl),
                       x_in[t_idx * 128:(t_idx + 1) * 128, :], deps=(x0_free[xsl],))

            def f_womm(eng, tt=tt, sl=sl):
                for c in range(DC):
                    for half in range(2):
                        eng.matmul(psB[1][:, half * 512:(half + 1) * 512],
                                   mix_bf(sl)[:, c * 512 + tt * 128: c * 512 + (tt + 1) * 128],
                                   wo_bf[:, c * D + half * 512: c * D + (half + 1) * 512],
                                   start=(c == 0), stop=False)
                for half in range(2):
                    ins = eng.matmul(psB[1][:, half * 512:(half + 1) * 512],
                                     ones_row[:, :],
                                     bo_bf[:, half * 512:(half + 1) * 512],
                                     start=False, stop=True)
                ins.then_inc(sems["tensor"], 1)
            t_womm = step("tensor", f_womm, deps=(tuple(mix_toks), ps_wo_free, t_wo))
            last_womm = t_womm

            def f_r1(eng, tt=tt, xsl=xsl):
                eng.scalar_tensor_tensor(
                    out=r1_st(tt), in0=psB[1][:, :], scalar=1.0,
                    in1=x0_tile(xsl), op0=ALU.mult, op1=ALU.add,
                    accum_out=rsums[:, tt:tt + 1]).then_inc(sems["vector"], 1)
            t_r1 = step("vector", f_r1, deps=(t_womm, t_x0, stats_tok, r1_free[tt]))
            ps_wo_free = (t_r1,)
            x0_free[xsl] = (t_r1,)
            wo_toks.append(t_r1)

            def f_sq(eng, tt=tt):
                eng.scalar_tensor_tensor(
                    out=sq_junk[:, :], in0=r1_st(tt), scalar=1.0, in1=r1_st(tt),
                    op0=ALU.mult, op1=ALU.mult,
                    accum_out=ssqs[:, tt:tt + 1]).then_inc(sems["vector"], 1)
            t_sq = step("vector", f_sq, deps=(t_r1, sqj_tok))
            sqj_tok = (t_sq,)
            sq_toks.append(t_sq)
        mix_slot_free[sl] = (last_womm,)
        pend_tail_c = (wo_toks, sq_toks, ch)

    emit_tail_c(*pend_tail_c)
    pend_tail_c = None

    bar3 = plan.barrier()
    stk_b.close()

    def f_xred(eng):
        eng.tensor_reduce(out=red_sc[:, :], in_=xmax_cols[:, :], axis=AX,
                          op=ALU.max).then_inc(sems["vector"], 1)
    t_xred = step("vector", f_xred, deps=(tuple(xmax_toks),))
    t_pm1 = dma("scalar", "pm", pmax_d, red_sc[:, :], deps=(t_xred,))
    t_pm1b = dma("scalar", "pm", smax1[:, :], pmax_d, deps=(t_pm1,))

    def f_xred2(eng):
        eng.tensor_reduce(out=gmax1[:, :], in_=smax1[:, :], axis=AX,
                          op=ALU.max).then_inc(sems["vector"], 1)
    t_xred2 = step("vector", f_xred2, deps=(t_pm1b,))
    t_m1o = dma("gpsimd", "cc", cc_m1_i, gmax1[:, :], deps=(t_xred2,))

    def f_cc_m1(eng):
        eng.collective_compute(
            "AllReduce", ALU.max, replica_groups=[list(range(NCORES))],
            ins=[cc_m1_i.opt()], outs=[cc_m1_o.opt()]).then_inc(sems["gpsimd"], 1)
    t_cc_m1 = step("gpsimd", f_cc_m1, deps=(t_m1o,), inc=1)
    t_m1i = dma("gpsimd", "cc", gmax1[:, :], cc_m1_o, deps=(t_cc_m1,))

    def f_sigx(eng):
        eng.activation(out=sc_small[:, 0:1], in_=clipx_s[:, :],
                       func=AF.Sigmoid).then_inc(sems["scalar"], 1)
    t_sigx = step("scalar", f_sigx, deps=(t_clip, bar3))

    def f_sx(eng):
        vchain(eng, eng.tensor_tensor(out=sc_small[:, 1:2], in0=sc_small[:, 0:1],
                                      in1=gmax1[:, :], op=ALU.mult))
        vchain(eng, eng.tensor_scalar(out=sc_small[:, 1:2], in0=sc_small[:, 1:2],
                                      scalar1=1.0 / QMAX, scalar2=None, op0=ALU.mult))
        eng.reciprocal(out=sc_small[:, 2:3], in_=sc_small[:, 1:2]).then_inc(
            sems["vector"], 1)
    t_sx = step("vector", f_sx, deps=(t_sigx, t_m1i))
    t_b1 = dma("scalar", "bc1", bc1_d, sc_small[:, 1:2], deps=(t_sx,))
    t_b2 = dma("scalar", "bc2", bc2_d, sc_small[:, 2:3], deps=(t_sx,))
    t_sxbc = dma("scalar", "bc1", sx_bc[:, :], bc1_d.partition_broadcast(128),
                 deps=(t_b1,))
    t_rsxbc = dma("scalar", "bc2", rsx_bc[:, :], bc2_d.partition_broadcast(128),
                  deps=(t_b2,))

    def f_svec(eng):
        vchain(eng, eng.tensor_scalar(out=svec_g[:, :], in0=swg_cols[:, :],
                                      scalar1=sx_bc[:, :], scalar2=1.0 / QMAX,
                                      op0=ALU.mult, op1=ALU.mult))
        eng.tensor_scalar(out=svec_u[:, :], in0=swu_cols[:, :],
                          scalar1=sx_bc[:, :], scalar2=1.0 / QMAX,
                          op0=ALU.mult, op1=ALU.mult).then_inc(sems["vector"], 1)
    t_svec = step("vector", f_svec, deps=(t_sxbc, tuple(g_row_toks), tuple(u_row_toks)))

    stk_c1 = ExitStack()

    def sb_c1(name, shape, dtype):
        return stk_c1.enter_context(nc.sbuf_tensor(name, shape, dtype))

    xqi8 = sb_c1("xqi8", [128, 512], I8)
    wfm_ = sb_c1("wfm", [128, 2 * 2 * D], BF16)
    wfm = lambda s, gu: wfm_[:, (s * 2 + gu) * D:(s * 2 + gu + 1) * D]
    g_act = sb_c1("g_act", [128, 512], F32)
    h_bf_ = sb_c1("h_bf", [128, 2 * 512], BF16)
    h_bf = lambda s: h_bf_[:, s * 512:(s + 1) * 512]

    g_banks = [psA[0][:, :], psA[1][:, :], psB[0][:, 0:512], psB[0][:, 512:1024]]
    u_banks = [psC[0][:, :], psC[1][:, :], psB[1][:, 0:512], psB[1][:, 512:1024]]
    quant_toks = []
    for ch in range(NCH):
        ch_tr4 = tuple(x1tr_toks[ch * 4 + k] for k in range(4))

        def f_xq(eng, ch=ch):
            for hc in range(DC):
                strip = xq_ap(hc)[:, ch * 512:(ch + 1) * 512]
                vchain(eng, eng.tensor_scalar(out=xqi8[:, :], in0=strip,
                                              scalar1=rsx_bc[:, :], scalar2=None,
                                              op0=ALU.mult))
                ins = eng.tensor_copy(out=strip, in_=xqi8[:, :])
                if hc < DC - 1:
                    vchain(eng, ins)
            ins.then_inc(sems["vector"], 1)
        prev_q = quant_toks[-1] if quant_toks else ()
        t_xq = step("vector", f_xq, deps=(t_rsxbc, ch_tr4, bar3, prev_q))
        quant_toks.append(t_xq)

    def wload(i, wsl, deps):
        src_g = wgq_d[:, :, i * 128:(i + 1) * 128].transpose([1, 0, 2])
        src_u = wuq_d[:, :, i * 128:(i + 1) * 128].transpose([1, 0, 2])
        return dmas("sync", f"wl{wsl}",
                    [(wfm(wsl, 0).rearrange("p (c j) -> p c j", j=128), src_g),
                     (wfm(wsl, 1).rearrange("p (c j) -> p c j", j=128), src_u)],
                    deps=deps)

    wfm_free = [(), ()]
    ps_g_free = [(), (), (), ()]
    ps_u_free = [(), (), (), ()]
    g_act_free = ()
    h_slot_free = [(), ()]
    wl_toks = [None] * IC
    h_store_toks = []
    hmax_toks = []
    last_gumm = ()

    wl_toks[0] = wload(0, 0, deps=(bar3,))
    if IC > 1:
        wl_toks[1] = wload(1, 1, deps=(bar3,))

    for i in range(IC):
        wsl = i % 2
        t_wl = wl_toks[i]

        gumm_toks = []
        for ch in range(NCH):
            bank = ch % 4

            def f_gumm(eng, wsl=wsl, ch=ch, bank=bank):
                for hc in range(DC):
                    eng.matmul(g_banks[bank],
                               wfm(wsl, 0)[:, hc * 128:(hc + 1) * 128],
                               xq_ap(hc)[:, ch * 512:(ch + 1) * 512],
                               start=(hc == 0), stop=(hc == DC - 1))
                for hc in range(DC):
                    ins = eng.matmul(u_banks[bank],
                                     wfm(wsl, 1)[:, hc * 128:(hc + 1) * 128],
                                     xq_ap(hc)[:, ch * 512:(ch + 1) * 512],
                                     start=(hc == 0), stop=(hc == DC - 1))
                ins.then_inc(sems["tensor"], 1)
            t_gumm = step("tensor", f_gumm,
                          deps=(t_wl, quant_toks[ch], ps_g_free[bank], ps_u_free[bank]))
            gumm_toks.append(t_gumm)

            def f_gact(eng, i=i, bank=bank):
                eng.activation(out=g_act[:, :], in_=g_banks[bank], func=AF.Silu,
                               scale=svec_g[:, i:i + 1]).then_inc(sems["scalar"], 1)
            t_gact = step("scalar", f_gact, deps=(t_gumm, t_svec, g_act_free))

            hsl = ch % 2

            def f_h(eng, i=i, ch=ch, hsl=hsl, bank=bank):
                vchain(eng, eng.scalar_tensor_tensor(
                    out=h_bf(hsl), in0=u_banks[bank], scalar=svec_u[:, i:i + 1],
                    in1=g_act[:, :], op0=ALU.mult, op1=ALU.mult))
                eng.tensor_reduce(out=hmax_cols[:, i * NCH + ch:i * NCH + ch + 1],
                                  in_=h_bf(hsl), axis=AX, op=ALU.max,
                                  apply_absolute_value=True).then_inc(sems["vector"], 1)
            t_h = step("vector", f_h, deps=(t_gact, h_slot_free[hsl], t_svec))
            hmax_toks.append(t_h)
            ps_g_free[bank] = (t_h,)
            ps_u_free[bank] = (t_h,)
            g_act_free = (t_h,)
            t_hst = dma("gpsimd", f"hst{hsl}", h_d[i, :, ch * 512:(ch + 1) * 512],
                        h_bf(hsl), deps=(t_h,))
            h_slot_free[hsl] = (t_hst,)
            h_store_toks.append(t_hst)

        last_gumm = (gumm_toks[-1],)
        if DEBUG_DUMPS and i == 20:
            dmas("gpsimd", "dbgw", [(dbg_wfm[:, 0:D], wfm(wsl, 0)),
                                    (dbg_wfm[:, D:2 * D], wfm(wsl, 1))],
                 deps=(gumm_toks[-1],))
        if i + 2 < IC:
            wfm_free[wsl] = (gumm_toks[-1],)
            wl_toks[i + 2] = wload(i + 2, wsl, deps=(gumm_toks[-1],))

    if DEBUG_DUMPS:
        dmas("gpsimd", "dbgd",
             [(dbg_xq[c, :, :], xq_ap(c)) for c in range(DC)],
             deps=(last_gumm, tuple(quant_toks)))

    t_swdr = dmas("scalar", "swd", [(swd_dram[:, ht * 128:(ht + 1) * 128],
                                     swd_cols[:, ht:ht + 1]) for ht in range(HT)],
                  deps=(tuple(d_row_toks),))
    t_swdl = dma("scalar", "swd", swd_row[:, :], swd_dram, deps=(t_swdr,))

    d_free = (last_gumm, g_act_free, h_slot_free[0], h_slot_free[1],
              tuple(h_store_toks))
    bar4 = plan.barrier()
    stk_c1.close()
    stk_c.close()

    stk_d = ExitStack()

    def sb_d(name, shape, dtype):
        return stk_d.enter_context(nc.sbuf_tensor(name, shape, dtype))

    h_ld_ = sb_d("h_ld", [128, 2 * IC * 512], BF16)
    h_ld = lambda s, ic: h_ld_[:, (s * IC + ic) * 512:(s * IC + ic + 1) * 512]
    hq_i8_ = sb_d("hq_i8", [128, 2 * 1024], I8)
    hq_i8 = lambda s: hq_i8_[:, s * 1024:(s + 1) * 1024]
    x1f_ = sb_d("x1f", [128, 2 * D], BF16)
    x1f = lambda s: x1f_[:, s * D:(s + 1) * D]
    r2_st_ = sb_d("r2_st", [128, 4 * D], F32)
    r2_st = lambda s: r2_st_[:, s * D:(s + 1) * D]
    t2_st = sb_d("t2_st", [128, D], F32)
    out_t_ = sb_d("out_t", [128, 4 * D], F32)
    out_t = lambda s: out_t_[:, s * D:(s + 1) * D]
    sq_junk2 = sb_d("sq_junk2", [128, D], F32)
    swd_bc = sb_d("swd_bc", [128, D], F32)
    wd_fm = sb_d("wd_fm", [128, IC * D], BF16)

    def wdfm_ap(ic):
        return wd_fm[:, ic * D:(ic + 1) * D]

    t_wdfm = dmas("sync", "wdfm", [(wdfm_ap(ic), wdq_d[ic, :, :]) for ic in range(IC)],
                  deps=(d_free,))

    def f_hred(eng):
        eng.tensor_reduce(out=red_sc[:, :], in_=hmax_cols[:, :], axis=AX,
                          op=ALU.max).then_inc(sems["vector"], 1)
    t_hred = step("vector", f_hred, deps=(tuple(hmax_toks),))
    t_pm2 = dma("scalar", "pm", pmax_d, red_sc[:, :], deps=(t_hred, t_pm1b))
    t_pm2b = dma("scalar", "pm", smax1[:, :], pmax_d, deps=(t_pm2, t_xred2))

    def f_hred2(eng):
        eng.tensor_reduce(out=gmax2[:, :], in_=smax1[:, :], axis=AX,
                          op=ALU.max).then_inc(sems["vector"], 1)
    t_hred2 = step("vector", f_hred2, deps=(t_pm2b,))
    t_m2o = dma("gpsimd", "cc", cc_m2_i, gmax2[:, :], deps=(t_hred2,))

    def f_cc_m2(eng):
        eng.collective_compute(
            "AllReduce", ALU.max, replica_groups=[list(range(NCORES))],
            ins=[cc_m2_i.opt()], outs=[cc_m2_o.opt()]).then_inc(sems["gpsimd"], 1)
    t_cc_m2 = step("gpsimd", f_cc_m2, deps=(t_m2o,), inc=1)
    t_m2i = dma("gpsimd", "cc", gmax2[:, :], cc_m2_o, deps=(t_cc_m2,))

    def f_sigd(eng):
        eng.activation(out=sc_small[:, 3:4], in_=clipd_s[:, :],
                       func=AF.Sigmoid).then_inc(sems["scalar"], 1)
    t_sigd = step("scalar", f_sigd, deps=(t_clip, bar4))

    def f_sh(eng):
        vchain(eng, eng.tensor_tensor(out=sc_small[:, 4:5], in0=sc_small[:, 3:4],
                                      in1=gmax2[:, :], op=ALU.mult))
        vchain(eng, eng.tensor_scalar(out=sc_small[:, 4:5], in0=sc_small[:, 4:5],
                                      scalar1=1.0 / QMAX, scalar2=None, op0=ALU.mult))
        vchain(eng, eng.reciprocal(out=sc_small[:, 5:6], in_=sc_small[:, 4:5]))
        eng.tensor_scalar(out=sc_small[:, 7:8], in0=sc_small[:, 4:5],
                          scalar1=1.0 / QMAX, scalar2=None,
                          op0=ALU.mult).then_inc(sems["vector"], 1)
    t_sh = step("vector", f_sh, deps=(t_sigd, t_m2i))
    t_b3 = dma("scalar", "bc3", bc3_d, sc_small[:, 5:6], deps=(t_sh,))
    t_rshbc = dma("scalar", "bc3", rsh_bc[:, :], bc3_d.partition_broadcast(128),
                  deps=(t_b3,))

    def f_swdsc(eng):
        eng.activation(out=swd_sc[:, :], in_=swd_row[:, :], func=AF.Copy,
                       scale=sc_small[:, 7:8]).then_inc(sems["scalar"], 1)
    t_swdsc = step("scalar", f_swdsc, deps=(t_sh, t_swdl))
    t_swdd = dma("scalar", "swd", swd_dram, swd_sc[:, :], deps=(t_swdsc, t_swdl))
    t_swdbc = dma("scalar", "swd", swd_bc[:, :], swd_dram.partition_broadcast(128),
                  deps=(t_swdd,))

    h_ld_free = [(), ()]
    ps_dn_free = [(), ()]
    x1f_free = [(), ()]
    out_free = [(), (), (), ()]
    stats2_tok = ()
    r2_free = [()] * 4
    i8_free = [(), ()]
    out_toks = []
    pend_tail_f = None

    def emit_tail_f(dn_toks, sq2_toks, ch):
        nonlocal stats2_tok

        def f_stats2(eng):
            negm = ln2_sc[:, 12:16]
            v = ln2_sc[:, 4:8]
            vchain(eng, eng.tensor_scalar(out=negm, in0=rsums2[:, 0:4],
                                          scalar1=-1.0 / D, scalar2=None, op0=ALU.mult))
            vchain(eng, eng.tensor_tensor(out=v, in0=negm, in1=negm, op=ALU.mult))
            vchain(eng, eng.scalar_tensor_tensor(out=v, in0=ssqs2[:, 0:4],
                                                 scalar=1.0 / D, in1=v,
                                                 op0=ALU.mult, op1=ALU.subtract))
            vchain(eng, eng.tensor_scalar(out=v, in0=v, scalar1=EPS_LN, scalar2=None,
                                          op0=ALU.add))
            eng.reciprocal(out=ln2_sc[:, 8:12], in_=v).then_inc(sems["vector"], 1)
        t_stats2 = step("vector", f_stats2, deps=(tuple(dn_toks), tuple(sq2_toks)))

        def f_rstd2(eng):
            eng.activation(out=ln2_sc[:, 8:12], in_=ln2_sc[:, 8:12],
                           func=AF.Sqrt).then_inc(sems["scalar"], 1)
        t_rstd2 = step("scalar", f_rstd2, deps=(t_stats2,))
        stats2_tok = (t_rstd2,)

        for tt in range(4):
            t_idx = ch * 4 + tt
            osl = tt

            def f_aff2(eng, tt=tt, osl=osl):
                eng.tensor_scalar(out=out_t(osl), in0=r2_st(tt),
                                  scalar1=ln2_sc[:, 12 + tt:13 + tt],
                                  scalar2=ln2_sc[:, 8 + tt:9 + tt],
                                  op0=ALU.add, op1=ALU.mult).then_inc(sems["vector"], 1)
            t_aff2 = step("vector", f_aff2, deps=((t_rstd2,), out_free[osl]))
            r2_free[tt] = (t_aff2,)
            t_ost = dma("gpsimd", f"ost{osl}", out_d[t_idx * 128:(t_idx + 1) * 128, :],
                        out_t(osl), deps=(t_aff2,))
            out_free[osl] = (t_ost,)
            out_toks.append(t_ost)

    def emit_hld(ch):
        sl = ch % 2
        return dmas("sync", f"hld{sl}",
                    [(h_ld(sl, ic), h_d[ic, :, ch * 512:(ch + 1) * 512])
                     for ic in range(IC)],
                    deps=(d_free, h_ld_free[sl]))

    def emit_hq_unit(ch, j, t_hld):
        sl = ch % 2
        ic = 2 * j
        n2 = min(2, IC - ic)
        isl = j % 2

        def f_hqs(eng, sl=sl, ic=ic, isl=isl, n2=n2):
            eng.activation(out=hq_i8(isl)[:, 0:n2 * 512],
                           in_=h_ld_[:, (sl * IC + ic) * 512:
                                     (sl * IC + ic + n2) * 512],
                           func=AF.Copy,
                           scale=rsh_bc[:, :]).then_inc(sems["scalar"], 1)
        t_hqs = step("scalar", f_hqs, deps=(t_hld, t_rshbc, i8_free[isl]))

        def f_hqv(eng, sl=sl, ic=ic, isl=isl, n2=n2):
            eng.tensor_copy(out=h_ld_[:, (sl * IC + ic) * 512:
                                      (sl * IC + ic + n2) * 512],
                            in_=hq_i8(isl)[:, 0:n2 * 512]).then_inc(
                sems["vector"], 1)
        t_hqv = step("vector", f_hqv, deps=(t_hqs,))
        i8_free[isl] = (t_hqv,)
        return t_hqv

    NQ = (IC + 1) // 2
    hld_by = {0: emit_hld(0)}
    hq_by = {0: [emit_hq_unit(0, j, hld_by[0]) for j in range(NQ)]}
    qsplit = [(0, 3), (3, 6), (6, 9), (9, NQ)]

    for ch in range(NCH):
        sl = ch % 2
        if ch + 1 < NCH:
            hld_by[ch + 1] = emit_hld(ch + 1)

        if pend_tail_f is not None:
            emit_tail_f(*pend_tail_f)
            pend_tail_f = None

        all_hq = tuple(hq_by.pop(ch))
        nxt_hq = []
        dn_toks = []
        sq2_toks = []
        for tt in range(4):
            t_idx = ch * 4 + tt
            xsl = tt % 2
            dbank = tt % 2
            t_x1f = dma("gpsimd", f"x1f{xsl}", x1f(xsl), x1_d[t_idx, :, :],
                        deps=(d_free, bar3, x1f_free[xsl]))

            def f_dnmm(eng, sl=sl, tt=tt, dbank=dbank):
                for ic in range(IC):
                    for half in range(2):
                        ins = eng.matmul(
                            psB[dbank][:, half * 512:(half + 1) * 512],
                            h_ld(sl, ic)[:, tt * 128:(tt + 1) * 128],
                            wdfm_ap(ic)[:, half * 512:(half + 1) * 512],
                            start=(ic == 0), stop=(ic == IC - 1))
                ins.then_inc(sems["tensor"], 1)
            t_dnmm = step("tensor", f_dnmm,
                          deps=(all_hq, ps_dn_free[dbank], t_wdfm, bar4))

            def f_r2(eng, tt=tt, xsl=xsl, dbank=dbank):
                vchain(eng, eng.tensor_tensor(out=t2_st[:, :], in0=psB[dbank][:, :],
                                              in1=swd_bc[:, :], op=ALU.mult))
                eng.scalar_tensor_tensor(
                    out=r2_st(tt), in0=t2_st[:, :], scalar=1.0,
                    in1=x1f(xsl), op0=ALU.mult, op1=ALU.add,
                    accum_out=rsums2[:, tt:tt + 1]).then_inc(sems["vector"], 1)
            prev_r2 = dn_toks[-1] if dn_toks else ()
            t_r2 = step("vector", f_r2,
                        deps=(t_dnmm, t_x1f, t_swdbc, stats2_tok, r2_free[tt], prev_r2))
            ps_dn_free[dbank] = (t_r2,)
            x1f_free[xsl] = (t_r2,)
            dn_toks.append(t_r2)

            def f_sq2(eng, tt=tt):
                eng.scalar_tensor_tensor(
                    out=sq_junk2[:, :], in0=r2_st(tt), scalar=1.0, in1=r2_st(tt),
                    op0=ALU.mult, op1=ALU.mult,
                    accum_out=ssqs2[:, tt:tt + 1]).then_inc(sems["vector"], 1)
            t_sq2 = step("vector", f_sq2, deps=(t_r2, sqj_tok))
            sqj_tok = (t_sq2,)
            sq2_toks.append(t_sq2)

            if ch + 1 < NCH:
                for j in range(*qsplit[tt]):
                    nxt_hq.append(emit_hq_unit(ch + 1, j, hld_by[ch + 1]))

        if ch + 1 < NCH:
            hq_by[ch + 1] = nxt_hq
        h_ld_free[sl] = (dn_toks[-1],)
        pend_tail_f = (dn_toks, sq2_toks, ch)

    emit_tail_f(*pend_tail_f)
    pend_tail_f = None

    stk_d.close()

    if DEBUG_DUMPS:
        t_dbg = dmas("gpsimd", "dbgd",
                     [(dbg_x1, x1_d), (dbg_h, h_d), (dbg_q, q_d), (dbg_wg, wgq_d)],
                     deps=(tuple(out_toks),))
        t_dbg2 = dma("gpsimd", "dbgd", dbg_kv, cc_kv_o, deps=(t_dbg,))
        out_toks.append(t_dbg2)

    def f_final(eng):
        eng.memset(sc_small[:, 6:7], 0.0).then_inc(sems["vector"], 1)
    step("vector", f_final, deps=(tuple(out_toks),))

    for s in plan.count:
        sems[s] = nc.alloc_semaphore(name=f"sem_{s}")
    sems["vchain"] = nc.alloc_semaphore(name="sem_vchain")
    sems["schain"] = nc.alloc_semaphore(name="sem_schain")

    def emit(eng_name, eng):
        watermark = {s: 0 for s in plan.count}
        for (e, fn, deps) in plan.steps:
            if e != eng_name:
                continue
            for (de, dv) in deps:
                if watermark[de] >= dv:
                    continue
                eng.wait_ge(sems[de], dv)
                watermark[de] = dv
            fn(eng)

    with nc.Block() as block:
        @block.tensor
        def _(tensor):
            emit("tensor", tensor)

        @block.vector
        def _(vector):
            emit("vector", vector)

        @block.scalar
        def _(scalar):
            emit("scalar", scalar)

        @block.sync
        def _(sync):
            emit("sync", sync)

        @block.gpsimd
        def _(gpsimd):
            emit("gpsimd", gpsimd)

    return nc


_NC_CACHE = []


def kernel(L0_attn_inputs_q, L0_attn_Wq, L0_attn_bq, L0_attn_Wk, L0_attn_bk,
           L0_attn_Wv, L0_attn_bv, L0_attn_Wo, L0_attn_bo,
           L0_ffn_clip_x, L0_ffn_clip_down, L0_ffn_w_gate, L0_ffn_w_up,
           L0_ffn_w_down):
    x = np.ascontiguousarray(np.asarray(L0_attn_inputs_q, dtype=np.float32))
    onehot = np.zeros((NH, NP * 128), dtype=ml_dtypes.bfloat16)
    for p in range(NP):
        onehot[2 * p, p * 128:p * 128 + 64] = 1.0
        onehot[2 * p + 1, p * 128 + 64:p * 128 + 128] = 1.0

    wg = np.ascontiguousarray(L0_ffn_w_gate, dtype=np.float32)
    wu = np.ascontiguousarray(L0_ffn_w_up, dtype=np.float32)
    wd = np.ascontiguousarray(L0_ffn_w_down, dtype=np.float32)
    common = {
        "wq": np.ascontiguousarray(L0_attn_Wq, dtype=np.float32),
        "wk": np.ascontiguousarray(L0_attn_Wk, dtype=np.float32),
        "wv": np.ascontiguousarray(L0_attn_Wv, dtype=np.float32),
        "wo": np.ascontiguousarray(L0_attn_Wo, dtype=np.float32),
        "bq": np.ascontiguousarray(L0_attn_bq, dtype=np.float32).reshape(1, D),
        "bk": np.ascontiguousarray(L0_attn_bk, dtype=np.float32).reshape(1, D),
        "bv": np.ascontiguousarray(L0_attn_bv, dtype=np.float32).reshape(1, D),
        "bo": np.ascontiguousarray(L0_attn_bo, dtype=np.float32).reshape(1, D),
        "clipx": np.ascontiguousarray(L0_ffn_clip_x, dtype=np.float32).reshape(1, 1),
        "clipd": np.ascontiguousarray(L0_ffn_clip_down, dtype=np.float32).reshape(1, 1),
        "wg": wg, "wu": wu, "wd": wd,
        "wgt": np.ascontiguousarray(wg.T),
        "wut": np.ascontiguousarray(wu.T),
        "wdt": np.ascontiguousarray(wd.T),
        "onehot": onehot,
    }
    in_maps = []
    for c in range(NCORES):
        b, half = c // 2, c % 2
        xc = np.ascontiguousarray(x[b, half * T:(half + 1) * T, :])
        in_maps.append({"x": xc, **common})

    if not _NC_CACHE:
        _NC_CACHE.append(build_program())
    nc = _NC_CACHE[0]
    res = run_bass_kernel_spmd(nc, in_maps, core_ids=list(range(NCORES)))
    out = np.empty((B, S, D), dtype=np.float32)
    for c in range(NCORES):
        b, half = c // 2, c % 2
        out[b, half * T:(half + 1) * T, :] = res.results[c]["out"]
    return out


# version 17
# speedup vs baseline: 1.1397x; 1.0354x over previous
from contextlib import ExitStack

import numpy as np
import ml_dtypes
import concourse.bass as bass
import concourse.mybir as mybir
from concourse.bass_utils import run_bass_kernel_spmd

F32 = mybir.dt.float32
BF16 = mybir.dt.bfloat16
I8 = mybir.dt.int8
AF = mybir.ActivationFunctionType
ALU = mybir.AluOpType
AX = mybir.AxisListType.X

DEBUG_DUMPS = False
NCORES = 8
B, S, D = 4, 8192, 1024
T = 4096
NT = T // 128
NCH = T // 512
DC = D // 128
NH, DH = 16, 64
NP = NH // 2
I = 2816
IC = I // 128
HT = D // 128
EPS_ATTN = 1e-6
EPS_LN = 1e-5
QMAX = 127.0

ENGS = ("tensor", "vector", "scalar", "sync", "gpsimd")
COMPUTE_ENGS = ("tensor", "vector", "scalar")


class Plan:

    def __init__(self):
        self.steps = []
        self.count = {}
        self.eng_of = {}

    @staticmethod
    def _flatten(deps, out):
        for d in deps:
            if d is None or d == () or d == []:
                continue
            if isinstance(d, tuple) and len(d) == 2 and isinstance(d[0], str):
                out.append(d)
            else:
                Plan._flatten(d, out)

    def add(self, eng, fn, deps=(), inc=1, stream=None):
        stream = stream or eng
        self.eng_of.setdefault(stream, eng)
        flat = []
        self._flatten(deps, flat)
        self.count[stream] = self.count.get(stream, 0) + inc
        tok = (stream, self.count[stream])
        self.steps.append((eng, fn, tuple(flat)))
        return tok

    def barrier(self):
        return tuple((s, v) for s, v in self.count.items() if v > 0)


def build_program():
    nc = bass.Bass("TRN2", target_bir_lowering=False)
    plan = Plan()

    x_in = nc.dram_tensor("x", [T, D], F32, kind="ExternalInput").ap()
    wq_in = nc.dram_tensor("wq", [D, D], F32, kind="ExternalInput").ap()
    wk_in = nc.dram_tensor("wk", [D, D], F32, kind="ExternalInput").ap()
    wv_in = nc.dram_tensor("wv", [D, D], F32, kind="ExternalInput").ap()
    wo_in = nc.dram_tensor("wo", [D, D], F32, kind="ExternalInput").ap()
    bq_in = nc.dram_tensor("bq", [1, D], F32, kind="ExternalInput").ap()
    bk_in = nc.dram_tensor("bk", [1, D], F32, kind="ExternalInput").ap()
    bv_in = nc.dram_tensor("bv", [1, D], F32, kind="ExternalInput").ap()
    bo_in = nc.dram_tensor("bo", [1, D], F32, kind="ExternalInput").ap()
    clipx_in = nc.dram_tensor("clipx", [1, 1], F32, kind="ExternalInput").ap()
    clipd_in = nc.dram_tensor("clipd", [1, 1], F32, kind="ExternalInput").ap()
    wg_in = nc.dram_tensor("wg", [I, D], F32, kind="ExternalInput").ap()
    wu_in = nc.dram_tensor("wu", [I, D], F32, kind="ExternalInput").ap()
    wd_in = nc.dram_tensor("wd", [D, I], F32, kind="ExternalInput").ap()
    wgt_in = nc.dram_tensor("wgt", [D, I], F32, kind="ExternalInput").ap()
    wut_in = nc.dram_tensor("wut", [D, I], F32, kind="ExternalInput").ap()
    wdt_in = nc.dram_tensor("wdt", [I, D], F32, kind="ExternalInput").ap()
    oh_in = nc.dram_tensor("onehot", [NH, NP * 128], BF16, kind="ExternalInput").ap()
    out_d = nc.dram_tensor("out", [T, D], F32, kind="ExternalOutput").ap()
    if DEBUG_DUMPS:
        dbg_x1 = nc.dram_tensor("dbg_x1", [NT, 128, D], BF16, kind="ExternalOutput").ap()
        dbg_h = nc.dram_tensor("dbg_h", [IC, 128, T], BF16, kind="ExternalOutput").ap()
        dbg_q = nc.dram_tensor("dbg_q", [DC, 128, T], BF16, kind="ExternalOutput").ap()
        dbg_kv = nc.dram_tensor("dbg_kv", [128, NP * 65], F32, kind="ExternalOutput").ap()
        dbg_wg = nc.dram_tensor("dbg_wg", [DC, 128, I], BF16, kind="ExternalOutput").ap()
        dbg_xq = nc.dram_tensor("dbg_xq", [DC, 128, T], BF16, kind="ExternalOutput").ap()
        dbg_wfm = nc.dram_tensor("dbg_wfm", [128, 2 * D], BF16, kind="ExternalOutput").ap()

    q_d = nc.dram_tensor("q_d", [DC, 128, T], BF16).ap()
    x1_d = nc.dram_tensor("x1_d", [NT, 128, D], BF16).ap()
    h_d = nc.dram_tensor("h_d", [IC, 128, T], BF16).ap()
    wgq_d = nc.dram_tensor("wgq_d", [DC, 128, I], BF16).ap()
    wuq_d = nc.dram_tensor("wuq_d", [DC, 128, I], BF16).ap()
    wdq_d = nc.dram_tensor("wdq_d", [IC, 128, D], BF16).ap()
    rsc_d = nc.dram_tensor("rsc_d", [1, I], F32).ap()
    cc_kv_i = nc.dram_tensor("cc_kv_i", [128, NP * 65], F32).ap()
    cc_kv_o = nc.dram_tensor("cc_kv_o", [128, NP * 65], F32).ap()
    cc_m1_i = nc.dram_tensor("cc_m1_i", [1, 1], F32).ap()
    cc_m1_o = nc.dram_tensor("cc_m1_o", [1, 1], F32).ap()
    cc_m2_i = nc.dram_tensor("cc_m2_i", [1, 1], F32).ap()
    cc_m2_o = nc.dram_tensor("cc_m2_o", [1, 1], F32).ap()
    pmax_d = nc.dram_tensor("pmax_d", [1, 128], F32).ap()
    bc1_d = nc.dram_tensor("bc1_d", [1, 1], F32).ap()
    bc2_d = nc.dram_tensor("bc2_d", [1, 1], F32).ap()
    bc3_d = nc.dram_tensor("bc3_d", [1, 1], F32).ap()
    swd_dram = nc.dram_tensor("swd_dram", [1, D], F32).ap()

    sems = {}
    chain_cnt = {"vector": 0, "scalar": 0}

    def vchain(eng, ins):
        chain_cnt["vector"] += 1
        ins.then_inc(sems["vchain"], 1)
        eng.wait_ge(sems["vchain"], chain_cnt["vector"])

    def schain(eng, ins):
        chain_cnt["scalar"] += 1
        ins.then_inc(sems["schain"], 1)
        eng.wait_ge(sems["schain"], chain_cnt["scalar"])

    def dma(queue, stream, out, in_, deps=(), slow=False):
        def fn(eng):
            eng.dma_start(out=out, in_=in_,
                          allow_slow_non_contiguous=slow).then_inc(sems[stream], 16)
        return plan.add(queue, fn, deps, inc=16, stream=stream)

    def dmas(queue, stream, pairs, deps=()):
        def fn(eng):
            for o, i_ in pairs:
                eng.dma_start(out=o, in_=i_).then_inc(sems[stream], 16)
        return plan.add(queue, fn, deps, inc=16 * len(pairs), stream=stream)

    def step(eng, fn, deps=(), inc=1):
        return plan.add(eng, fn, deps, inc)

    psA = [nc.alloc_psum_tensor(f"psA{i}", [128, 512], F32) for i in range(2)]
    psB = [nc.alloc_psum_tensor(f"psB{i}", [128, 1024], F32) for i in range(2)]
    psC = [nc.alloc_psum_tensor(f"psC{i}", [128, 512], F32) for i in range(2)]

    ones_row = nc.alloc_sbuf_tensor("ones_row", [1, 128], BF16)
    ones512 = nc.alloc_sbuf_tensor("ones512", [1, 512], BF16)
    bq_bf = nc.alloc_sbuf_tensor("bq_bf", [1, D], BF16)
    bk_bf = nc.alloc_sbuf_tensor("bk_bf", [1, D], BF16)
    bv_bf = nc.alloc_sbuf_tensor("bv_bf", [1, D], BF16)
    bo_bf = nc.alloc_sbuf_tensor("bo_bf", [1, D], BF16)
    clipx_s = nc.alloc_sbuf_tensor("clipx_s", [1, 1], F32)
    clipd_s = nc.alloc_sbuf_tensor("clipd_s", [1, 1], F32)
    oh_sb = nc.alloc_sbuf_tensor("oh_sb", [NH, NP * 128], BF16)
    kv_sb = nc.alloc_sbuf_tensor("kv_sb", [128, NP * 65], F32)
    kvt_sb = nc.alloc_sbuf_tensor("kvt_sb", [128, NP * 65], F32)
    xmax_cols = nc.alloc_sbuf_tensor("xmax_cols", [128, NT], F32)
    hmax_cols = nc.alloc_sbuf_tensor("hmax_cols", [128, IC * NCH], F32)
    smax1 = nc.alloc_sbuf_tensor("smax1", [1, 128], F32)
    gmax1 = nc.alloc_sbuf_tensor("gmax1", [1, 1], F32)
    gmax2 = nc.alloc_sbuf_tensor("gmax2", [1, 1], F32)
    sc_small = nc.alloc_sbuf_tensor("sc_small", [1, 8], F32)
    rsx_bc = nc.alloc_sbuf_tensor("rsx_bc", [128, 1], F32)
    sx_bc = nc.alloc_sbuf_tensor("sx_bc", [128, 1], F32)
    rsh_bc = nc.alloc_sbuf_tensor("rsh_bc", [128, 1], F32)
    swg_cols = nc.alloc_sbuf_tensor("swg_cols", [128, IC], F32)
    swu_cols = nc.alloc_sbuf_tensor("swu_cols", [128, IC], F32)
    swd_cols = nc.alloc_sbuf_tensor("swd_cols", [128, HT], F32)
    svec_g = nc.alloc_sbuf_tensor("svec_g", [128, IC], F32)
    svec_u = nc.alloc_sbuf_tensor("svec_u", [128, IC], F32)
    swd_row = nc.alloc_sbuf_tensor("swd_row", [1, D], F32)
    swd_sc = nc.alloc_sbuf_tensor("swd_sc", [1, D], F32)
    ln_sc = nc.alloc_sbuf_tensor("ln_sc", [128, 16], F32)
    red_sc = nc.alloc_sbuf_tensor("red_sc", [128, 1], F32)
    ln2_sc = nc.alloc_sbuf_tensor("ln2_sc", [128, 16], F32)
    rsums = nc.alloc_sbuf_tensor("rsums", [128, 4], F32)
    ssqs = nc.alloc_sbuf_tensor("ssqs", [128, 4], F32)
    rsums2 = nc.alloc_sbuf_tensor("rsums2", [128, 4], F32)
    ssqs2 = nc.alloc_sbuf_tensor("ssqs2", [128, 4], F32)
    rcols = nc.alloc_sbuf_tensor("rcols", [128, IC], F32)

    t_bias = dmas("gpsimd", "bias", [(bq_bf[:, :], bq_in), (bk_bf[:, :], bk_in),
                                     (bv_bf[:, :], bv_in), (bo_bf[:, :], bo_in)])
    t_clip = dmas("sync", "clip", [(clipx_s[:, :], clipx_in), (clipd_s[:, :], clipd_in),
                                   (oh_sb[:, :], oh_in)])

    def f_consts(eng):
        eng.memset(ones_row[:, :], 1.0)
        eng.memset(ones512[:, :], 1.0).then_inc(sems["vector"], 1)
    t_ones = step("vector", f_consts)

    stk_a = ExitStack()

    def sb_a(name, shape, dtype):
        return stk_a.enter_context(nc.sbuf_tensor(name, shape, dtype))

    wq_bf = sb_a("wq_bf", [128, DC * D], BF16)
    wk_bf = sb_a("wk_bf", [128, DC * D], BF16)
    wv_bf = sb_a("wv_bf", [128, DC * D], BF16)
    t_wq = dmas("gpsimd", "wq", [(wq_bf[:, k * D:(k + 1) * D], wq_in[k * 128:(k + 1) * 128, :])
                                 for k in range(DC)])
    t_wk = dmas("gpsimd", "wk", [(wk_bf[:, k * D:(k + 1) * D], wk_in[k * 128:(k + 1) * 128, :])
                                 for k in range(DC)])
    t_wv = dmas("gpsimd", "wv", [(wv_bf[:, k * D:(k + 1) * D], wv_in[k * 128:(k + 1) * 128, :])
                                 for k in range(DC)])

    wrow_ = sb_a("wrow", [128, 2 * I], F32)
    wrow_sl = lambda s: wrow_[:, s * I:(s + 1) * I]
    wtc_ = sb_a("wtc", [128, 3 * I], F32)
    wtc = lambda s: wtc_[:, s * I:(s + 1) * I]
    wti8_ = sb_a("wti8", [128, 2 * I], I8)
    wti8 = lambda s: wti8_[:, s * I:(s + 1) * I]
    rs_bc = sb_a("rs_bc", [128, I], F32)
    rs2_bc = sb_a("rs2_bc", [128, 2 * D], F32)

    prep_q = []
    ps = {"rowfree0": (), "rowfree1": (), "tcfree0": (), "tcfree1": (),
          "tcfree2": (), "i8free0": (), "i8free1": (), "rcols": (),
          "rsd": (), "rsbc": (), "rsbc_readers": []}

    def mk_rows(w_in, cols, n_rows, width, toks):
        nb_max = 2 if width <= 1024 else 1
        units = []
        bidx = [0]

        def mk(rt0, nb):
            def u():
                sl = bidx[0] % 2
                bidx[0] += 1
                buf = wrow_sl(sl)
                src_ap = w_in[rt0 * 128:(rt0 + nb) * 128, :].rearrange(
                    "(a p) d -> p a d", p=128)
                t_l = dma("gpsimd", f"pwr{sl}",
                          buf[:, 0:nb * width].rearrange("p (a d) -> p a d", d=width),
                          src_ap, deps=(ps[f"rowfree{sl}"],))

                def f_red(eng, rt0=rt0, nb=nb, cols=cols, width=width, buf=buf):
                    for j in range(nb):
                        ins = eng.tensor_reduce(
                            out=cols[:, rt0 + j:rt0 + j + 1],
                            in_=buf[:, j * width:(j + 1) * width],
                            axis=AX, op=ALU.max, apply_absolute_value=True)
                        if j < nb - 1:
                            vchain(eng, ins)
                    ins.then_inc(sems["vector"], 1)
                t_r = step("vector", f_red, deps=(t_l,))
                ps[f"rowfree{sl}"] = (t_r,)
                toks.append(t_r)
            return u
        rt0 = 0
        while rt0 < n_rows:
            nb = min(nb_max, n_rows - rt0)
            units.append(mk(rt0, nb))
            rt0 += nb
        return units

    def mk_rsc(cols, n, rowtoks):
        def u():
            def f_rc(eng, cols=cols, n=n):
                eng.reciprocal(out=rcols[:, 0:n], in_=cols[:, 0:n]).then_inc(
                    sems["vector"], 1)
            t_rc = step("vector", f_rc, deps=(tuple(rowtoks), ps["rcols"]))
            ps["rcols"] = (t_rc,)
            dest = rsc_d.rearrange("a (i p) -> (a p) i", p=128)[:, 0:n]
            t_w = dma("scalar", "prsd", dest, rcols[:, 0:n],
                      deps=(t_rc, ps["rsd"]), slow=True)
            t_bc = dma("scalar", "prsb", rs_bc[:, 0:n * 128],
                       rsc_d[:, 0:n * 128].partition_broadcast(128),
                       deps=(t_w, tuple(ps["rsbc_readers"])))
            ps["rsd"] = (t_bc,)
            ps["rsbc"] = (t_bc,)
            ps["rsbc_readers"] = []
        return [u]

    def mk_fm(wt_in, width, n_chunks, dest_dram):
        units = []
        cnt = [0]
        nc2 = max(1, I // width // 2 + (0 if width == I else 1))
        nc2 = 1 if width == I else 2

        def flush(eng_unused=None):
            st = ps.pop("pend_st", None)
            if st:
                st()

        def mk(hc, nb):
            def u():
                flush()
                sl = cnt[0] % 3
                j = cnt[0] % 2
                cnt[0] += 1
                t_l = dma("gpsimd", f"ptc{sl}",
                          wtc(sl)[:, 0:nb * width].rearrange(
                              "p (a d) -> p a d", d=width),
                          wt_in[hc * 128:(hc + nb) * 128, :].rearrange(
                              "(a p) d -> p a d", p=128),
                          deps=(ps[f"tcfree{sl}"],))

                def f_q(eng, sl=sl, j=j, width=width, nb=nb):
                    eng.scalar_tensor_tensor(
                        out=wti8(j)[:, 0:nb * width], in0=wtc(sl)[:, 0:nb * width],
                        scalar=QMAX,
                        in1=rs_bc[:, 0:width] if nb == 1 else rs2_bc[:, 0:nb * width],
                        op0=ALU.mult, op1=ALU.mult).then_inc(sems["vector"], 1)
                t_q = step("vector", f_q,
                           deps=(t_l, ps["rsbc"], ps.get("rs2", ()),
                                 ps[f"i8free{j}"]))
                ps[f"tcfree{sl}"] = (t_q,)
                ps["rsbc_readers"].append(t_q)

                def emit_st(hc=hc, nb=nb, j=j, t_q=t_q, width=width):
                    t_s = dma("gpsimd", f"pst{j}",
                              dest_dram[hc:hc + nb, :, 0:width].transpose([1, 0, 2]),
                              wti8(j)[:, 0:nb * width].rearrange(
                                  "p (a d) -> p a d", d=width),
                              deps=(t_q,))
                    ps[f"i8free{j}"] = (t_s,)
                ps["pend_st"] = emit_st
            return u
        hc = 0
        while hc < n_chunks:
            nb = min(nc2, n_chunks - hc)
            units.append(mk(hc, nb))
            hc += nb
        units.append(lambda: flush())
        return units

    g_row_toks, u_row_toks, d_row_toks = [], [], []
    prep_q += mk_rows(wg_in, swg_cols, IC, D, g_row_toks)
    prep_q += mk_rows(wu_in, swu_cols, IC, D, u_row_toks)
    prep_q += mk_rows(wd_in, swd_cols, HT, I, d_row_toks)
    prep_q += mk_rsc(swg_cols, IC, g_row_toks)
    prep_q += mk_fm(wgt_in, I, DC, wgq_d)
    prep_q += mk_rsc(swu_cols, IC, u_row_toks)
    prep_q += mk_fm(wut_in, I, DC, wuq_d)
    prep_q += mk_rsc(swd_cols, HT, d_row_toks)

    def u_rs2():
        def f_cp(eng):
            vchain(eng, eng.tensor_copy(out=rs2_bc[:, 0:D], in_=rs_bc[:, 0:D]))
            eng.tensor_copy(out=rs2_bc[:, D:2 * D],
                            in_=rs_bc[:, 0:D]).then_inc(sems["vector"], 1)
        t_cp = step("vector", f_cp, deps=(ps["rsbc"],))
        ps["rsbc_readers"].append(t_cp)
        ps["rs2"] = (t_cp,)
    prep_q += [u_rs2]
    prep_q += mk_fm(wdt_in, D, IC, wdq_d)
    prep_idx = [0]

    def pump(n):
        while n > 0 and prep_idx[0] < len(prep_q):
            prep_q[prep_idx[0]]()
            prep_idx[0] += 1
            n -= 1

    x_tok_bf_ = sb_a("x_tok_bf", [128, 2 * D], BF16)
    x_tok_bf = lambda s: x_tok_bf_[:, s * D:(s + 1) * D]

    xfm_ = sb_a("xfm", [128, 2 * DC * 512], BF16)
    q_e = sb_a("q_e", [128, 512], BF16)
    q_m = sb_a("q_m", [128, 512], BF16)
    q_out_ = sb_a("q_out", [128, DC * 512], BF16)
    k_e = sb_a("k_e", [128, 1024], BF16)
    k_m = sb_a("k_m", [128, 1024], BF16)
    k_bf_ = sb_a("k_bf", [128, 4 * 1024], BF16)
    k_bf = lambda s: k_bf_[:, s * 1024:(s + 1) * 1024]
    v_aug_ = sb_a("v_aug", [128, 4 * NH * 65], BF16)
    v_aug = lambda s: v_aug_[:, s * NH * 65:(s + 1) * NH * 65]

    def xfm_ap(sl, c):
        return xfm_[:, (sl * DC + c) * 512:(sl * DC + c + 1) * 512]

    def f_vones(eng):
        for tt in range(4):
            va = v_aug(tt).rearrange("p (h c) -> p h c", c=65)
            ins = eng.memset(va[:, :, 64:65], 1.0)
        ins.then_inc(sems["vector"], 1)
    t_vones = step("vector", f_vones)

    xtok_free = [(), ()]
    xf_free = ()
    xfm_free = [(), ()]
    tr_tok = [[None] * 4, [None] * 4]
    ps_q_free = [(), ()]
    ps_k_free = ()
    ps_v_free = ()
    ps_kv_free = [(), ()]
    q_e_free = ()
    q_out_free = ()
    k_e_free = ()
    kslot_free = [()] * 4
    kv_sb_tok = ()
    last_kv_mm = ()

    for ch in range(NCH):
        sl = ch % 2
        for tt in range(4):
            t_idx = ch * 4 + tt
            tsl = t_idx % 2
            t_xld = dma("gpsimd", f"xt{tsl}", x_tok_bf(tsl),
                        x_in[t_idx * 128:(t_idx + 1) * 128, :],
                        deps=(xtok_free[tsl],))

            def f_tr(eng, sl=sl, tt=tt, tsl=tsl):
                for c in range(DC):
                    eng.dma_start(out=xfm_ap(sl, c)[:, tt * 128:(tt + 1) * 128],
                                  in_=x_tok_bf(tsl)[:, c * 128:(c + 1) * 128],
                                  transpose=True).then_inc(sems[f"tr{tsl}"], 16)
            t_tr = plan.add("sync", f_tr, (t_xld, xfm_free[sl]), inc=16 * DC,
                            stream=f"tr{tsl}")
            tr_tok[sl][tt] = t_tr
            xtok_free[tsl] = (t_tr,)
        ch_tr = tuple(tr_tok[sl])

        q_mm_toks = []
        qvec_chunk_toks = []
        for m in range(DC):
            bank = m % 2

            def f_qmm(eng, m=m, bank=bank, sl=sl):
                for k in range(DC):
                    eng.matmul(psA[bank][:, :],
                               wq_bf[:, k * D + m * 128: k * D + (m + 1) * 128],
                               xfm_ap(sl, k),
                               start=(k == 0), stop=False)
                eng.matmul(psA[bank][:, :],
                           bq_bf[:, m * 128:(m + 1) * 128],
                           ones512[:, :],
                           start=False, stop=True).then_inc(sems["tensor"], 1)
            t_qmm = step("tensor", f_qmm,
                         deps=(t_wq, t_bias, t_ones, ps_q_free[bank], ch_tr))
            q_mm_toks.append(t_qmm)

            def f_qexp(eng, bank=bank):
                eng.activation(out=q_e[:, :], in_=psA[bank][:, :],
                               func=AF.Exp).then_inc(sems["scalar"], 1)
            t_qexp = step("scalar", f_qexp, deps=(t_qmm, q_e_free))

            def f_qvec(eng, bank=bank, m=m):
                vchain(eng, eng.tensor_scalar(out=q_m[:, :], in0=q_e[:, :],
                                              scalar1=1.0, scalar2=None, op0=ALU.min))
                eng.scalar_tensor_tensor(
                    out=q_out_[:, m * 512:(m + 1) * 512], in0=psA[bank][:, :],
                    scalar=0.0, in1=q_m[:, :], op0=ALU.max,
                    op1=ALU.add).then_inc(sems["vector"], 1)
            t_qvec = step("vector", f_qvec, deps=(t_qexp, q_out_free))
            ps_q_free[bank] = (t_qvec,)
            q_e_free = (t_qvec,)
            qvec_chunk_toks.append(t_qvec)
            pump(1)

        t_qst = dma("scalar", "qst",
                    q_d[:, :, ch * 512:(ch + 1) * 512].transpose([1, 0, 2]),
                    q_out_[:, :].rearrange("p (a d) -> p a d", d=512),
                    deps=(tuple(qvec_chunk_toks),))
        q_out_free = (t_qst,)

        kvec_toks = []
        for tt in range(4):
            def f_kvmm(eng, sl=sl, tt=tt):
                for k in range(DC):
                    lhs = xfm_ap(sl, k)[:, tt * 128:(tt + 1) * 128]
                    for half in range(2):
                        eng.matmul(psB[0][:, half * 512:(half + 1) * 512], lhs,
                                   wk_bf[:, k * D + half * 512: k * D + (half + 1) * 512],
                                   start=(k == 0), stop=False)
                        eng.matmul(psB[1][:, half * 512:(half + 1) * 512], lhs,
                                   wv_bf[:, k * D + half * 512: k * D + (half + 1) * 512],
                                   start=(k == 0), stop=False)
                for half in range(2):
                    eng.matmul(psB[0][:, half * 512:(half + 1) * 512], ones_row[:, :],
                               bk_bf[:, half * 512:(half + 1) * 512],
                               start=False, stop=True)
                    ins = eng.matmul(psB[1][:, half * 512:(half + 1) * 512], ones_row[:, :],
                                     bv_bf[:, half * 512:(half + 1) * 512],
                                     start=False, stop=True)
                ins.then_inc(sems["tensor"], 1)
            t_kvmm = step("tensor", f_kvmm,
                          deps=(t_wk, t_wv, t_bias, t_ones, ps_k_free, ps_v_free,
                                kslot_free[tt], ch_tr))

            def f_kexp(eng):
                eng.activation(out=k_e[:, :], in_=psB[0][:, :],
                               func=AF.Exp).then_inc(sems["scalar"], 1)
            t_kexp = step("scalar", f_kexp, deps=(t_kvmm, k_e_free))

            def f_kvec(eng, tt=tt):
                vchain(eng, eng.tensor_scalar(out=k_m[:, :], in0=k_e[:, :],
                                              scalar1=1.0, scalar2=None, op0=ALU.min))
                eng.scalar_tensor_tensor(
                    out=k_bf(tt), in0=psB[0][:, :], scalar=0.0,
                    in1=k_m[:, :], op0=ALU.max, op1=ALU.add)
                va = v_aug(tt).rearrange("p (h c) -> p h c", c=65)
                pv = psB[1][:, :].rearrange("p (h c) -> p h c", c=64)
                eng.tensor_copy(out=va[:, :, 0:64], in_=pv[:, :, :]).then_inc(
                    sems["vector"], 1)
            t_kvec = step("vector", f_kvec, deps=(t_kexp, t_vones, kslot_free[tt]))
            ps_k_free = (t_kvec,)
            ps_v_free = (t_kvec,)
            k_e_free = (t_kvec,)
            kvec_toks.append(t_kvec)
            pump(1)

        for p in range(NP):
            bank = p % 2

            def f_kvps(eng, p=p, bank=bank):
                hA, hB = 2 * p, 2 * p + 1
                for tt in range(4):
                    eng.matmul(psC[bank][0:64, 0:65],
                               k_bf(tt)[:, hA * 64:(hA + 1) * 64],
                               v_aug(tt)[:, hA * 65:(hA + 1) * 65],
                               start=(tt == 0), stop=(tt == 3),
                               tile_position=(0, 0))
                for tt in range(4):
                    ins = eng.matmul(psC[bank][64:128, 0:65],
                                     k_bf(tt)[:, hB * 64:(hB + 1) * 64],
                                     v_aug(tt)[:, hB * 65:(hB + 1) * 65],
                                     start=(tt == 0), stop=(tt == 3),
                                     tile_position=(0, 64))
                ins.then_inc(sems["tensor"], 1)
            t_kvps = step("tensor", f_kvps, deps=(tuple(kvec_toks), ps_kv_free[bank]))
            last_kv_mm = (t_kvps,)

            def f_kvacc(eng, p=p, bank=bank, first=(ch == 0)):
                dst = kv_sb[:, p * 65:(p + 1) * 65]
                if first:
                    eng.tensor_copy(out=dst, in_=psC[bank][:, 0:65]).then_inc(
                        sems["vector"], 1)
                else:
                    eng.tensor_tensor(out=dst, in0=dst, in1=psC[bank][:, 0:65],
                                      op=ALU.add).then_inc(sems["vector"], 1)
            t_kvacc = step("vector", f_kvacc, deps=(t_kvps, kv_sb_tok))
            kv_sb_tok = (t_kvacc,)
            ps_kv_free[bank] = (t_kvacc,)

        for tt in range(4):
            kslot_free[tt] = last_kv_mm
        xfm_free[sl] = (q_mm_toks[-1], t_kvmm)

        pump(2)

    pump(len(prep_q))

    t_kv_out = dma("gpsimd", "cc", cc_kv_i, kv_sb[:, :], deps=(kv_sb_tok,))

    def f_cc_kv(eng):
        eng.collective_compute(
            "AllReduce", ALU.add,
            replica_groups=[[0, 1], [2, 3], [4, 5], [6, 7]],
            ins=[cc_kv_i.opt()], outs=[cc_kv_o.opt()]).then_inc(sems["gpsimd"], 1)
    t_cc_kv = step("gpsimd", f_cc_kv, deps=(t_kv_out,), inc=1)
    t_kv_in = dma("gpsimd", "cc", kvt_sb[:, :], cc_kv_o, deps=(t_cc_kv,))

    bar2 = plan.barrier()
    stk_a.close()

    stk_c = ExitStack()
    xq_fm = stk_c.enter_context(nc.sbuf_tensor("xq_fm", [128, DC * T], BF16))

    def xq_ap(c):
        return xq_fm[:, c * T:(c + 1) * T]

    stk_b = ExitStack()

    def sb_b(name, shape, dtype):
        return stk_b.enter_context(nc.sbuf_tensor(name, shape, dtype))

    wo_bf = sb_b("wo_bf", [128, DC * D], BF16)
    blockdiag = sb_b("blockdiag", [128, NP * 128], BF16)
    normlhs = sb_b("normlhs", [128, DC * NH], BF16)
    q_ch_ = sb_b("q_ch", [128, 2 * DC * 512], BF16)
    q_ch = lambda s, c: q_ch_[:, (s * DC + c) * 512:(s * DC + c + 1) * 512]
    mix_bf_ = sb_b("mix_bf", [128, 2 * DC * 512], BF16)
    mix_bf = lambda s: mix_bf_[:, s * DC * 512:(s + 1) * DC * 512]
    norm_sb = sb_b("norm_sb", [16, 512], F32)
    norm_sb2 = sb_b("norm_sb2", [16, 512], F32)
    rexp_sb_ = sb_b("rexp_sb", [128, 2 * 512], F32)
    rexp_sb = lambda s: rexp_sb_[:, s * 512:(s + 1) * 512]
    recip_bf = sb_b("recip_bf", [16, 512], BF16)
    x0_tile_ = sb_b("x0_tile", [128, 4 * D], F32)
    x0_tile = lambda s: x0_tile_[:, s * D:(s + 1) * D]
    r1_st_ = sb_b("r1_st", [128, 4 * D], F32)
    r1_st = lambda s: r1_st_[:, s * D:(s + 1) * D]
    x1_bf_ = sb_b("x1_bf", [128, 4 * D], BF16)
    x1_bf = lambda s: x1_bf_[:, s * D:(s + 1) * D]
    sq_junk = sb_b("sq_junk", [128, D], F32)

    t_wo = dmas("gpsimd", "wo", [(wo_bf[:, k * D:(k + 1) * D], wo_in[k * 128:(k + 1) * 128, :])
                                 for k in range(DC)], deps=(bar2,))

    def f_blockdiag(eng):
        vchain(eng, eng.memset(blockdiag[:, :], 0.0))
        vchain(eng, eng.memset(normlhs[:, :], 0.0))
        for p in range(NP):
            eng.tensor_copy(out=blockdiag[0:64, p * 128: p * 128 + 64],
                            in_=kvt_sb[0:64, p * 65: p * 65 + 64])
            eng.tensor_copy(out=blockdiag[64:128, p * 128 + 64: p * 128 + 128],
                            in_=kvt_sb[64:128, p * 65: p * 65 + 64])
            eng.tensor_copy(out=normlhs[0:64, p * NH + 2 * p: p * NH + 2 * p + 1],
                            in_=kvt_sb[0:64, p * 65 + 64: p * 65 + 65])
            ins = eng.tensor_copy(out=normlhs[64:128, p * NH + 2 * p + 1: p * NH + 2 * p + 2],
                                  in_=kvt_sb[64:128, p * 65 + 64: p * 65 + 65])
        ins.then_inc(sems["vector"], 1)
    t_bd = step("vector", f_blockdiag, deps=(t_kv_in, bar2))

    q_slot_free = [(), ()]
    mix_slot_free = [(), ()]
    ps_norm_free = ()
    ps_mix_free = [(), ()]
    ps_rexp_free = [(), ()]
    rexp_sb_free = [(), ()]
    ps_wo_free = [(), ()]
    psB0_free = ()
    recip_free = ()
    x0_free = [bar2, bar2, bar2, bar2]
    x1_free = [(), (), (), ()]
    stats_tok = ()
    r1_free = [()] * 4
    sqj_tok = ()
    x1_store_toks = []
    x1tr_toks = []
    xmax_toks = []
    pend_tail_c = None
    pend_tail2_c = None

    def emit_tail1_c(wo_toks, sq_toks, ch):
        nonlocal stats_tok

        def f_stats(eng):
            negm = ln_sc[:, 12:16]
            v = ln_sc[:, 4:8]
            vchain(eng, eng.tensor_scalar(out=negm, in0=rsums[:, 0:4], scalar1=-1.0 / D,
                                          scalar2=None, op0=ALU.mult))
            vchain(eng, eng.tensor_tensor(out=v, in0=negm, in1=negm, op=ALU.mult))
            vchain(eng, eng.scalar_tensor_tensor(out=v, in0=ssqs[:, 0:4], scalar=1.0 / D,
                                                 in1=v, op0=ALU.mult, op1=ALU.subtract))
            vchain(eng, eng.tensor_scalar(out=v, in0=v, scalar1=EPS_LN, scalar2=None,
                                          op0=ALU.add))
            eng.reciprocal(out=ln_sc[:, 8:12], in_=v).then_inc(sems["vector"], 1)
        t_stats = step("vector", f_stats, deps=(tuple(wo_toks), tuple(sq_toks)))

        def f_rstd(eng):
            eng.activation(out=ln_sc[:, 8:12], in_=ln_sc[:, 8:12],
                           func=AF.Sqrt).then_inc(sems["scalar"], 1)
        t_rstd = step("scalar", f_rstd, deps=(t_stats,))
        stats_tok = (t_rstd,)

        affs = []
        for tt in range(4):
            xsl = tt

            def f_aff(eng, tt=tt, xsl=xsl):
                eng.tensor_scalar(out=x1_bf(xsl), in0=r1_st(tt),
                                  scalar1=ln_sc[:, 12 + tt:13 + tt],
                                  scalar2=ln_sc[:, 8 + tt:9 + tt],
                                  op0=ALU.add, op1=ALU.mult).then_inc(sems["vector"], 1)
            t_aff = step("vector", f_aff, deps=((t_rstd,), x1_free[xsl]))
            r1_free[tt] = (t_aff,)
            affs.append(t_aff)
        return affs

    def emit_tail2_unit(ch, tt, t_aff):
        t_idx = ch * 4 + tt
        xsl = tt

        def f_xmax(eng, t_idx=t_idx, xsl=xsl):
            eng.tensor_reduce(out=xmax_cols[:, t_idx:t_idx + 1],
                              in_=x1_bf(xsl), axis=AX, op=ALU.max,
                              apply_absolute_value=True).then_inc(sems["vector"], 1)
        t_xmax = step("vector", f_xmax, deps=(t_aff,))
        xmax_toks.append(t_xmax)
        t_x1st = dma("gpsimd", f"x1st{xsl}", x1_d[t_idx, :, :], x1_bf(xsl),
                     deps=(t_aff,))

        def f_x1tr(eng, t_idx=t_idx, xsl=xsl):
            for c in range(DC):
                eng.dma_start(out=xq_ap(c)[:, t_idx * 128:(t_idx + 1) * 128],
                              in_=x1_bf(xsl)[:, c * 128:(c + 1) * 128],
                              transpose=True).then_inc(sems[f"x1tr{xsl}"], 16)
        t_x1tr = plan.add("sync", f_x1tr, (t_aff, bar2), inc=16 * DC,
                          stream=f"x1tr{xsl}")
        x1_free[xsl] = (t_x1st, t_xmax, t_x1tr)
        x1_store_toks.append(t_x1st)
        x1tr_toks.append(t_x1tr)

    for ch in range(NCH):
        sl = ch % 2
        t_qld = dmas("gpsimd", f"qch{sl}",
                     [(q_ch(sl, c), q_d[c, :, ch * 512:(ch + 1) * 512]) for c in range(DC)],
                     deps=(bar2, q_slot_free[sl]))

        def f_nmm(eng, sl=sl):
            for c in range(DC):
                ins = eng.matmul(psA[0][0:16, :],
                                 normlhs[:, c * NH:(c + 1) * NH],
                                 q_ch(sl, c),
                                 start=(c == 0), stop=(c == DC - 1))
            ins.then_inc(sems["tensor"], 1)
        t_nmm = step("tensor", f_nmm, deps=(t_qld, t_bd, ps_norm_free))

        def f_nrm(eng):
            vchain(eng, eng.tensor_scalar(out=norm_sb[:, :], in0=psA[0][0:16, :],
                                          scalar1=EPS_ATTN, scalar2=None, op0=ALU.max))
            vchain(eng, eng.reciprocal(out=norm_sb2[:, :], in_=norm_sb[:, :]))
            eng.tensor_copy(out=recip_bf[:, :], in_=norm_sb2[:, :]).then_inc(
                sems["vector"], 1)
        t_nrm = step("vector", f_nrm, deps=(t_nmm, recip_free))
        ps_norm_free = (t_nrm,)

        mix_toks = []
        rexpcp_toks = []
        last_mixmm = None
        for p in range(NP):
            bank = p % 2

            def f_mixmm(eng, p=p, bank=bank, sl=sl):
                eng.matmul(psC[bank][:, :],
                           blockdiag[:, p * 128:(p + 1) * 128],
                           q_ch(sl, p),
                           start=True, stop=True)
                eng.matmul(psB[0][:, bank * 512:(bank + 1) * 512],
                           oh_sb[:, p * 128:(p + 1) * 128],
                           recip_bf[:, :],
                           start=True, stop=True).then_inc(sems["tensor"], 1)
            t_mixmm = step("tensor", f_mixmm,
                           deps=(t_nrm, ps_mix_free[bank], ps_rexp_free[bank],
                                 mix_slot_free[sl], t_clip, psB0_free))
            last_mixmm = t_mixmm

            def f_rexpcp(eng, bank=bank):
                eng.copy(out=rexp_sb(bank),
                         in_=psB[0][:, bank * 512:(bank + 1) * 512]).then_inc(
                    sems["scalar"], 1)
            t_rexpcp = step("scalar", f_rexpcp,
                            deps=(t_mixmm, ps_rexp_free[bank], rexp_sb_free[bank]))
            ps_rexp_free[bank] = (t_rexpcp,)
            rexpcp_toks.append(t_rexpcp)

            def f_div(eng, p=p, bank=bank, sl=sl):
                eng.tensor_tensor(out=mix_bf(sl)[:, p * 512:(p + 1) * 512],
                                  in0=psC[bank][:, :],
                                  in1=rexp_sb(bank),
                                  op=ALU.mult).then_inc(sems["vector"], 1)
            t_div = step("vector", f_div, deps=(t_mixmm, t_rexpcp))
            ps_mix_free[bank] = (t_div,)
            rexp_sb_free[bank] = (t_div,)
            mix_toks.append(t_div)
        recip_free = (last_mixmm,)
        q_slot_free[sl] = (last_mixmm, t_nmm)

        x0_toks = []
        for tt in range(4):
            t_idx = ch * 4 + tt
            t_x0 = dma("gpsimd", f"x0{tt}", x0_tile(tt),
                       x_in[t_idx * 128:(t_idx + 1) * 128, :], deps=(x0_free[tt],))
            x0_toks.append(t_x0)

        if pend_tail_c is not None:
            pend_tail2_c = (pend_tail_c[2], emit_tail1_c(*pend_tail_c))
            pend_tail_c = None

        wo_toks = []
        sq_toks = []
        last_womm = None
        for tt in range(4):
            wbank = tt % 2
            wps = psB[1] if wbank == 0 else psB[0]

            def f_womm(eng, tt=tt, sl=sl, wps=wps):
                for c in range(DC):
                    for half in range(2):
                        eng.matmul(wps[:, half * 512:(half + 1) * 512],
                                   mix_bf(sl)[:, c * 512 + tt * 128: c * 512 + (tt + 1) * 128],
                                   wo_bf[:, c * D + half * 512: c * D + (half + 1) * 512],
                                   start=(c == 0), stop=False)
                for half in range(2):
                    ins = eng.matmul(wps[:, half * 512:(half + 1) * 512],
                                     ones_row[:, :],
                                     bo_bf[:, half * 512:(half + 1) * 512],
                                     start=False, stop=True)
                ins.then_inc(sems["tensor"], 1)
            extra = tuple(rexpcp_toks) if wbank == 1 else ()
            t_womm = step("tensor", f_womm,
                          deps=(tuple(mix_toks), ps_wo_free[wbank], t_wo, extra))
            last_womm = t_womm

            def f_r1(eng, tt=tt, wps=wps):
                eng.scalar_tensor_tensor(
                    out=r1_st(tt), in0=wps[:, :], scalar=1.0,
                    in1=x0_tile(tt), op0=ALU.mult, op1=ALU.add,
                    accum_out=rsums[:, tt:tt + 1]).then_inc(sems["vector"], 1)
            t_r1 = step("vector", f_r1,
                        deps=(t_womm, x0_toks[tt], stats_tok, r1_free[tt]))
            ps_wo_free[wbank] = (t_r1,)
            x0_free[tt] = (t_r1,)
            wo_toks.append(t_r1)

            def f_sq(eng, tt=tt):
                eng.scalar_tensor_tensor(
                    out=sq_junk[:, :], in0=r1_st(tt), scalar=1.0, in1=r1_st(tt),
                    op0=ALU.mult, op1=ALU.mult,
                    accum_out=ssqs[:, tt:tt + 1]).then_inc(sems["vector"], 1)
            t_sq = step("vector", f_sq, deps=(t_r1, sqj_tok))
            sqj_tok = (t_sq,)
            sq_toks.append(t_sq)

            if pend_tail2_c is not None:
                emit_tail2_unit(pend_tail2_c[0], tt, pend_tail2_c[1][tt])

        pend_tail2_c = None
        psB0_free = (wo_toks[3],)
        mix_slot_free[sl] = (last_womm,)
        pend_tail_c = (wo_toks, sq_toks, ch)

    last_affs = emit_tail1_c(*pend_tail_c)
    pend_tail_c = None
    for tt in range(4):
        emit_tail2_unit(NCH - 1, tt, last_affs[tt])

    bar3 = plan.barrier()
    stk_b.close()

    def f_xred(eng):
        eng.tensor_reduce(out=red_sc[:, :], in_=xmax_cols[:, :], axis=AX,
                          op=ALU.max).then_inc(sems["vector"], 1)
    t_xred = step("vector", f_xred, deps=(tuple(xmax_toks),))
    t_pm1 = dma("scalar", "pm", pmax_d, red_sc[:, :], deps=(t_xred,))
    t_pm1b = dma("scalar", "pm", smax1[:, :], pmax_d, deps=(t_pm1,))

    def f_xred2(eng):
        eng.tensor_reduce(out=gmax1[:, :], in_=smax1[:, :], axis=AX,
                          op=ALU.max).then_inc(sems["vector"], 1)
    t_xred2 = step("vector", f_xred2, deps=(t_pm1b,))
    t_m1o = dma("gpsimd", "cc", cc_m1_i, gmax1[:, :], deps=(t_xred2,))

    def f_cc_m1(eng):
        eng.collective_compute(
            "AllReduce", ALU.max, replica_groups=[list(range(NCORES))],
            ins=[cc_m1_i.opt()], outs=[cc_m1_o.opt()]).then_inc(sems["gpsimd"], 1)
    t_cc_m1 = step("gpsimd", f_cc_m1, deps=(t_m1o,), inc=1)
    t_m1i = dma("gpsimd", "cc", gmax1[:, :], cc_m1_o, deps=(t_cc_m1,))

    def f_sigx(eng):
        eng.activation(out=sc_small[:, 0:1], in_=clipx_s[:, :],
                       func=AF.Sigmoid).then_inc(sems["scalar"], 1)
    t_sigx = step("scalar", f_sigx, deps=(t_clip, bar3))

    def f_sx(eng):
        vchain(eng, eng.tensor_tensor(out=sc_small[:, 1:2], in0=sc_small[:, 0:1],
                                      in1=gmax1[:, :], op=ALU.mult))
        vchain(eng, eng.tensor_scalar(out=sc_small[:, 1:2], in0=sc_small[:, 1:2],
                                      scalar1=1.0 / QMAX, scalar2=None, op0=ALU.mult))
        eng.reciprocal(out=sc_small[:, 2:3], in_=sc_small[:, 1:2]).then_inc(
            sems["vector"], 1)
    t_sx = step("vector", f_sx, deps=(t_sigx, t_m1i))
    t_b1 = dma("scalar", "bc1", bc1_d, sc_small[:, 1:2], deps=(t_sx,))
    t_b2 = dma("scalar", "bc2", bc2_d, sc_small[:, 2:3], deps=(t_sx,))
    t_sxbc = dma("scalar", "bc1", sx_bc[:, :], bc1_d.partition_broadcast(128),
                 deps=(t_b1,))
    t_rsxbc = dma("scalar", "bc2", rsx_bc[:, :], bc2_d.partition_broadcast(128),
                  deps=(t_b2,))

    def f_svec(eng):
        vchain(eng, eng.tensor_scalar(out=svec_g[:, :], in0=swg_cols[:, :],
                                      scalar1=sx_bc[:, :], scalar2=1.0 / QMAX,
                                      op0=ALU.mult, op1=ALU.mult))
        eng.tensor_scalar(out=svec_u[:, :], in0=swu_cols[:, :],
                          scalar1=sx_bc[:, :], scalar2=1.0 / QMAX,
                          op0=ALU.mult, op1=ALU.mult).then_inc(sems["vector"], 1)
    t_svec = step("vector", f_svec, deps=(t_sxbc, tuple(g_row_toks), tuple(u_row_toks)))

    stk_c1 = ExitStack()

    def sb_c1(name, shape, dtype):
        return stk_c1.enter_context(nc.sbuf_tensor(name, shape, dtype))

    xqi8 = sb_c1("xqi8", [128, 512], I8)
    wfm_ = sb_c1("wfm", [128, 2 * 2 * D], BF16)
    wfm = lambda s, gu: wfm_[:, (s * 2 + gu) * D:(s * 2 + gu + 1) * D]
    g_act = sb_c1("g_act", [128, 512], F32)
    h_bf_ = sb_c1("h_bf", [128, 2 * 512], BF16)
    h_bf = lambda s: h_bf_[:, s * 512:(s + 1) * 512]

    g_banks = [psA[0][:, :], psA[1][:, :], psB[0][:, 0:512], psB[0][:, 512:1024]]
    u_banks = [psC[0][:, :], psC[1][:, :], psB[1][:, 0:512], psB[1][:, 512:1024]]
    quant_toks = []
    for ch in range(NCH):
        ch_tr4 = tuple(x1tr_toks[ch * 4 + k] for k in range(4))

        def f_xq(eng, ch=ch):
            for hc in range(DC):
                strip = xq_ap(hc)[:, ch * 512:(ch + 1) * 512]
                vchain(eng, eng.tensor_scalar(out=xqi8[:, :], in0=strip,
                                              scalar1=rsx_bc[:, :], scalar2=None,
                                              op0=ALU.mult))
                ins = eng.tensor_copy(out=strip, in_=xqi8[:, :])
                if hc < DC - 1:
                    vchain(eng, ins)
            ins.then_inc(sems["vector"], 1)
        prev_q = quant_toks[-1] if quant_toks else ()
        t_xq = step("vector", f_xq, deps=(t_rsxbc, ch_tr4, bar3, prev_q))
        quant_toks.append(t_xq)

    def wload(i, wsl, deps):
        src_g = wgq_d[:, :, i * 128:(i + 1) * 128].transpose([1, 0, 2])
        src_u = wuq_d[:, :, i * 128:(i + 1) * 128].transpose([1, 0, 2])
        return dmas("sync", f"wl{wsl}",
                    [(wfm(wsl, 0).rearrange("p (c j) -> p c j", j=128), src_g),
                     (wfm(wsl, 1).rearrange("p (c j) -> p c j", j=128), src_u)],
                    deps=deps)

    wfm_free = [(), ()]
    ps_g_free = [(), (), (), ()]
    ps_u_free = [(), (), (), ()]
    g_act_free = ()
    h_slot_free = [(), ()]
    wl_toks = [None] * IC
    h_store_toks = []
    hmax_toks = []
    last_gumm = ()

    wl_toks[0] = wload(0, 0, deps=(bar3,))
    if IC > 1:
        wl_toks[1] = wload(1, 1, deps=(bar3,))

    for i in range(IC):
        wsl = i % 2
        t_wl = wl_toks[i]

        gumm_toks = []
        for ch in range(NCH):
            bank = ch % 4

            def f_gumm(eng, wsl=wsl, ch=ch, bank=bank):
                for hc in range(DC):
                    eng.matmul(g_banks[bank],
                               wfm(wsl, 0)[:, hc * 128:(hc + 1) * 128],
                               xq_ap(hc)[:, ch * 512:(ch + 1) * 512],
                               start=(hc == 0), stop=(hc == DC - 1))
                for hc in range(DC):
                    ins = eng.matmul(u_banks[bank],
                                     wfm(wsl, 1)[:, hc * 128:(hc + 1) * 128],
                                     xq_ap(hc)[:, ch * 512:(ch + 1) * 512],
                                     start=(hc == 0), stop=(hc == DC - 1))
                ins.then_inc(sems["tensor"], 1)
            t_gumm = step("tensor", f_gumm,
                          deps=(t_wl, quant_toks[ch], ps_g_free[bank], ps_u_free[bank]))
            gumm_toks.append(t_gumm)

            def f_gact(eng, i=i, bank=bank):
                eng.activation(out=g_act[:, :], in_=g_banks[bank], func=AF.Silu,
                               scale=svec_g[:, i:i + 1]).then_inc(sems["scalar"], 1)
            t_gact = step("scalar", f_gact, deps=(t_gumm, t_svec, g_act_free))

            hsl = ch % 2

            def f_h(eng, i=i, ch=ch, hsl=hsl, bank=bank):
                vchain(eng, eng.scalar_tensor_tensor(
                    out=h_bf(hsl), in0=u_banks[bank], scalar=svec_u[:, i:i + 1],
                    in1=g_act[:, :], op0=ALU.mult, op1=ALU.mult))
                eng.tensor_reduce(out=hmax_cols[:, i * NCH + ch:i * NCH + ch + 1],
                                  in_=h_bf(hsl), axis=AX, op=ALU.max,
                                  apply_absolute_value=True).then_inc(sems["vector"], 1)
            t_h = step("vector", f_h, deps=(t_gact, h_slot_free[hsl], t_svec))
            hmax_toks.append(t_h)
            ps_g_free[bank] = (t_h,)
            ps_u_free[bank] = (t_h,)
            g_act_free = (t_h,)
            t_hst = dma("gpsimd", f"hst{hsl}", h_d[i, :, ch * 512:(ch + 1) * 512],
                        h_bf(hsl), deps=(t_h,))
            h_slot_free[hsl] = (t_hst,)
            h_store_toks.append(t_hst)

        last_gumm = (gumm_toks[-1],)
        if DEBUG_DUMPS and i == 20:
            dmas("gpsimd", "dbgw", [(dbg_wfm[:, 0:D], wfm(wsl, 0)),
                                    (dbg_wfm[:, D:2 * D], wfm(wsl, 1))],
                 deps=(gumm_toks[-1],))
        if i + 2 < IC:
            wfm_free[wsl] = (gumm_toks[-1],)
            wl_toks[i + 2] = wload(i + 2, wsl, deps=(gumm_toks[-1],))

    if DEBUG_DUMPS:
        dmas("gpsimd", "dbgd",
             [(dbg_xq[c, :, :], xq_ap(c)) for c in range(DC)],
             deps=(last_gumm, tuple(quant_toks)))

    t_swdr = dmas("scalar", "swd", [(swd_dram[:, ht * 128:(ht + 1) * 128],
                                     swd_cols[:, ht:ht + 1]) for ht in range(HT)],
                  deps=(tuple(d_row_toks),))
    t_swdl = dma("scalar", "swd", swd_row[:, :], swd_dram, deps=(t_swdr,))

    d_free = (last_gumm, g_act_free, h_slot_free[0], h_slot_free[1],
              tuple(h_store_toks))
    bar4 = plan.barrier()
    stk_c1.close()
    stk_c.close()

    stk_d = ExitStack()

    def sb_d(name, shape, dtype):
        return stk_d.enter_context(nc.sbuf_tensor(name, shape, dtype))

    h_ld_ = sb_d("h_ld", [128, 2 * IC * 512], BF16)
    h_ld = lambda s, ic: h_ld_[:, (s * IC + ic) * 512:(s * IC + ic + 1) * 512]
    hq_i8_ = sb_d("hq_i8", [128, 2 * 1024], I8)
    hq_i8 = lambda s: hq_i8_[:, s * 1024:(s + 1) * 1024]
    x1f_ = sb_d("x1f", [128, 2 * D], BF16)
    x1f = lambda s: x1f_[:, s * D:(s + 1) * D]
    r2_st_ = sb_d("r2_st", [128, 4 * D], F32)
    r2_st = lambda s: r2_st_[:, s * D:(s + 1) * D]
    t2_st = sb_d("t2_st", [128, D], F32)
    out_t_ = sb_d("out_t", [128, 4 * D], F32)
    out_t = lambda s: out_t_[:, s * D:(s + 1) * D]
    sq_junk2 = sb_d("sq_junk2", [128, D], F32)
    swd_bc = sb_d("swd_bc", [128, D], F32)
    wd_fm = sb_d("wd_fm", [128, IC * D], BF16)

    def wdfm_ap(ic):
        return wd_fm[:, ic * D:(ic + 1) * D]

    t_wdfm = dmas("sync", "wdfm", [(wdfm_ap(ic), wdq_d[ic, :, :]) for ic in range(IC)],
                  deps=(d_free,))

    def f_hred(eng):
        eng.tensor_reduce(out=red_sc[:, :], in_=hmax_cols[:, :], axis=AX,
                          op=ALU.max).then_inc(sems["vector"], 1)
    t_hred = step("vector", f_hred, deps=(tuple(hmax_toks),))
    t_pm2 = dma("scalar", "pm", pmax_d, red_sc[:, :], deps=(t_hred, t_pm1b))
    t_pm2b = dma("scalar", "pm", smax1[:, :], pmax_d, deps=(t_pm2, t_xred2))

    def f_hred2(eng):
        eng.tensor_reduce(out=gmax2[:, :], in_=smax1[:, :], axis=AX,
                          op=ALU.max).then_inc(sems["vector"], 1)
    t_hred2 = step("vector", f_hred2, deps=(t_pm2b,))
    t_m2o = dma("gpsimd", "cc", cc_m2_i, gmax2[:, :], deps=(t_hred2,))

    def f_cc_m2(eng):
        eng.collective_compute(
            "AllReduce", ALU.max, replica_groups=[list(range(NCORES))],
            ins=[cc_m2_i.opt()], outs=[cc_m2_o.opt()]).then_inc(sems["gpsimd"], 1)
    t_cc_m2 = step("gpsimd", f_cc_m2, deps=(t_m2o,), inc=1)
    t_m2i = dma("gpsimd", "cc", gmax2[:, :], cc_m2_o, deps=(t_cc_m2,))

    def f_sigd(eng):
        eng.activation(out=sc_small[:, 3:4], in_=clipd_s[:, :],
                       func=AF.Sigmoid).then_inc(sems["scalar"], 1)
    t_sigd = step("scalar", f_sigd, deps=(t_clip, bar4))

    def f_sh(eng):
        vchain(eng, eng.tensor_tensor(out=sc_small[:, 4:5], in0=sc_small[:, 3:4],
                                      in1=gmax2[:, :], op=ALU.mult))
        vchain(eng, eng.tensor_scalar(out=sc_small[:, 4:5], in0=sc_small[:, 4:5],
                                      scalar1=1.0 / QMAX, scalar2=None, op0=ALU.mult))
        vchain(eng, eng.reciprocal(out=sc_small[:, 5:6], in_=sc_small[:, 4:5]))
        eng.tensor_scalar(out=sc_small[:, 7:8], in0=sc_small[:, 4:5],
                          scalar1=1.0 / QMAX, scalar2=None,
                          op0=ALU.mult).then_inc(sems["vector"], 1)
    t_sh = step("vector", f_sh, deps=(t_sigd, t_m2i))
    t_b3 = dma("scalar", "bc3", bc3_d, sc_small[:, 5:6], deps=(t_sh,))
    t_rshbc = dma("scalar", "bc3", rsh_bc[:, :], bc3_d.partition_broadcast(128),
                  deps=(t_b3,))

    def f_swdsc(eng):
        eng.activation(out=swd_sc[:, :], in_=swd_row[:, :], func=AF.Copy,
                       scale=sc_small[:, 7:8]).then_inc(sems["scalar"], 1)
    t_swdsc = step("scalar", f_swdsc, deps=(t_sh, t_swdl))
    t_swdd = dma("scalar", "swd", swd_dram, swd_sc[:, :], deps=(t_swdsc, t_swdl))
    t_swdbc = dma("scalar", "swd", swd_bc[:, :], swd_dram.partition_broadcast(128),
                  deps=(t_swdd,))

    h_ld_free = [(), ()]
    ps_dn_free = [(), ()]
    x1f_free = [(), ()]
    out_free = [(), (), (), ()]
    stats2_tok = ()
    r2_free = [()] * 4
    i8_free = [(), ()]
    out_toks = []
    pend_tail_f = None

    def emit_tail_f(dn_toks, sq2_toks, ch):
        nonlocal stats2_tok

        def f_stats2(eng):
            negm = ln2_sc[:, 12:16]
            v = ln2_sc[:, 4:8]
            vchain(eng, eng.tensor_scalar(out=negm, in0=rsums2[:, 0:4],
                                          scalar1=-1.0 / D, scalar2=None, op0=ALU.mult))
            vchain(eng, eng.tensor_tensor(out=v, in0=negm, in1=negm, op=ALU.mult))
            vchain(eng, eng.scalar_tensor_tensor(out=v, in0=ssqs2[:, 0:4],
                                                 scalar=1.0 / D, in1=v,
                                                 op0=ALU.mult, op1=ALU.subtract))
            vchain(eng, eng.tensor_scalar(out=v, in0=v, scalar1=EPS_LN, scalar2=None,
                                          op0=ALU.add))
            eng.reciprocal(out=ln2_sc[:, 8:12], in_=v).then_inc(sems["vector"], 1)
        t_stats2 = step("vector", f_stats2, deps=(tuple(dn_toks), tuple(sq2_toks)))

        def f_rstd2(eng):
            eng.activation(out=ln2_sc[:, 8:12], in_=ln2_sc[:, 8:12],
                           func=AF.Sqrt).then_inc(sems["scalar"], 1)
        t_rstd2 = step("scalar", f_rstd2, deps=(t_stats2,))
        stats2_tok = (t_rstd2,)

        for tt in range(4):
            t_idx = ch * 4 + tt
            osl = tt

            def f_aff2(eng, tt=tt, osl=osl):
                eng.tensor_scalar(out=out_t(osl), in0=r2_st(tt),
                                  scalar1=ln2_sc[:, 12 + tt:13 + tt],
                                  scalar2=ln2_sc[:, 8 + tt:9 + tt],
                                  op0=ALU.add, op1=ALU.mult).then_inc(sems["vector"], 1)
            t_aff2 = step("vector", f_aff2, deps=((t_rstd2,), out_free[osl]))
            r2_free[tt] = (t_aff2,)
            t_ost = dma("gpsimd", f"ost{osl}", out_d[t_idx * 128:(t_idx + 1) * 128, :],
                        out_t(osl), deps=(t_aff2,))
            out_free[osl] = (t_ost,)
            out_toks.append(t_ost)

    def emit_hld(ch):
        sl = ch % 2
        return dmas("sync", f"hld{sl}",
                    [(h_ld(sl, ic), h_d[ic, :, ch * 512:(ch + 1) * 512])
                     for ic in range(IC)],
                    deps=(d_free, h_ld_free[sl]))

    def emit_hq_unit(ch, j, t_hld):
        sl = ch % 2
        ic = 2 * j
        n2 = min(2, IC - ic)
        isl = j % 2

        def f_hqs(eng, sl=sl, ic=ic, isl=isl, n2=n2):
            eng.activation(out=hq_i8(isl)[:, 0:n2 * 512],
                           in_=h_ld_[:, (sl * IC + ic) * 512:
                                     (sl * IC + ic + n2) * 512],
                           func=AF.Copy,
                           scale=rsh_bc[:, :]).then_inc(sems["scalar"], 1)
        t_hqs = step("scalar", f_hqs, deps=(t_hld, t_rshbc, i8_free[isl]))

        def f_hqv(eng, sl=sl, ic=ic, isl=isl, n2=n2):
            eng.tensor_copy(out=h_ld_[:, (sl * IC + ic) * 512:
                                      (sl * IC + ic + n2) * 512],
                            in_=hq_i8(isl)[:, 0:n2 * 512]).then_inc(
                sems["vector"], 1)
        t_hqv = step("vector", f_hqv, deps=(t_hqs,))
        i8_free[isl] = (t_hqv,)
        return t_hqv

    NQ = (IC + 1) // 2
    hld_by = {0: emit_hld(0)}
    hq_by = {0: [emit_hq_unit(0, j, hld_by[0]) for j in range(NQ)]}
    qsplit = [(0, 3), (3, 6), (6, 9), (9, NQ)]

    for ch in range(NCH):
        sl = ch % 2
        if ch + 1 < NCH:
            hld_by[ch + 1] = emit_hld(ch + 1)

        if pend_tail_f is not None:
            emit_tail_f(*pend_tail_f)
            pend_tail_f = None

        all_hq = tuple(hq_by.pop(ch))
        nxt_hq = []
        dn_toks = []
        sq2_toks = []
        for tt in range(4):
            t_idx = ch * 4 + tt
            xsl = tt % 2
            dbank = tt % 2
            t_x1f = dma("gpsimd", f"x1f{xsl}", x1f(xsl), x1_d[t_idx, :, :],
                        deps=(d_free, bar3, x1f_free[xsl]))

            def f_dnmm(eng, sl=sl, tt=tt, dbank=dbank):
                for ic in range(IC):
                    for half in range(2):
                        ins = eng.matmul(
                            psB[dbank][:, half * 512:(half + 1) * 512],
                            h_ld(sl, ic)[:, tt * 128:(tt + 1) * 128],
                            wdfm_ap(ic)[:, half * 512:(half + 1) * 512],
                            start=(ic == 0), stop=(ic == IC - 1))
                ins.then_inc(sems["tensor"], 1)
            t_dnmm = step("tensor", f_dnmm,
                          deps=(all_hq, ps_dn_free[dbank], t_wdfm, bar4))

            def f_r2(eng, tt=tt, xsl=xsl, dbank=dbank):
                vchain(eng, eng.tensor_tensor(out=t2_st[:, :], in0=psB[dbank][:, :],
                                              in1=swd_bc[:, :], op=ALU.mult))
                eng.scalar_tensor_tensor(
                    out=r2_st(tt), in0=t2_st[:, :], scalar=1.0,
                    in1=x1f(xsl), op0=ALU.mult, op1=ALU.add,
                    accum_out=rsums2[:, tt:tt + 1]).then_inc(sems["vector"], 1)
            prev_r2 = dn_toks[-1] if dn_toks else ()
            t_r2 = step("vector", f_r2,
                        deps=(t_dnmm, t_x1f, t_swdbc, stats2_tok, r2_free[tt], prev_r2))
            ps_dn_free[dbank] = (t_r2,)
            x1f_free[xsl] = (t_r2,)
            dn_toks.append(t_r2)

            def f_sq2(eng, tt=tt):
                eng.scalar_tensor_tensor(
                    out=sq_junk2[:, :], in0=r2_st(tt), scalar=1.0, in1=r2_st(tt),
                    op0=ALU.mult, op1=ALU.mult,
                    accum_out=ssqs2[:, tt:tt + 1]).then_inc(sems["vector"], 1)
            t_sq2 = step("vector", f_sq2, deps=(t_r2, sqj_tok))
            sqj_tok = (t_sq2,)
            sq2_toks.append(t_sq2)

            if ch + 1 < NCH:
                for j in range(*qsplit[tt]):
                    nxt_hq.append(emit_hq_unit(ch + 1, j, hld_by[ch + 1]))

        if ch + 1 < NCH:
            hq_by[ch + 1] = nxt_hq
        h_ld_free[sl] = (dn_toks[-1],)
        pend_tail_f = (dn_toks, sq2_toks, ch)

    emit_tail_f(*pend_tail_f)
    pend_tail_f = None

    stk_d.close()

    if DEBUG_DUMPS:
        t_dbg = dmas("gpsimd", "dbgd",
                     [(dbg_x1, x1_d), (dbg_h, h_d), (dbg_q, q_d), (dbg_wg, wgq_d)],
                     deps=(tuple(out_toks),))
        t_dbg2 = dma("gpsimd", "dbgd", dbg_kv, cc_kv_o, deps=(t_dbg,))
        out_toks.append(t_dbg2)

    def f_final(eng):
        eng.memset(sc_small[:, 6:7], 0.0).then_inc(sems["vector"], 1)
    step("vector", f_final, deps=(tuple(out_toks),))

    for s in plan.count:
        sems[s] = nc.alloc_semaphore(name=f"sem_{s}")
    sems["vchain"] = nc.alloc_semaphore(name="sem_vchain")
    sems["schain"] = nc.alloc_semaphore(name="sem_schain")

    def emit(eng_name, eng):
        watermark = {s: 0 for s in plan.count}
        for (e, fn, deps) in plan.steps:
            if e != eng_name:
                continue
            for (de, dv) in deps:
                if watermark[de] >= dv:
                    continue
                eng.wait_ge(sems[de], dv)
                watermark[de] = dv
            fn(eng)

    with nc.Block() as block:
        @block.tensor
        def _(tensor):
            emit("tensor", tensor)

        @block.vector
        def _(vector):
            emit("vector", vector)

        @block.scalar
        def _(scalar):
            emit("scalar", scalar)

        @block.sync
        def _(sync):
            emit("sync", sync)

        @block.gpsimd
        def _(gpsimd):
            emit("gpsimd", gpsimd)

    return nc


_NC_CACHE = []


def kernel(L0_attn_inputs_q, L0_attn_Wq, L0_attn_bq, L0_attn_Wk, L0_attn_bk,
           L0_attn_Wv, L0_attn_bv, L0_attn_Wo, L0_attn_bo,
           L0_ffn_clip_x, L0_ffn_clip_down, L0_ffn_w_gate, L0_ffn_w_up,
           L0_ffn_w_down):
    x = np.ascontiguousarray(np.asarray(L0_attn_inputs_q, dtype=np.float32))
    onehot = np.zeros((NH, NP * 128), dtype=ml_dtypes.bfloat16)
    for p in range(NP):
        onehot[2 * p, p * 128:p * 128 + 64] = 1.0
        onehot[2 * p + 1, p * 128 + 64:p * 128 + 128] = 1.0

    wg = np.ascontiguousarray(L0_ffn_w_gate, dtype=np.float32)
    wu = np.ascontiguousarray(L0_ffn_w_up, dtype=np.float32)
    wd = np.ascontiguousarray(L0_ffn_w_down, dtype=np.float32)
    common = {
        "wq": np.ascontiguousarray(L0_attn_Wq, dtype=np.float32),
        "wk": np.ascontiguousarray(L0_attn_Wk, dtype=np.float32),
        "wv": np.ascontiguousarray(L0_attn_Wv, dtype=np.float32),
        "wo": np.ascontiguousarray(L0_attn_Wo, dtype=np.float32),
        "bq": np.ascontiguousarray(L0_attn_bq, dtype=np.float32).reshape(1, D),
        "bk": np.ascontiguousarray(L0_attn_bk, dtype=np.float32).reshape(1, D),
        "bv": np.ascontiguousarray(L0_attn_bv, dtype=np.float32).reshape(1, D),
        "bo": np.ascontiguousarray(L0_attn_bo, dtype=np.float32).reshape(1, D),
        "clipx": np.ascontiguousarray(L0_ffn_clip_x, dtype=np.float32).reshape(1, 1),
        "clipd": np.ascontiguousarray(L0_ffn_clip_down, dtype=np.float32).reshape(1, 1),
        "wg": wg, "wu": wu, "wd": wd,
        "wgt": np.ascontiguousarray(wg.T),
        "wut": np.ascontiguousarray(wu.T),
        "wdt": np.ascontiguousarray(wd.T),
        "onehot": onehot,
    }
    in_maps = []
    for c in range(NCORES):
        b, half = c // 2, c % 2
        xc = np.ascontiguousarray(x[b, half * T:(half + 1) * T, :])
        in_maps.append({"x": xc, **common})

    if not _NC_CACHE:
        _NC_CACHE.append(build_program())
    nc = _NC_CACHE[0]
    res = run_bass_kernel_spmd(nc, in_maps, core_ids=list(range(NCORES)))
    out = np.empty((B, S, D), dtype=np.float32)
    for c in range(NCORES):
        b, half = c // 2, c % 2
        out[b, half * T:(half + 1) * T, :] = res.results[c]["out"]
    return out


# version 18
# speedup vs baseline: 1.1538x; 1.0124x over previous
from contextlib import ExitStack

import numpy as np
import ml_dtypes
import concourse.bass as bass
import concourse.mybir as mybir
from concourse.bass_utils import run_bass_kernel_spmd

F32 = mybir.dt.float32
BF16 = mybir.dt.bfloat16
I8 = mybir.dt.int8
AF = mybir.ActivationFunctionType
ALU = mybir.AluOpType
AX = mybir.AxisListType.X

DEBUG_DUMPS = False
NCORES = 8
B, S, D = 4, 8192, 1024
T = 4096
NT = T // 128
NCH = T // 512
DC = D // 128
NH, DH = 16, 64
NP = NH // 2
I = 2816
IC = I // 128
HT = D // 128
EPS_ATTN = 1e-6
EPS_LN = 1e-5
QMAX = 127.0

ENGS = ("tensor", "vector", "scalar", "sync", "gpsimd")
COMPUTE_ENGS = ("tensor", "vector", "scalar")


class Plan:

    def __init__(self):
        self.steps = []
        self.count = {}
        self.eng_of = {}

    @staticmethod
    def _flatten(deps, out):
        for d in deps:
            if d is None or d == () or d == []:
                continue
            if isinstance(d, tuple) and len(d) == 2 and isinstance(d[0], str):
                out.append(d)
            else:
                Plan._flatten(d, out)

    def add(self, eng, fn, deps=(), inc=1, stream=None):
        stream = stream or eng
        self.eng_of.setdefault(stream, eng)
        flat = []
        self._flatten(deps, flat)
        self.count[stream] = self.count.get(stream, 0) + inc
        tok = (stream, self.count[stream])
        self.steps.append((eng, fn, tuple(flat)))
        return tok

    def barrier(self):
        return tuple((s, v) for s, v in self.count.items() if v > 0)


def build_program():
    nc = bass.Bass("TRN2", target_bir_lowering=False)
    plan = Plan()

    x_in = nc.dram_tensor("x", [T, D], F32, kind="ExternalInput").ap()
    wq_in = nc.dram_tensor("wq", [D, D], F32, kind="ExternalInput").ap()
    wk_in = nc.dram_tensor("wk", [D, D], F32, kind="ExternalInput").ap()
    wv_in = nc.dram_tensor("wv", [D, D], F32, kind="ExternalInput").ap()
    wo_in = nc.dram_tensor("wo", [D, D], F32, kind="ExternalInput").ap()
    bq_in = nc.dram_tensor("bq", [1, D], F32, kind="ExternalInput").ap()
    bk_in = nc.dram_tensor("bk", [1, D], F32, kind="ExternalInput").ap()
    bv_in = nc.dram_tensor("bv", [1, D], F32, kind="ExternalInput").ap()
    bo_in = nc.dram_tensor("bo", [1, D], F32, kind="ExternalInput").ap()
    clipx_in = nc.dram_tensor("clipx", [1, 1], F32, kind="ExternalInput").ap()
    clipd_in = nc.dram_tensor("clipd", [1, 1], F32, kind="ExternalInput").ap()
    wg_in = nc.dram_tensor("wg", [I, D], F32, kind="ExternalInput").ap()
    wu_in = nc.dram_tensor("wu", [I, D], F32, kind="ExternalInput").ap()
    wd_in = nc.dram_tensor("wd", [D, I], F32, kind="ExternalInput").ap()
    wgt_in = nc.dram_tensor("wgt", [D, I], F32, kind="ExternalInput").ap()
    wut_in = nc.dram_tensor("wut", [D, I], F32, kind="ExternalInput").ap()
    wdt_in = nc.dram_tensor("wdt", [I, D], F32, kind="ExternalInput").ap()
    oh_in = nc.dram_tensor("onehot", [NH, NP * 128], BF16, kind="ExternalInput").ap()
    out_d = nc.dram_tensor("out", [T, D], F32, kind="ExternalOutput").ap()
    if DEBUG_DUMPS:
        dbg_x1 = nc.dram_tensor("dbg_x1", [NT, 128, D], BF16, kind="ExternalOutput").ap()
        dbg_h = nc.dram_tensor("dbg_h", [IC, 128, T], BF16, kind="ExternalOutput").ap()
        dbg_q = nc.dram_tensor("dbg_q", [DC, 128, T], BF16, kind="ExternalOutput").ap()
        dbg_kv = nc.dram_tensor("dbg_kv", [128, NP * 65], F32, kind="ExternalOutput").ap()
        dbg_wg = nc.dram_tensor("dbg_wg", [DC, 128, I], BF16, kind="ExternalOutput").ap()
        dbg_xq = nc.dram_tensor("dbg_xq", [DC, 128, T], BF16, kind="ExternalOutput").ap()
        dbg_wfm = nc.dram_tensor("dbg_wfm", [128, 2 * D], BF16, kind="ExternalOutput").ap()

    q_d = nc.dram_tensor("q_d", [DC, 128, T], BF16).ap()
    x1_d = nc.dram_tensor("x1_d", [NT, 128, D], BF16).ap()
    h_d = nc.dram_tensor("h_d", [IC, 128, T], BF16).ap()
    wgq_d = nc.dram_tensor("wgq_d", [DC, 128, I], BF16).ap()
    wuq_d = nc.dram_tensor("wuq_d", [DC, 128, I], BF16).ap()
    wdq_d = nc.dram_tensor("wdq_d", [IC, 128, D], BF16).ap()
    rsc_d = nc.dram_tensor("rsc_d", [1, I], F32).ap()
    cc_kv_i = nc.dram_tensor("cc_kv_i", [128, NP * 65], F32).ap()
    cc_kv_o = nc.dram_tensor("cc_kv_o", [128, NP * 65], F32).ap()
    cc_m1_i = nc.dram_tensor("cc_m1_i", [1, 1], F32).ap()
    cc_m1_o = nc.dram_tensor("cc_m1_o", [1, 1], F32).ap()
    cc_m2_i = nc.dram_tensor("cc_m2_i", [1, 1], F32).ap()
    cc_m2_o = nc.dram_tensor("cc_m2_o", [1, 1], F32).ap()
    pmax_d = nc.dram_tensor("pmax_d", [1, 128], F32).ap()
    bc1_d = nc.dram_tensor("bc1_d", [1, 1], F32).ap()
    bc2_d = nc.dram_tensor("bc2_d", [1, 1], F32).ap()
    bc3_d = nc.dram_tensor("bc3_d", [1, 1], F32).ap()
    swd_dram = nc.dram_tensor("swd_dram", [1, D], F32).ap()

    sems = {}
    chain_cnt = {"vector": 0, "scalar": 0}

    def vchain(eng, ins):
        chain_cnt["vector"] += 1
        ins.then_inc(sems["vchain"], 1)
        eng.wait_ge(sems["vchain"], chain_cnt["vector"])

    def schain(eng, ins):
        chain_cnt["scalar"] += 1
        ins.then_inc(sems["schain"], 1)
        eng.wait_ge(sems["schain"], chain_cnt["scalar"])

    def dma(queue, stream, out, in_, deps=(), slow=False):
        def fn(eng):
            eng.dma_start(out=out, in_=in_,
                          allow_slow_non_contiguous=slow).then_inc(sems[stream], 16)
        return plan.add(queue, fn, deps, inc=16, stream=stream)

    def dmas(queue, stream, pairs, deps=()):
        def fn(eng):
            for o, i_ in pairs:
                eng.dma_start(out=o, in_=i_).then_inc(sems[stream], 16)
        return plan.add(queue, fn, deps, inc=16 * len(pairs), stream=stream)

    def step(eng, fn, deps=(), inc=1):
        return plan.add(eng, fn, deps, inc)

    psA = [nc.alloc_psum_tensor(f"psA{i}", [128, 512], F32) for i in range(2)]
    psB = [nc.alloc_psum_tensor(f"psB{i}", [128, 1024], F32) for i in range(2)]
    psC = [nc.alloc_psum_tensor(f"psC{i}", [128, 512], F32) for i in range(2)]

    ones_row = nc.alloc_sbuf_tensor("ones_row", [1, 128], BF16)
    ones512 = nc.alloc_sbuf_tensor("ones512", [1, 512], BF16)
    bq_bf = nc.alloc_sbuf_tensor("bq_bf", [1, D], BF16)
    bk_bf = nc.alloc_sbuf_tensor("bk_bf", [1, D], BF16)
    bv_bf = nc.alloc_sbuf_tensor("bv_bf", [1, D], BF16)
    bo_bf = nc.alloc_sbuf_tensor("bo_bf", [1, D], BF16)
    clipx_s = nc.alloc_sbuf_tensor("clipx_s", [1, 1], F32)
    clipd_s = nc.alloc_sbuf_tensor("clipd_s", [1, 1], F32)
    oh_sb = nc.alloc_sbuf_tensor("oh_sb", [NH, NP * 128], BF16)
    kv_sb = nc.alloc_sbuf_tensor("kv_sb", [128, NP * 65], F32)
    kvt_sb = nc.alloc_sbuf_tensor("kvt_sb", [128, NP * 65], F32)
    xmax_cols = nc.alloc_sbuf_tensor("xmax_cols", [128, NT], F32)
    hmax_cols = nc.alloc_sbuf_tensor("hmax_cols", [128, IC * NCH], F32)
    smax1 = nc.alloc_sbuf_tensor("smax1", [1, 128], F32)
    gmax1 = nc.alloc_sbuf_tensor("gmax1", [1, 1], F32)
    gmax2 = nc.alloc_sbuf_tensor("gmax2", [1, 1], F32)
    sc_small = nc.alloc_sbuf_tensor("sc_small", [1, 8], F32)
    rsx_bc = nc.alloc_sbuf_tensor("rsx_bc", [128, 1], F32)
    sx_bc = nc.alloc_sbuf_tensor("sx_bc", [128, 1], F32)
    rsh_bc = nc.alloc_sbuf_tensor("rsh_bc", [128, 1], F32)
    swg_cols = nc.alloc_sbuf_tensor("swg_cols", [128, IC], F32)
    swu_cols = nc.alloc_sbuf_tensor("swu_cols", [128, IC], F32)
    swd_cols = nc.alloc_sbuf_tensor("swd_cols", [128, HT], F32)
    svec_g = nc.alloc_sbuf_tensor("svec_g", [128, IC], F32)
    svec_u = nc.alloc_sbuf_tensor("svec_u", [128, IC], F32)
    swd_row = nc.alloc_sbuf_tensor("swd_row", [1, D], F32)
    swd_sc = nc.alloc_sbuf_tensor("swd_sc", [1, D], F32)
    ln_sc = nc.alloc_sbuf_tensor("ln_sc", [128, 16], F32)
    red_sc = nc.alloc_sbuf_tensor("red_sc", [128, 1], F32)
    ln2_sc = nc.alloc_sbuf_tensor("ln2_sc", [128, 16], F32)
    rsums = nc.alloc_sbuf_tensor("rsums", [128, 4], F32)
    ssqs = nc.alloc_sbuf_tensor("ssqs", [128, 4], F32)
    rsums2 = nc.alloc_sbuf_tensor("rsums2", [128, 4], F32)
    ssqs2 = nc.alloc_sbuf_tensor("ssqs2", [128, 4], F32)
    rcols = nc.alloc_sbuf_tensor("rcols", [128, IC], F32)

    t_bias = dmas("gpsimd", "bias", [(bq_bf[:, :], bq_in), (bk_bf[:, :], bk_in),
                                     (bv_bf[:, :], bv_in), (bo_bf[:, :], bo_in)])
    t_clip = dmas("sync", "clip", [(clipx_s[:, :], clipx_in), (clipd_s[:, :], clipd_in),
                                   (oh_sb[:, :], oh_in)])

    def f_consts(eng):
        eng.memset(ones_row[:, :], 1.0)
        eng.memset(ones512[:, :], 1.0).then_inc(sems["vector"], 1)
    t_ones = step("vector", f_consts)

    stk_a = ExitStack()

    def sb_a(name, shape, dtype):
        return stk_a.enter_context(nc.sbuf_tensor(name, shape, dtype))

    wq_bf = sb_a("wq_bf", [128, DC * D], BF16)
    wk_bf = sb_a("wk_bf", [128, DC * D], BF16)
    wv_bf = sb_a("wv_bf", [128, DC * D], BF16)
    t_wq = dmas("gpsimd", "wq", [(wq_bf[:, k * D:(k + 1) * D], wq_in[k * 128:(k + 1) * 128, :])
                                 for k in range(DC)])
    t_wk = dmas("gpsimd", "wk", [(wk_bf[:, k * D:(k + 1) * D], wk_in[k * 128:(k + 1) * 128, :])
                                 for k in range(DC)])
    t_wv = dmas("gpsimd", "wv", [(wv_bf[:, k * D:(k + 1) * D], wv_in[k * 128:(k + 1) * 128, :])
                                 for k in range(DC)])

    wrow_ = sb_a("wrow", [128, 2 * I], F32)
    wrow_sl = lambda s: wrow_[:, s * I:(s + 1) * I]
    wtc_ = sb_a("wtc", [128, 2 * I], F32)
    wtc = lambda s: wtc_[:, s * I:(s + 1) * I]
    wti8_ = sb_a("wti8", [128, 2 * I], I8)
    wti8 = lambda s: wti8_[:, s * I:(s + 1) * I]
    rs_bc_ = sb_a("rs_bc", [128, 2 * I], F32)
    rs_bc = lambda s: rs_bc_[:, s * I:(s + 1) * I]
    rs2_bc = sb_a("rs2_bc", [128, 2 * D], F32)

    prep_q = []
    ps = {"rowfree0": (), "rowfree1": (), "tcfree0": (), "tcfree1": (),
          "tcfree2": (), "i8free0": (), "i8free1": (), "rcols": (),
          "rsd": (), "rsbc0": (), "rsbc1": (),
          "rsbc_readers0": [], "rsbc_readers1": []}

    def mk_rows(w_in, cols, n_rows, width, toks):
        nb_max = 2 if width <= 1024 else 1
        units = []
        bidx = [0]

        def mk(rt0, nb):
            def u():
                sl = bidx[0] % 2
                bidx[0] += 1
                buf = wrow_sl(sl)
                src_ap = w_in[rt0 * 128:(rt0 + nb) * 128, :].rearrange(
                    "(a p) d -> p a d", p=128)
                t_l = dma("gpsimd", f"pwr{sl}",
                          buf[:, 0:nb * width].rearrange("p (a d) -> p a d", d=width),
                          src_ap, deps=(ps[f"rowfree{sl}"],))

                def f_red(eng, rt0=rt0, nb=nb, cols=cols, width=width, buf=buf):
                    for j in range(nb):
                        ins = eng.tensor_reduce(
                            out=cols[:, rt0 + j:rt0 + j + 1],
                            in_=buf[:, j * width:(j + 1) * width],
                            axis=AX, op=ALU.max, apply_absolute_value=True)
                        if j < nb - 1:
                            vchain(eng, ins)
                    ins.then_inc(sems["vector"], 1)
                t_r = step("vector", f_red, deps=(t_l,))
                ps[f"rowfree{sl}"] = (t_r,)
                toks.append(t_r)
            return u
        rt0 = 0
        while rt0 < n_rows:
            nb = min(nb_max, n_rows - rt0)
            units.append(mk(rt0, nb))
            rt0 += nb
        return units

    def mk_rsc(cols, n, rowtoks, slot):
        def u():
            def f_rc(eng, cols=cols, n=n):
                eng.reciprocal(out=rcols[:, 0:n], in_=cols[:, 0:n]).then_inc(
                    sems["vector"], 1)
            t_rc = step("vector", f_rc, deps=(tuple(rowtoks), ps["rcols"]))
            dest = rsc_d.rearrange("a (i p) -> (a p) i", p=128)[:, 0:n]
            t_w = dma("scalar", "prsd", dest, rcols[:, 0:n],
                      deps=(t_rc, ps["rsd"]), slow=True)
            ps["rcols"] = (t_w,)
            t_bc = dma("scalar", "prsb", rs_bc(slot)[:, 0:n * 128],
                       rsc_d[:, 0:n * 128].partition_broadcast(128),
                       deps=(t_w, tuple(ps[f"rsbc_readers{slot}"])))
            ps["rsd"] = (t_bc,)
            ps[f"rsbc{slot}"] = (t_bc,)
            ps[f"rsbc_readers{slot}"] = []
        return [u]

    def mk_fm(wt_in, width, n_chunks, dest_dram, slot):
        units = []
        cnt = [0]
        nc2 = max(1, I // width // 2 + (0 if width == I else 1))
        nc2 = 1 if width == I else 2

        def flush(eng_unused=None):
            st = ps.pop("pend_st", None)
            if st:
                st()

        def mk(hc, nb):
            def u():
                flush()
                sl = cnt[0] % 2
                j = cnt[0] % 2
                cnt[0] += 1
                t_l = dma("gpsimd", f"ptc{sl}",
                          wtc(sl)[:, 0:nb * width].rearrange(
                              "p (a d) -> p a d", d=width),
                          wt_in[hc * 128:(hc + nb) * 128, :].rearrange(
                              "(a p) d -> p a d", p=128),
                          deps=(ps[f"tcfree{sl}"],))

                def f_q(eng, sl=sl, j=j, width=width, nb=nb, slot=slot):
                    eng.scalar_tensor_tensor(
                        out=wti8(j)[:, 0:nb * width], in0=wtc(sl)[:, 0:nb * width],
                        scalar=QMAX,
                        in1=rs_bc(slot)[:, 0:width] if nb == 1
                        else rs2_bc[:, 0:nb * width],
                        op0=ALU.mult, op1=ALU.mult).then_inc(sems["vector"], 1)
                t_q = step("vector", f_q,
                           deps=(t_l, ps[f"rsbc{slot}"], ps.get("rs2", ()),
                                 ps[f"i8free{j}"]))
                ps[f"tcfree{sl}"] = (t_q,)
                ps[f"rsbc_readers{slot}"].append(t_q)

                def emit_st(hc=hc, nb=nb, j=j, t_q=t_q, width=width):
                    t_s = dma("gpsimd", f"pst{j}",
                              dest_dram[hc:hc + nb, :, 0:width].transpose([1, 0, 2]),
                              wti8(j)[:, 0:nb * width].rearrange(
                                  "p (a d) -> p a d", d=width),
                              deps=(t_q,))
                    ps[f"i8free{j}"] = (t_s,)
                ps["pend_st"] = emit_st
            return u
        hc = 0
        while hc < n_chunks:
            nb = min(nc2, n_chunks - hc)
            units.append(mk(hc, nb))
            hc += nb
        units.append(lambda: flush())
        return units

    g_row_toks, u_row_toks, d_row_toks = [], [], []
    prep_q += mk_rows(wg_in, swg_cols, IC, D, g_row_toks)
    prep_q += mk_rsc(swg_cols, IC, g_row_toks, 0)
    prep_q += mk_rows(wu_in, swu_cols, IC, D, u_row_toks)
    prep_q += mk_rsc(swu_cols, IC, u_row_toks, 1)
    prep_q += mk_rows(wd_in, swd_cols, HT, I, d_row_toks)
    prep_q += mk_fm(wgt_in, I, DC, wgq_d, 0)
    prep_q += mk_rsc(swd_cols, HT, d_row_toks, 0)

    def u_rs2():
        def f_cp(eng):
            vchain(eng, eng.tensor_copy(out=rs2_bc[:, 0:D], in_=rs_bc(0)[:, 0:D]))
            eng.tensor_copy(out=rs2_bc[:, D:2 * D],
                            in_=rs_bc(0)[:, 0:D]).then_inc(sems["vector"], 1)
        t_cp = step("vector", f_cp, deps=(ps["rsbc0"],))
        ps["rsbc_readers0"].append(t_cp)
        ps["rs2"] = (t_cp,)
    prep_q += [u_rs2]
    prep_q += mk_fm(wut_in, I, DC, wuq_d, 1)
    prep_q += mk_fm(wdt_in, D, IC, wdq_d, 0)
    prep_idx = [0]

    def pump(n):
        while n > 0 and prep_idx[0] < len(prep_q):
            prep_q[prep_idx[0]]()
            prep_idx[0] += 1
            n -= 1

    x_tok_bf_ = sb_a("x_tok_bf", [128, 2 * D], BF16)
    x_tok_bf = lambda s: x_tok_bf_[:, s * D:(s + 1) * D]

    xfm_ = sb_a("xfm", [128, 2 * DC * 512], BF16)
    q_e = sb_a("q_e", [128, 512], BF16)
    q_m = sb_a("q_m", [128, 512], BF16)
    q_out_ = sb_a("q_out", [128, DC * 512], BF16)
    k_e = sb_a("k_e", [128, 1024], BF16)
    k_m = sb_a("k_m", [128, 1024], BF16)
    k_bf_ = sb_a("k_bf", [128, 4 * 1024], BF16)
    k_bf = lambda s: k_bf_[:, s * 1024:(s + 1) * 1024]
    v_aug_ = sb_a("v_aug", [128, 4 * NH * 65], BF16)
    v_aug = lambda s: v_aug_[:, s * NH * 65:(s + 1) * NH * 65]

    def xfm_ap(sl, c):
        return xfm_[:, (sl * DC + c) * 512:(sl * DC + c + 1) * 512]

    def f_vones(eng):
        for tt in range(4):
            va = v_aug(tt).rearrange("p (h c) -> p h c", c=65)
            ins = eng.memset(va[:, :, 64:65], 1.0)
        ins.then_inc(sems["vector"], 1)
    t_vones = step("vector", f_vones)

    xtok_free = [(), ()]
    xf_free = ()
    xfm_free = [(), ()]
    tr_tok = [[None] * 4, [None] * 4]
    ps_q_free = [(), ()]
    ps_k_free = ()
    ps_v_free = ()
    ps_kv_free = [(), ()]
    q_e_free = ()
    q_out_free = ()
    k_e_free = ()
    kslot_free = [()] * 4
    kv_sb_tok = ()
    last_kv_mm = ()

    for ch in range(NCH):
        sl = ch % 2
        for tt in range(4):
            t_idx = ch * 4 + tt
            tsl = t_idx % 2
            t_xld = dma("gpsimd", f"xt{tsl}", x_tok_bf(tsl),
                        x_in[t_idx * 128:(t_idx + 1) * 128, :],
                        deps=(xtok_free[tsl],))

            def f_tr(eng, sl=sl, tt=tt, tsl=tsl):
                for c in range(DC):
                    eng.dma_start(out=xfm_ap(sl, c)[:, tt * 128:(tt + 1) * 128],
                                  in_=x_tok_bf(tsl)[:, c * 128:(c + 1) * 128],
                                  transpose=True).then_inc(sems[f"tr{tsl}"], 16)
            t_tr = plan.add("sync", f_tr, (t_xld, xfm_free[sl]), inc=16 * DC,
                            stream=f"tr{tsl}")
            tr_tok[sl][tt] = t_tr
            xtok_free[tsl] = (t_tr,)
        ch_tr = tuple(tr_tok[sl])

        q_mm_toks = []
        qvec_chunk_toks = []
        for m in range(DC):
            bank = m % 2

            def f_qmm(eng, m=m, bank=bank, sl=sl):
                for k in range(DC):
                    eng.matmul(psA[bank][:, :],
                               wq_bf[:, k * D + m * 128: k * D + (m + 1) * 128],
                               xfm_ap(sl, k),
                               start=(k == 0), stop=False)
                eng.matmul(psA[bank][:, :],
                           bq_bf[:, m * 128:(m + 1) * 128],
                           ones512[:, :],
                           start=False, stop=True).then_inc(sems["tensor"], 1)
            t_qmm = step("tensor", f_qmm,
                         deps=(t_wq, t_bias, t_ones, ps_q_free[bank], ch_tr))
            q_mm_toks.append(t_qmm)

            def f_qexp(eng, bank=bank):
                eng.activation(out=q_e[:, :], in_=psA[bank][:, :],
                               func=AF.Exp).then_inc(sems["scalar"], 1)
            t_qexp = step("scalar", f_qexp, deps=(t_qmm, q_e_free))

            def f_qvec(eng, bank=bank, m=m):
                vchain(eng, eng.tensor_scalar(out=q_m[:, :], in0=q_e[:, :],
                                              scalar1=1.0, scalar2=None, op0=ALU.min))
                eng.scalar_tensor_tensor(
                    out=q_out_[:, m * 512:(m + 1) * 512], in0=psA[bank][:, :],
                    scalar=0.0, in1=q_m[:, :], op0=ALU.max,
                    op1=ALU.add).then_inc(sems["vector"], 1)
            t_qvec = step("vector", f_qvec, deps=(t_qexp, q_out_free))
            ps_q_free[bank] = (t_qvec,)
            q_e_free = (t_qvec,)
            qvec_chunk_toks.append(t_qvec)
            pump(1)

        t_qst = dma("scalar", "qst",
                    q_d[:, :, ch * 512:(ch + 1) * 512].transpose([1, 0, 2]),
                    q_out_[:, :].rearrange("p (a d) -> p a d", d=512),
                    deps=(tuple(qvec_chunk_toks),))
        q_out_free = (t_qst,)

        kvec_toks = []
        for tt in range(4):
            def f_kvmm(eng, sl=sl, tt=tt):
                for k in range(DC):
                    lhs = xfm_ap(sl, k)[:, tt * 128:(tt + 1) * 128]
                    for half in range(2):
                        eng.matmul(psB[0][:, half * 512:(half + 1) * 512], lhs,
                                   wk_bf[:, k * D + half * 512: k * D + (half + 1) * 512],
                                   start=(k == 0), stop=False)
                        eng.matmul(psB[1][:, half * 512:(half + 1) * 512], lhs,
                                   wv_bf[:, k * D + half * 512: k * D + (half + 1) * 512],
                                   start=(k == 0), stop=False)
                for half in range(2):
                    eng.matmul(psB[0][:, half * 512:(half + 1) * 512], ones_row[:, :],
                               bk_bf[:, half * 512:(half + 1) * 512],
                               start=False, stop=True)
                    ins = eng.matmul(psB[1][:, half * 512:(half + 1) * 512], ones_row[:, :],
                                     bv_bf[:, half * 512:(half + 1) * 512],
                                     start=False, stop=True)
                ins.then_inc(sems["tensor"], 1)
            t_kvmm = step("tensor", f_kvmm,
                          deps=(t_wk, t_wv, t_bias, t_ones, ps_k_free, ps_v_free,
                                kslot_free[tt], ch_tr))

            def f_kexp(eng):
                eng.activation(out=k_e[:, :], in_=psB[0][:, :],
                               func=AF.Exp).then_inc(sems["scalar"], 1)
            t_kexp = step("scalar", f_kexp, deps=(t_kvmm, k_e_free))

            def f_kvec(eng, tt=tt):
                vchain(eng, eng.tensor_scalar(out=k_m[:, :], in0=k_e[:, :],
                                              scalar1=1.0, scalar2=None, op0=ALU.min))
                eng.scalar_tensor_tensor(
                    out=k_bf(tt), in0=psB[0][:, :], scalar=0.0,
                    in1=k_m[:, :], op0=ALU.max, op1=ALU.add)
                va = v_aug(tt).rearrange("p (h c) -> p h c", c=65)
                pv = psB[1][:, :].rearrange("p (h c) -> p h c", c=64)
                eng.tensor_copy(out=va[:, :, 0:64], in_=pv[:, :, :]).then_inc(
                    sems["vector"], 1)
            t_kvec = step("vector", f_kvec, deps=(t_kexp, t_vones, kslot_free[tt]))
            ps_k_free = (t_kvec,)
            ps_v_free = (t_kvec,)
            k_e_free = (t_kvec,)
            kvec_toks.append(t_kvec)
            pump(1)

        for p in range(NP):
            bank = p % 2

            def f_kvps(eng, p=p, bank=bank):
                hA, hB = 2 * p, 2 * p + 1
                for tt in range(4):
                    eng.matmul(psC[bank][0:64, 0:65],
                               k_bf(tt)[:, hA * 64:(hA + 1) * 64],
                               v_aug(tt)[:, hA * 65:(hA + 1) * 65],
                               start=(tt == 0), stop=(tt == 3),
                               tile_position=(0, 0))
                for tt in range(4):
                    ins = eng.matmul(psC[bank][64:128, 0:65],
                                     k_bf(tt)[:, hB * 64:(hB + 1) * 64],
                                     v_aug(tt)[:, hB * 65:(hB + 1) * 65],
                                     start=(tt == 0), stop=(tt == 3),
                                     tile_position=(0, 64))
                ins.then_inc(sems["tensor"], 1)
            t_kvps = step("tensor", f_kvps, deps=(tuple(kvec_toks), ps_kv_free[bank]))
            last_kv_mm = (t_kvps,)

            def f_kvacc(eng, p=p, bank=bank, first=(ch == 0)):
                dst = kv_sb[:, p * 65:(p + 1) * 65]
                if first:
                    eng.tensor_copy(out=dst, in_=psC[bank][:, 0:65]).then_inc(
                        sems["vector"], 1)
                else:
                    eng.tensor_tensor(out=dst, in0=dst, in1=psC[bank][:, 0:65],
                                      op=ALU.add).then_inc(sems["vector"], 1)
            t_kvacc = step("vector", f_kvacc, deps=(t_kvps, kv_sb_tok))
            kv_sb_tok = (t_kvacc,)
            ps_kv_free[bank] = (t_kvacc,)

        for tt in range(4):
            kslot_free[tt] = last_kv_mm
        xfm_free[sl] = (q_mm_toks[-1], t_kvmm)

        pump(2)

    pump(len(prep_q))

    t_kv_out = dma("gpsimd", "cc", cc_kv_i, kv_sb[:, :], deps=(kv_sb_tok,))

    def f_cc_kv(eng):
        eng.collective_compute(
            "AllReduce", ALU.add,
            replica_groups=[[0, 1], [2, 3], [4, 5], [6, 7]],
            ins=[cc_kv_i.opt()], outs=[cc_kv_o.opt()]).then_inc(sems["gpsimd"], 1)
    t_cc_kv = step("gpsimd", f_cc_kv, deps=(t_kv_out,), inc=1)
    t_kv_in = dma("gpsimd", "cc", kvt_sb[:, :], cc_kv_o, deps=(t_cc_kv,))

    bar2 = plan.barrier()
    stk_a.close()

    stk_c = ExitStack()
    xq_fm = stk_c.enter_context(nc.sbuf_tensor("xq_fm", [128, DC * T], BF16))

    def xq_ap(c):
        return xq_fm[:, c * T:(c + 1) * T]

    stk_b = ExitStack()

    def sb_b(name, shape, dtype):
        return stk_b.enter_context(nc.sbuf_tensor(name, shape, dtype))

    wo_bf = sb_b("wo_bf", [128, DC * D], BF16)
    blockdiag = sb_b("blockdiag", [128, NP * 128], BF16)
    normlhs = sb_b("normlhs", [128, DC * NH], BF16)
    q_ch_ = sb_b("q_ch", [128, 2 * DC * 512], BF16)
    q_ch = lambda s, c: q_ch_[:, (s * DC + c) * 512:(s * DC + c + 1) * 512]
    mix_bf_ = sb_b("mix_bf", [128, 2 * DC * 512], BF16)
    mix_bf = lambda s: mix_bf_[:, s * DC * 512:(s + 1) * DC * 512]
    norm_sb = sb_b("norm_sb", [16, 512], F32)
    norm_sb2 = sb_b("norm_sb2", [16, 512], F32)
    rexp_sb_ = sb_b("rexp_sb", [128, 2 * 512], F32)
    rexp_sb = lambda s: rexp_sb_[:, s * 512:(s + 1) * 512]
    recip_bf = sb_b("recip_bf", [16, 512], BF16)
    x0_tile_ = sb_b("x0_tile", [128, 4 * D], F32)
    x0_tile = lambda s: x0_tile_[:, s * D:(s + 1) * D]
    r1_st_ = sb_b("r1_st", [128, 4 * D], F32)
    r1_st = lambda s: r1_st_[:, s * D:(s + 1) * D]
    x1_bf_ = sb_b("x1_bf", [128, 4 * D], BF16)
    x1_bf = lambda s: x1_bf_[:, s * D:(s + 1) * D]
    sq_junk = sb_b("sq_junk", [128, D], F32)

    t_wo = dmas("gpsimd", "wo", [(wo_bf[:, k * D:(k + 1) * D], wo_in[k * 128:(k + 1) * 128, :])
                                 for k in range(DC)], deps=(bar2,))

    def f_blockdiag(eng):
        vchain(eng, eng.memset(blockdiag[:, :], 0.0))
        vchain(eng, eng.memset(normlhs[:, :], 0.0))
        for p in range(NP):
            eng.tensor_copy(out=blockdiag[0:64, p * 128: p * 128 + 64],
                            in_=kvt_sb[0:64, p * 65: p * 65 + 64])
            eng.tensor_copy(out=blockdiag[64:128, p * 128 + 64: p * 128 + 128],
                            in_=kvt_sb[64:128, p * 65: p * 65 + 64])
            eng.tensor_copy(out=normlhs[0:64, p * NH + 2 * p: p * NH + 2 * p + 1],
                            in_=kvt_sb[0:64, p * 65 + 64: p * 65 + 65])
            ins = eng.tensor_copy(out=normlhs[64:128, p * NH + 2 * p + 1: p * NH + 2 * p + 2],
                                  in_=kvt_sb[64:128, p * 65 + 64: p * 65 + 65])
        ins.then_inc(sems["vector"], 1)
    t_bd = step("vector", f_blockdiag, deps=(t_kv_in, bar2))

    q_slot_free = [(), ()]
    mix_slot_free = [(), ()]
    ps_norm_free = ()
    ps_mix_free = [(), ()]
    ps_rexp_free = [(), ()]
    rexp_sb_free = [(), ()]
    ps_wo_free = [(), ()]
    psB0_free = ()
    recip_free = ()
    x0_free = [bar2, bar2, bar2, bar2]
    x1_free = [(), (), (), ()]
    stats_tok = ()
    r1_free = [()] * 4
    sqj_tok = ()
    x1_store_toks = []
    x1tr_toks = []
    xmax_toks = []
    pend_tail_c = None
    pend_tail2_c = None

    def emit_tail1_c(wo_toks, sq_toks, ch):
        nonlocal stats_tok

        def f_stats(eng):
            negm = ln_sc[:, 12:16]
            v = ln_sc[:, 4:8]
            vchain(eng, eng.tensor_scalar(out=negm, in0=rsums[:, 0:4], scalar1=-1.0 / D,
                                          scalar2=None, op0=ALU.mult))
            vchain(eng, eng.tensor_tensor(out=v, in0=negm, in1=negm, op=ALU.mult))
            vchain(eng, eng.scalar_tensor_tensor(out=v, in0=ssqs[:, 0:4], scalar=1.0 / D,
                                                 in1=v, op0=ALU.mult, op1=ALU.subtract))
            vchain(eng, eng.tensor_scalar(out=v, in0=v, scalar1=EPS_LN, scalar2=None,
                                          op0=ALU.add))
            eng.reciprocal(out=ln_sc[:, 8:12], in_=v).then_inc(sems["vector"], 1)
        t_stats = step("vector", f_stats, deps=(tuple(wo_toks), tuple(sq_toks)))

        def f_rstd(eng):
            eng.activation(out=ln_sc[:, 8:12], in_=ln_sc[:, 8:12],
                           func=AF.Sqrt).then_inc(sems["scalar"], 1)
        t_rstd = step("scalar", f_rstd, deps=(t_stats,))
        stats_tok = (t_rstd,)

        affs = []
        for tt in range(4):
            xsl = tt

            def f_aff(eng, tt=tt, xsl=xsl):
                eng.tensor_scalar(out=x1_bf(xsl), in0=r1_st(tt),
                                  scalar1=ln_sc[:, 12 + tt:13 + tt],
                                  scalar2=ln_sc[:, 8 + tt:9 + tt],
                                  op0=ALU.add, op1=ALU.mult).then_inc(sems["vector"], 1)
            t_aff = step("vector", f_aff, deps=((t_rstd,), x1_free[xsl]))
            r1_free[tt] = (t_aff,)
            affs.append(t_aff)
        return affs

    def emit_tail2_unit(ch, tt, t_aff):
        t_idx = ch * 4 + tt
        xsl = tt

        def f_xmax(eng, t_idx=t_idx, xsl=xsl):
            eng.tensor_reduce(out=xmax_cols[:, t_idx:t_idx + 1],
                              in_=x1_bf(xsl), axis=AX, op=ALU.max,
                              apply_absolute_value=True).then_inc(sems["vector"], 1)
        t_xmax = step("vector", f_xmax, deps=(t_aff,))
        xmax_toks.append(t_xmax)
        t_x1st = dma("gpsimd", f"x1st{xsl}", x1_d[t_idx, :, :], x1_bf(xsl),
                     deps=(t_aff,))

        def f_x1tr(eng, t_idx=t_idx, xsl=xsl):
            for c in range(DC):
                eng.dma_start(out=xq_ap(c)[:, t_idx * 128:(t_idx + 1) * 128],
                              in_=x1_bf(xsl)[:, c * 128:(c + 1) * 128],
                              transpose=True).then_inc(sems[f"x1tr{xsl}"], 16)
        t_x1tr = plan.add("sync", f_x1tr, (t_aff, bar2), inc=16 * DC,
                          stream=f"x1tr{xsl}")
        x1_free[xsl] = (t_x1st, t_xmax, t_x1tr)
        x1_store_toks.append(t_x1st)
        x1tr_toks.append(t_x1tr)

    for ch in range(NCH):
        sl = ch % 2
        t_qld = dmas("gpsimd", f"qch{sl}",
                     [(q_ch(sl, c), q_d[c, :, ch * 512:(ch + 1) * 512]) for c in range(DC)],
                     deps=(bar2, q_slot_free[sl]))

        def f_nmm(eng, sl=sl):
            for c in range(DC):
                ins = eng.matmul(psA[0][0:16, :],
                                 normlhs[:, c * NH:(c + 1) * NH],
                                 q_ch(sl, c),
                                 start=(c == 0), stop=(c == DC - 1))
            ins.then_inc(sems["tensor"], 1)
        t_nmm = step("tensor", f_nmm, deps=(t_qld, t_bd, ps_norm_free))

        def f_nrm(eng):
            vchain(eng, eng.tensor_scalar(out=norm_sb[:, :], in0=psA[0][0:16, :],
                                          scalar1=EPS_ATTN, scalar2=None, op0=ALU.max))
            vchain(eng, eng.reciprocal(out=norm_sb2[:, :], in_=norm_sb[:, :]))
            eng.tensor_copy(out=recip_bf[:, :], in_=norm_sb2[:, :]).then_inc(
                sems["vector"], 1)
        t_nrm = step("vector", f_nrm, deps=(t_nmm, recip_free))
        ps_norm_free = (t_nrm,)

        mix_toks = []
        rexpcp_toks = []
        last_mixmm = None
        for p in range(NP):
            bank = p % 2

            def f_mixmm(eng, p=p, bank=bank, sl=sl):
                eng.matmul(psC[bank][:, :],
                           blockdiag[:, p * 128:(p + 1) * 128],
                           q_ch(sl, p),
                           start=True, stop=True)
                eng.matmul(psB[0][:, bank * 512:(bank + 1) * 512],
                           oh_sb[:, p * 128:(p + 1) * 128],
                           recip_bf[:, :],
                           start=True, stop=True).then_inc(sems["tensor"], 1)
            t_mixmm = step("tensor", f_mixmm,
                           deps=(t_nrm, ps_mix_free[bank], ps_rexp_free[bank],
                                 mix_slot_free[sl], t_clip, psB0_free))
            last_mixmm = t_mixmm

            def f_rexpcp(eng, bank=bank):
                eng.copy(out=rexp_sb(bank),
                         in_=psB[0][:, bank * 512:(bank + 1) * 512]).then_inc(
                    sems["scalar"], 1)
            t_rexpcp = step("scalar", f_rexpcp,
                            deps=(t_mixmm, ps_rexp_free[bank], rexp_sb_free[bank]))
            ps_rexp_free[bank] = (t_rexpcp,)
            rexpcp_toks.append(t_rexpcp)

            def f_div(eng, p=p, bank=bank, sl=sl):
                eng.tensor_tensor(out=mix_bf(sl)[:, p * 512:(p + 1) * 512],
                                  in0=psC[bank][:, :],
                                  in1=rexp_sb(bank),
                                  op=ALU.mult).then_inc(sems["vector"], 1)
            t_div = step("vector", f_div, deps=(t_mixmm, t_rexpcp))
            ps_mix_free[bank] = (t_div,)
            rexp_sb_free[bank] = (t_div,)
            mix_toks.append(t_div)
        recip_free = (last_mixmm,)
        q_slot_free[sl] = (last_mixmm, t_nmm)

        x0_toks = []
        for tt in range(4):
            t_idx = ch * 4 + tt
            t_x0 = dma("gpsimd", f"x0{tt}", x0_tile(tt),
                       x_in[t_idx * 128:(t_idx + 1) * 128, :], deps=(x0_free[tt],))
            x0_toks.append(t_x0)

        if pend_tail_c is not None:
            pend_tail2_c = (pend_tail_c[2], emit_tail1_c(*pend_tail_c))
            pend_tail_c = None

        wo_toks = []
        sq_toks = []
        last_womm = None
        for tt in range(4):
            wbank = tt % 2
            wps = psB[1] if wbank == 0 else psB[0]

            def f_womm(eng, tt=tt, sl=sl, wps=wps):
                for c in range(DC):
                    for half in range(2):
                        eng.matmul(wps[:, half * 512:(half + 1) * 512],
                                   mix_bf(sl)[:, c * 512 + tt * 128: c * 512 + (tt + 1) * 128],
                                   wo_bf[:, c * D + half * 512: c * D + (half + 1) * 512],
                                   start=(c == 0), stop=False)
                for half in range(2):
                    ins = eng.matmul(wps[:, half * 512:(half + 1) * 512],
                                     ones_row[:, :],
                                     bo_bf[:, half * 512:(half + 1) * 512],
                                     start=False, stop=True)
                ins.then_inc(sems["tensor"], 1)
            extra = tuple(rexpcp_toks) if wbank == 1 else ()
            t_womm = step("tensor", f_womm,
                          deps=(tuple(mix_toks), ps_wo_free[wbank], t_wo, extra))
            last_womm = t_womm

            def f_r1(eng, tt=tt, wps=wps):
                eng.scalar_tensor_tensor(
                    out=r1_st(tt), in0=wps[:, :], scalar=1.0,
                    in1=x0_tile(tt), op0=ALU.mult, op1=ALU.add,
                    accum_out=rsums[:, tt:tt + 1]).then_inc(sems["vector"], 1)
            t_r1 = step("vector", f_r1,
                        deps=(t_womm, x0_toks[tt], stats_tok, r1_free[tt]))
            ps_wo_free[wbank] = (t_r1,)
            x0_free[tt] = (t_r1,)
            wo_toks.append(t_r1)

            def f_sq(eng, tt=tt):
                eng.scalar_tensor_tensor(
                    out=sq_junk[:, :], in0=r1_st(tt), scalar=1.0, in1=r1_st(tt),
                    op0=ALU.mult, op1=ALU.mult,
                    accum_out=ssqs[:, tt:tt + 1]).then_inc(sems["vector"], 1)
            t_sq = step("vector", f_sq, deps=(t_r1, sqj_tok))
            sqj_tok = (t_sq,)
            sq_toks.append(t_sq)

            if pend_tail2_c is not None:
                emit_tail2_unit(pend_tail2_c[0], tt, pend_tail2_c[1][tt])

        pend_tail2_c = None
        psB0_free = (wo_toks[3],)
        mix_slot_free[sl] = (last_womm,)
        pend_tail_c = (wo_toks, sq_toks, ch)

    last_affs = emit_tail1_c(*pend_tail_c)
    pend_tail_c = None
    for tt in range(4):
        emit_tail2_unit(NCH - 1, tt, last_affs[tt])

    bar3 = plan.barrier()
    stk_b.close()

    def f_xred(eng):
        eng.tensor_reduce(out=red_sc[:, :], in_=xmax_cols[:, :], axis=AX,
                          op=ALU.max).then_inc(sems["vector"], 1)
    t_xred = step("vector", f_xred, deps=(tuple(xmax_toks),))
    t_pm1 = dma("scalar", "pm", pmax_d, red_sc[:, :], deps=(t_xred,))
    t_pm1b = dma("scalar", "pm", smax1[:, :], pmax_d, deps=(t_pm1,))

    def f_xred2(eng):
        eng.tensor_reduce(out=gmax1[:, :], in_=smax1[:, :], axis=AX,
                          op=ALU.max).then_inc(sems["vector"], 1)
    t_xred2 = step("vector", f_xred2, deps=(t_pm1b,))
    t_m1o = dma("gpsimd", "cc", cc_m1_i, gmax1[:, :], deps=(t_xred2,))

    def f_cc_m1(eng):
        eng.collective_compute(
            "AllReduce", ALU.max, replica_groups=[list(range(NCORES))],
            ins=[cc_m1_i.opt()], outs=[cc_m1_o.opt()]).then_inc(sems["gpsimd"], 1)
    t_cc_m1 = step("gpsimd", f_cc_m1, deps=(t_m1o,), inc=1)
    t_m1i = dma("gpsimd", "cc", gmax1[:, :], cc_m1_o, deps=(t_cc_m1,))

    def f_sigx(eng):
        eng.activation(out=sc_small[:, 0:1], in_=clipx_s[:, :],
                       func=AF.Sigmoid).then_inc(sems["scalar"], 1)
    t_sigx = step("scalar", f_sigx, deps=(t_clip, bar3))

    def f_sx(eng):
        vchain(eng, eng.tensor_tensor(out=sc_small[:, 1:2], in0=sc_small[:, 0:1],
                                      in1=gmax1[:, :], op=ALU.mult))
        vchain(eng, eng.tensor_scalar(out=sc_small[:, 1:2], in0=sc_small[:, 1:2],
                                      scalar1=1.0 / QMAX, scalar2=None, op0=ALU.mult))
        eng.reciprocal(out=sc_small[:, 2:3], in_=sc_small[:, 1:2]).then_inc(
            sems["vector"], 1)
    t_sx = step("vector", f_sx, deps=(t_sigx, t_m1i))
    t_b1 = dma("scalar", "bc1", bc1_d, sc_small[:, 1:2], deps=(t_sx,))
    t_b2 = dma("scalar", "bc2", bc2_d, sc_small[:, 2:3], deps=(t_sx,))
    t_sxbc = dma("scalar", "bc1", sx_bc[:, :], bc1_d.partition_broadcast(128),
                 deps=(t_b1,))
    t_rsxbc = dma("scalar", "bc2", rsx_bc[:, :], bc2_d.partition_broadcast(128),
                  deps=(t_b2,))

    def f_svec(eng):
        vchain(eng, eng.tensor_scalar(out=svec_g[:, :], in0=swg_cols[:, :],
                                      scalar1=sx_bc[:, :], scalar2=1.0 / QMAX,
                                      op0=ALU.mult, op1=ALU.mult))
        eng.tensor_scalar(out=svec_u[:, :], in0=swu_cols[:, :],
                          scalar1=sx_bc[:, :], scalar2=1.0 / QMAX,
                          op0=ALU.mult, op1=ALU.mult).then_inc(sems["vector"], 1)
    t_svec = step("vector", f_svec, deps=(t_sxbc, tuple(g_row_toks), tuple(u_row_toks)))

    stk_c1 = ExitStack()

    def sb_c1(name, shape, dtype):
        return stk_c1.enter_context(nc.sbuf_tensor(name, shape, dtype))

    xqi8 = sb_c1("xqi8", [128, 512], I8)
    wfm_ = sb_c1("wfm", [128, 2 * 2 * D], BF16)
    wfm = lambda s, gu: wfm_[:, (s * 2 + gu) * D:(s * 2 + gu + 1) * D]
    g_act = sb_c1("g_act", [128, 512], F32)
    h_bf_ = sb_c1("h_bf", [128, 2 * 512], BF16)
    h_bf = lambda s: h_bf_[:, s * 512:(s + 1) * 512]

    g_banks = [psA[0][:, :], psA[1][:, :], psB[0][:, 0:512], psB[0][:, 512:1024]]
    u_banks = [psC[0][:, :], psC[1][:, :], psB[1][:, 0:512], psB[1][:, 512:1024]]
    quant_toks = []
    for ch in range(NCH):
        ch_tr4 = tuple(x1tr_toks[ch * 4 + k] for k in range(4))

        def f_xq(eng, ch=ch):
            for hc in range(DC):
                strip = xq_ap(hc)[:, ch * 512:(ch + 1) * 512]
                vchain(eng, eng.tensor_scalar(out=xqi8[:, :], in0=strip,
                                              scalar1=rsx_bc[:, :], scalar2=None,
                                              op0=ALU.mult))
                ins = eng.tensor_copy(out=strip, in_=xqi8[:, :])
                if hc < DC - 1:
                    vchain(eng, ins)
            ins.then_inc(sems["vector"], 1)
        prev_q = quant_toks[-1] if quant_toks else ()
        t_xq = step("vector", f_xq, deps=(t_rsxbc, ch_tr4, bar3, prev_q))
        quant_toks.append(t_xq)

    def wload(i, wsl, deps):
        src_g = wgq_d[:, :, i * 128:(i + 1) * 128].transpose([1, 0, 2])
        src_u = wuq_d[:, :, i * 128:(i + 1) * 128].transpose([1, 0, 2])
        return dmas("sync", f"wl{wsl}",
                    [(wfm(wsl, 0).rearrange("p (c j) -> p c j", j=128), src_g),
                     (wfm(wsl, 1).rearrange("p (c j) -> p c j", j=128), src_u)],
                    deps=deps)

    wfm_free = [(), ()]
    ps_g_free = [(), (), (), ()]
    ps_u_free = [(), (), (), ()]
    g_act_free = ()
    h_slot_free = [(), ()]
    wl_toks = [None] * IC
    h_store_toks = []
    hmax_toks = []
    last_gumm = ()

    wl_toks[0] = wload(0, 0, deps=(bar3,))
    if IC > 1:
        wl_toks[1] = wload(1, 1, deps=(bar3,))

    for i in range(IC):
        wsl = i % 2
        t_wl = wl_toks[i]

        gumm_toks = []
        for ch in range(NCH):
            bank = ch % 4

            def f_gumm(eng, wsl=wsl, ch=ch, bank=bank):
                for hc in range(DC):
                    eng.matmul(g_banks[bank],
                               wfm(wsl, 0)[:, hc * 128:(hc + 1) * 128],
                               xq_ap(hc)[:, ch * 512:(ch + 1) * 512],
                               start=(hc == 0), stop=(hc == DC - 1))
                for hc in range(DC):
                    ins = eng.matmul(u_banks[bank],
                                     wfm(wsl, 1)[:, hc * 128:(hc + 1) * 128],
                                     xq_ap(hc)[:, ch * 512:(ch + 1) * 512],
                                     start=(hc == 0), stop=(hc == DC - 1))
                ins.then_inc(sems["tensor"], 1)
            t_gumm = step("tensor", f_gumm,
                          deps=(t_wl, quant_toks[ch], ps_g_free[bank], ps_u_free[bank]))
            gumm_toks.append(t_gumm)

            def f_gact(eng, i=i, bank=bank):
                eng.activation(out=g_act[:, :], in_=g_banks[bank], func=AF.Silu,
                               scale=svec_g[:, i:i + 1]).then_inc(sems["scalar"], 1)
            t_gact = step("scalar", f_gact, deps=(t_gumm, t_svec, g_act_free))

            hsl = ch % 2

            def f_h(eng, i=i, ch=ch, hsl=hsl, bank=bank):
                vchain(eng, eng.scalar_tensor_tensor(
                    out=h_bf(hsl), in0=u_banks[bank], scalar=svec_u[:, i:i + 1],
                    in1=g_act[:, :], op0=ALU.mult, op1=ALU.mult))
                eng.tensor_reduce(out=hmax_cols[:, i * NCH + ch:i * NCH + ch + 1],
                                  in_=h_bf(hsl), axis=AX, op=ALU.max,
                                  apply_absolute_value=True).then_inc(sems["vector"], 1)
            t_h = step("vector", f_h, deps=(t_gact, h_slot_free[hsl], t_svec))
            hmax_toks.append(t_h)
            ps_g_free[bank] = (t_h,)
            ps_u_free[bank] = (t_h,)
            g_act_free = (t_h,)
            t_hst = dma("gpsimd", f"hst{hsl}", h_d[i, :, ch * 512:(ch + 1) * 512],
                        h_bf(hsl), deps=(t_h,))
            h_slot_free[hsl] = (t_hst,)
            h_store_toks.append(t_hst)

        last_gumm = (gumm_toks[-1],)
        if DEBUG_DUMPS and i == 20:
            dmas("gpsimd", "dbgw", [(dbg_wfm[:, 0:D], wfm(wsl, 0)),
                                    (dbg_wfm[:, D:2 * D], wfm(wsl, 1))],
                 deps=(gumm_toks[-1],))
        if i + 2 < IC:
            wfm_free[wsl] = (gumm_toks[-1],)
            wl_toks[i + 2] = wload(i + 2, wsl, deps=(gumm_toks[-1],))

    if DEBUG_DUMPS:
        dmas("gpsimd", "dbgd",
             [(dbg_xq[c, :, :], xq_ap(c)) for c in range(DC)],
             deps=(last_gumm, tuple(quant_toks)))

    t_swdr = dmas("scalar", "swd", [(swd_dram[:, ht * 128:(ht + 1) * 128],
                                     swd_cols[:, ht:ht + 1]) for ht in range(HT)],
                  deps=(tuple(d_row_toks),))
    t_swdl = dma("scalar", "swd", swd_row[:, :], swd_dram, deps=(t_swdr,))

    d_free = (last_gumm, g_act_free, h_slot_free[0], h_slot_free[1],
              tuple(h_store_toks))
    bar4 = plan.barrier()
    stk_c1.close()
    stk_c.close()

    stk_d = ExitStack()

    def sb_d(name, shape, dtype):
        return stk_d.enter_context(nc.sbuf_tensor(name, shape, dtype))

    h_ld_ = sb_d("h_ld", [128, 2 * IC * 512], BF16)
    h_ld = lambda s, ic: h_ld_[:, (s * IC + ic) * 512:(s * IC + ic + 1) * 512]
    hq_i8_ = sb_d("hq_i8", [128, 2 * 1024], I8)
    hq_i8 = lambda s: hq_i8_[:, s * 1024:(s + 1) * 1024]
    x1f_ = sb_d("x1f", [128, 2 * D], BF16)
    x1f = lambda s: x1f_[:, s * D:(s + 1) * D]
    r2_st_ = sb_d("r2_st", [128, 4 * D], F32)
    r2_st = lambda s: r2_st_[:, s * D:(s + 1) * D]
    t2_st = sb_d("t2_st", [128, D], F32)
    out_t_ = sb_d("out_t", [128, 4 * D], F32)
    out_t = lambda s: out_t_[:, s * D:(s + 1) * D]
    sq_junk2 = sb_d("sq_junk2", [128, D], F32)
    swd_bc = sb_d("swd_bc", [128, D], F32)
    wd_fm = sb_d("wd_fm", [128, IC * D], BF16)

    def wdfm_ap(ic):
        return wd_fm[:, ic * D:(ic + 1) * D]

    t_wdfm = dmas("sync", "wdfm", [(wdfm_ap(ic), wdq_d[ic, :, :]) for ic in range(IC)],
                  deps=(d_free,))

    def f_hred(eng):
        eng.tensor_reduce(out=red_sc[:, :], in_=hmax_cols[:, :], axis=AX,
                          op=ALU.max).then_inc(sems["vector"], 1)
    t_hred = step("vector", f_hred, deps=(tuple(hmax_toks),))
    t_pm2 = dma("scalar", "pm", pmax_d, red_sc[:, :], deps=(t_hred, t_pm1b))
    t_pm2b = dma("scalar", "pm", smax1[:, :], pmax_d, deps=(t_pm2, t_xred2))

    def f_hred2(eng):
        eng.tensor_reduce(out=gmax2[:, :], in_=smax1[:, :], axis=AX,
                          op=ALU.max).then_inc(sems["vector"], 1)
    t_hred2 = step("vector", f_hred2, deps=(t_pm2b,))
    t_m2o = dma("gpsimd", "cc", cc_m2_i, gmax2[:, :], deps=(t_hred2,))

    def f_cc_m2(eng):
        eng.collective_compute(
            "AllReduce", ALU.max, replica_groups=[list(range(NCORES))],
            ins=[cc_m2_i.opt()], outs=[cc_m2_o.opt()]).then_inc(sems["gpsimd"], 1)
    t_cc_m2 = step("gpsimd", f_cc_m2, deps=(t_m2o,), inc=1)
    t_m2i = dma("gpsimd", "cc", gmax2[:, :], cc_m2_o, deps=(t_cc_m2,))

    def f_sigd(eng):
        eng.activation(out=sc_small[:, 3:4], in_=clipd_s[:, :],
                       func=AF.Sigmoid).then_inc(sems["scalar"], 1)
    t_sigd = step("scalar", f_sigd, deps=(t_clip, bar4))

    def f_sh(eng):
        vchain(eng, eng.tensor_tensor(out=sc_small[:, 4:5], in0=sc_small[:, 3:4],
                                      in1=gmax2[:, :], op=ALU.mult))
        vchain(eng, eng.tensor_scalar(out=sc_small[:, 4:5], in0=sc_small[:, 4:5],
                                      scalar1=1.0 / QMAX, scalar2=None, op0=ALU.mult))
        vchain(eng, eng.reciprocal(out=sc_small[:, 5:6], in_=sc_small[:, 4:5]))
        eng.tensor_scalar(out=sc_small[:, 7:8], in0=sc_small[:, 4:5],
                          scalar1=1.0 / QMAX, scalar2=None,
                          op0=ALU.mult).then_inc(sems["vector"], 1)
    t_sh = step("vector", f_sh, deps=(t_sigd, t_m2i))
    t_b3 = dma("scalar", "bc3", bc3_d, sc_small[:, 5:6], deps=(t_sh,))
    t_rshbc = dma("scalar", "bc3", rsh_bc[:, :], bc3_d.partition_broadcast(128),
                  deps=(t_b3,))

    def f_swdsc(eng):
        eng.activation(out=swd_sc[:, :], in_=swd_row[:, :], func=AF.Copy,
                       scale=sc_small[:, 7:8]).then_inc(sems["scalar"], 1)
    t_swdsc = step("scalar", f_swdsc, deps=(t_sh, t_swdl))
    t_swdd = dma("scalar", "swd", swd_dram, swd_sc[:, :], deps=(t_swdsc, t_swdl))
    t_swdbc = dma("scalar", "swd", swd_bc[:, :], swd_dram.partition_broadcast(128),
                  deps=(t_swdd,))

    h_ld_free = [(), ()]
    ps_dn_free = [(), ()]
    x1f_free = [(), ()]
    out_free = [(), (), (), ()]
    stats2_tok = ()
    r2_free = [()] * 4
    i8_free = [(), ()]
    out_toks = []
    pend_tail_f = None

    def emit_tail_f(dn_toks, sq2_toks, ch):
        nonlocal stats2_tok

        def f_stats2(eng):
            negm = ln2_sc[:, 12:16]
            v = ln2_sc[:, 4:8]
            vchain(eng, eng.tensor_scalar(out=negm, in0=rsums2[:, 0:4],
                                          scalar1=-1.0 / D, scalar2=None, op0=ALU.mult))
            vchain(eng, eng.tensor_tensor(out=v, in0=negm, in1=negm, op=ALU.mult))
            vchain(eng, eng.scalar_tensor_tensor(out=v, in0=ssqs2[:, 0:4],
                                                 scalar=1.0 / D, in1=v,
                                                 op0=ALU.mult, op1=ALU.subtract))
            vchain(eng, eng.tensor_scalar(out=v, in0=v, scalar1=EPS_LN, scalar2=None,
                                          op0=ALU.add))
            eng.reciprocal(out=ln2_sc[:, 8:12], in_=v).then_inc(sems["vector"], 1)
        t_stats2 = step("vector", f_stats2, deps=(tuple(dn_toks), tuple(sq2_toks)))

        def f_rstd2(eng):
            eng.activation(out=ln2_sc[:, 8:12], in_=ln2_sc[:, 8:12],
                           func=AF.Sqrt).then_inc(sems["scalar"], 1)
        t_rstd2 = step("scalar", f_rstd2, deps=(t_stats2,))
        stats2_tok = (t_rstd2,)

        for tt in range(4):
            t_idx = ch * 4 + tt
            osl = tt

            def f_aff2(eng, tt=tt, osl=osl):
                eng.tensor_scalar(out=out_t(osl), in0=r2_st(tt),
                                  scalar1=ln2_sc[:, 12 + tt:13 + tt],
                                  scalar2=ln2_sc[:, 8 + tt:9 + tt],
                                  op0=ALU.add, op1=ALU.mult).then_inc(sems["vector"], 1)
            t_aff2 = step("vector", f_aff2, deps=((t_rstd2,), out_free[osl]))
            r2_free[tt] = (t_aff2,)
            t_ost = dma("gpsimd", f"ost{osl}", out_d[t_idx * 128:(t_idx + 1) * 128, :],
                        out_t(osl), deps=(t_aff2,))
            out_free[osl] = (t_ost,)
            out_toks.append(t_ost)

    def emit_hld(ch):
        sl = ch % 2
        return dmas("sync", f"hld{sl}",
                    [(h_ld(sl, ic), h_d[ic, :, ch * 512:(ch + 1) * 512])
                     for ic in range(IC)],
                    deps=(d_free, h_ld_free[sl]))

    def emit_hq_unit(ch, j, t_hld):
        sl = ch % 2
        ic = 2 * j
        n2 = min(2, IC - ic)
        isl = j % 2

        def f_hqs(eng, sl=sl, ic=ic, isl=isl, n2=n2):
            eng.activation(out=hq_i8(isl)[:, 0:n2 * 512],
                           in_=h_ld_[:, (sl * IC + ic) * 512:
                                     (sl * IC + ic + n2) * 512],
                           func=AF.Copy,
                           scale=rsh_bc[:, :]).then_inc(sems["scalar"], 1)
        t_hqs = step("scalar", f_hqs, deps=(t_hld, t_rshbc, i8_free[isl]))

        def f_hqv(eng, sl=sl, ic=ic, isl=isl, n2=n2):
            eng.tensor_copy(out=h_ld_[:, (sl * IC + ic) * 512:
                                      (sl * IC + ic + n2) * 512],
                            in_=hq_i8(isl)[:, 0:n2 * 512]).then_inc(
                sems["vector"], 1)
        t_hqv = step("vector", f_hqv, deps=(t_hqs,))
        i8_free[isl] = (t_hqv,)
        return t_hqv

    NQ = (IC + 1) // 2
    hld_by = {0: emit_hld(0)}
    hq_by = {0: [emit_hq_unit(0, j, hld_by[0]) for j in range(NQ)]}
    qsplit = [(0, 3), (3, 6), (6, 9), (9, NQ)]

    for ch in range(NCH):
        sl = ch % 2
        if ch + 1 < NCH:
            hld_by[ch + 1] = emit_hld(ch + 1)

        if pend_tail_f is not None:
            emit_tail_f(*pend_tail_f)
            pend_tail_f = None

        all_hq = tuple(hq_by.pop(ch))
        nxt_hq = []
        dn_toks = []
        sq2_toks = []
        for tt in range(4):
            t_idx = ch * 4 + tt
            xsl = tt % 2
            dbank = tt % 2
            t_x1f = dma("gpsimd", f"x1f{xsl}", x1f(xsl), x1_d[t_idx, :, :],
                        deps=(d_free, bar3, x1f_free[xsl]))

            def f_dnmm(eng, sl=sl, tt=tt, dbank=dbank):
                for ic in range(IC):
                    for half in range(2):
                        ins = eng.matmul(
                            psB[dbank][:, half * 512:(half + 1) * 512],
                            h_ld(sl, ic)[:, tt * 128:(tt + 1) * 128],
                            wdfm_ap(ic)[:, half * 512:(half + 1) * 512],
                            start=(ic == 0), stop=(ic == IC - 1))
                ins.then_inc(sems["tensor"], 1)
            t_dnmm = step("tensor", f_dnmm,
                          deps=(all_hq, ps_dn_free[dbank], t_wdfm, bar4))

            def f_r2(eng, tt=tt, xsl=xsl, dbank=dbank):
                vchain(eng, eng.tensor_tensor(out=t2_st[:, :], in0=psB[dbank][:, :],
                                              in1=swd_bc[:, :], op=ALU.mult))
                eng.scalar_tensor_tensor(
                    out=r2_st(tt), in0=t2_st[:, :], scalar=1.0,
                    in1=x1f(xsl), op0=ALU.mult, op1=ALU.add,
                    accum_out=rsums2[:, tt:tt + 1]).then_inc(sems["vector"], 1)
            prev_r2 = dn_toks[-1] if dn_toks else ()
            t_r2 = step("vector", f_r2,
                        deps=(t_dnmm, t_x1f, t_swdbc, stats2_tok, r2_free[tt], prev_r2))
            ps_dn_free[dbank] = (t_r2,)
            x1f_free[xsl] = (t_r2,)
            dn_toks.append(t_r2)

            def f_sq2(eng, tt=tt):
                eng.scalar_tensor_tensor(
                    out=sq_junk2[:, :], in0=r2_st(tt), scalar=1.0, in1=r2_st(tt),
                    op0=ALU.mult, op1=ALU.mult,
                    accum_out=ssqs2[:, tt:tt + 1]).then_inc(sems["vector"], 1)
            t_sq2 = step("vector", f_sq2, deps=(t_r2, sqj_tok))
            sqj_tok = (t_sq2,)
            sq2_toks.append(t_sq2)

            if ch + 1 < NCH:
                for j in range(*qsplit[tt]):
                    nxt_hq.append(emit_hq_unit(ch + 1, j, hld_by[ch + 1]))

        if ch + 1 < NCH:
            hq_by[ch + 1] = nxt_hq
        h_ld_free[sl] = (dn_toks[-1],)
        pend_tail_f = (dn_toks, sq2_toks, ch)

    emit_tail_f(*pend_tail_f)
    pend_tail_f = None

    stk_d.close()

    if DEBUG_DUMPS:
        t_dbg = dmas("gpsimd", "dbgd",
                     [(dbg_x1, x1_d), (dbg_h, h_d), (dbg_q, q_d), (dbg_wg, wgq_d)],
                     deps=(tuple(out_toks),))
        t_dbg2 = dma("gpsimd", "dbgd", dbg_kv, cc_kv_o, deps=(t_dbg,))
        out_toks.append(t_dbg2)

    def f_final(eng):
        eng.memset(sc_small[:, 6:7], 0.0).then_inc(sems["vector"], 1)
    step("vector", f_final, deps=(tuple(out_toks),))

    for s in plan.count:
        sems[s] = nc.alloc_semaphore(name=f"sem_{s}")
    sems["vchain"] = nc.alloc_semaphore(name="sem_vchain")
    sems["schain"] = nc.alloc_semaphore(name="sem_schain")

    def emit(eng_name, eng):
        watermark = {s: 0 for s in plan.count}
        for (e, fn, deps) in plan.steps:
            if e != eng_name:
                continue
            for (de, dv) in deps:
                if watermark[de] >= dv:
                    continue
                eng.wait_ge(sems[de], dv)
                watermark[de] = dv
            fn(eng)

    with nc.Block() as block:
        @block.tensor
        def _(tensor):
            emit("tensor", tensor)

        @block.vector
        def _(vector):
            emit("vector", vector)

        @block.scalar
        def _(scalar):
            emit("scalar", scalar)

        @block.sync
        def _(sync):
            emit("sync", sync)

        @block.gpsimd
        def _(gpsimd):
            emit("gpsimd", gpsimd)

    return nc


_NC_CACHE = []


def kernel(L0_attn_inputs_q, L0_attn_Wq, L0_attn_bq, L0_attn_Wk, L0_attn_bk,
           L0_attn_Wv, L0_attn_bv, L0_attn_Wo, L0_attn_bo,
           L0_ffn_clip_x, L0_ffn_clip_down, L0_ffn_w_gate, L0_ffn_w_up,
           L0_ffn_w_down):
    x = np.ascontiguousarray(np.asarray(L0_attn_inputs_q, dtype=np.float32))
    onehot = np.zeros((NH, NP * 128), dtype=ml_dtypes.bfloat16)
    for p in range(NP):
        onehot[2 * p, p * 128:p * 128 + 64] = 1.0
        onehot[2 * p + 1, p * 128 + 64:p * 128 + 128] = 1.0

    wg = np.ascontiguousarray(L0_ffn_w_gate, dtype=np.float32)
    wu = np.ascontiguousarray(L0_ffn_w_up, dtype=np.float32)
    wd = np.ascontiguousarray(L0_ffn_w_down, dtype=np.float32)
    common = {
        "wq": np.ascontiguousarray(L0_attn_Wq, dtype=np.float32),
        "wk": np.ascontiguousarray(L0_attn_Wk, dtype=np.float32),
        "wv": np.ascontiguousarray(L0_attn_Wv, dtype=np.float32),
        "wo": np.ascontiguousarray(L0_attn_Wo, dtype=np.float32),
        "bq": np.ascontiguousarray(L0_attn_bq, dtype=np.float32).reshape(1, D),
        "bk": np.ascontiguousarray(L0_attn_bk, dtype=np.float32).reshape(1, D),
        "bv": np.ascontiguousarray(L0_attn_bv, dtype=np.float32).reshape(1, D),
        "bo": np.ascontiguousarray(L0_attn_bo, dtype=np.float32).reshape(1, D),
        "clipx": np.ascontiguousarray(L0_ffn_clip_x, dtype=np.float32).reshape(1, 1),
        "clipd": np.ascontiguousarray(L0_ffn_clip_down, dtype=np.float32).reshape(1, 1),
        "wg": wg, "wu": wu, "wd": wd,
        "wgt": np.ascontiguousarray(wg.T),
        "wut": np.ascontiguousarray(wu.T),
        "wdt": np.ascontiguousarray(wd.T),
        "onehot": onehot,
    }
    in_maps = []
    for c in range(NCORES):
        b, half = c // 2, c % 2
        xc = np.ascontiguousarray(x[b, half * T:(half + 1) * T, :])
        in_maps.append({"x": xc, **common})

    if not _NC_CACHE:
        _NC_CACHE.append(build_program())
    nc = _NC_CACHE[0]
    res = run_bass_kernel_spmd(nc, in_maps, core_ids=list(range(NCORES)))
    out = np.empty((B, S, D), dtype=np.float32)
    for c in range(NCORES):
        b, half = c // 2, c % 2
        out[b, half * T:(half + 1) * T, :] = res.results[c]["out"]
    return out
